# version 14
# baseline (speedup 1.0000x reference)
import numpy as np
import ml_dtypes
from contextlib import ExitStack

import concourse.bass as bass
import concourse.mybir as mybir
from concourse.bass_utils import run_bass_kernel_spmd

F32 = mybir.dt.float32
BF16 = mybir.dt.bfloat16
I32 = mybir.dt.int32
AF = mybir.ActivationFunctionType
ALU = mybir.AluOpType
AX = mybir.AxisListType

B, S, D = 16, 2048, 2048
NCORES = 8
BPC = B // NCORES
T = BPC * S
NH, DK = 16, 128
H_FFN = 8192
MAC = 512
NMAC = T // MAC
MPB = NMAC // BPC
NSUB = MAC // 128
DCH = D // 128
HCH = H_FFN // 128
EPS_LN = 1e-5
EPS_RMS = float(np.finfo(np.float32).eps)
SM_SCALE = float(1.0 / np.sqrt(DK))
NEG_BIG = -1e30
BF16_CLIP_LO = float(np.float32(ml_dtypes.bfloat16(1e-8)))
RNE_C = 12582912.0
LN_CLIP_LO = float(np.log(np.float64(1e-10)))
BF16_HALF_UP = 0.5009765625

UPTO = 4
DEBUG_DUMPS = []



def _rope_tables():
    theta = (1.0 / 10000.0 ** (np.arange(0, DK, 2, dtype=np.float64) / DK))
    pos = np.arange(S, dtype=np.float64)
    fr = pos[:, None] * theta[None, :]
    cos = np.cos(fr)
    sin = np.sin(fr)
    cos_fm = np.empty((DK, S), np.float32)
    sin_fm = np.empty((DK, S), np.float32)
    cos_fm[:64] = cos.T
    cos_fm[64:] = cos.T
    sin_fm[:64] = sin.T
    sin_fm[64:] = -sin.T
    return np.ascontiguousarray(cos_fm), np.ascontiguousarray(sin_fm)


def _tile_lhsT(wT, kdim, fdim):
    kc, fc = kdim // 128, fdim // 128
    t = wT.reshape(kc, 128, fc, 128)
    return np.ascontiguousarray(t.transpose(2, 1, 0, 3))


def _cols(v):
    n = v.shape[0] // 128
    return np.ascontiguousarray(v.reshape(n, 128).T.astype(np.float32))


def _prep_inputs(inp):
    f = np.float32
    d = {}
    d["wq_t"] = _tile_lhsT(np.ascontiguousarray(inp["L0_attn_wq"].T), D, D)
    d["wk_t"] = _tile_lhsT(np.ascontiguousarray(inp["L0_attn_wk"].T), D, D)
    d["wv_t"] = _tile_lhsT(np.ascontiguousarray(inp["L0_attn_wv"].T), D, D)
    d["wo_t"] = _tile_lhsT(np.ascontiguousarray(inp["L0_attn_wo"].T), D, D)
    d["bq_c"] = _cols(inp["L0_attn_bq"])
    d["bk_c"] = _cols(inp["L0_attn_bk"])
    d["bv_c"] = _cols(inp["L0_attn_bv"])
    d["bo_c"] = _cols(inp["L0_attn_bo"])
    d["lng_b"] = np.ascontiguousarray(np.tile(inp["L0_attn_ln_g"].astype(f), (128, 1)))
    d["lnb_b"] = np.ascontiguousarray(np.tile(inp["L0_attn_ln_b"].astype(f), (128, 1)))
    d["wgu"] = np.ascontiguousarray(inp["L0_ffn_w_gate_up"].astype(f))
    d["wp"] = np.ascontiguousarray(inp["L0_ffn_w_proj"].astype(f))
    d["wr1_t"] = _tile_lhsT(np.ascontiguousarray(inp["L1_attn_Wr"].T), D, D)
    d["wk1_t"] = _tile_lhsT(np.ascontiguousarray(inp["L1_attn_Wk"].T), D, D)
    d["wv1_t"] = _tile_lhsT(np.ascontiguousarray(inp["L1_attn_Wv"].T), D, D)
    d["wo1_t"] = _tile_lhsT(np.ascontiguousarray(inp["L1_attn_Wo"].T), D, D)
    d["decay_c"] = _cols(inp["L1_attn_decay"])
    d["gnw_c"] = _cols(inp["L1_attn_gn_weight"])
    d["gnb_c"] = _cols(inp["L1_attn_gn_bias"])
    d["gw_t"] = _tile_lhsT(np.ascontiguousarray(inp["L1_ffn_gate_w"].T), D, H_FFN)
    d["uw_t"] = _tile_lhsT(np.ascontiguousarray(inp["L1_ffn_up_w"].T), D, H_FFN)
    d["dw_t"] = _tile_lhsT(np.ascontiguousarray(inp["L1_ffn_down_w"].T), H_FFN, D)
    d["gs_c"] = _cols(inp["L1_ffn_gate_scale"])
    d["us_c"] = _cols(inp["L1_ffn_up_scale"])
    d["ds_c"] = _cols(inp["L1_ffn_down_scale"])
    cos_fm, sin_fm = _rope_tables()
    d["cosf"] = cos_fm
    d["sinf"] = sin_fm
    d["ident"] = np.eye(128, dtype=f)
    m = np.zeros((128, 128), f)
    m[np.triu_indices(128, k=1)] = NEG_BIG
    d["mask"] = m
    d["iota"] = np.ascontiguousarray(np.tile(np.arange(S, dtype=f), (128, 1)))
    x = np.asarray(inp["L0_attn_h"], dtype=f).reshape(B, S, D)
    shards = [np.ascontiguousarray(x[BPC * c:BPC * (c + 1)].reshape(T, D)) for c in range(NCORES)]
    return d, shards



class Sem:
    def __init__(self, nc, es, name):
        self.h = es.enter_context(nc.semaphore(name))
        self.v = 0

    def inc(self, instr, k=1):
        instr.then_inc(self.h, k)
        self.v += k
        return (self, self.v)


def _flat(evs, out):
    for ev in evs:
        if ev is None:
            continue
        if isinstance(ev, list):
            _flat(ev, out)
        else:
            out.append(ev)


def _wait_all(eng, evs):
    flat = []
    _flat(list(evs), flat)
    best = {}
    for s, v in flat:
        cur = best.get(id(s))
        if cur is None or cur[1] < v:
            best[id(s)] = (s, v)
    for s, v in best.values():
        eng.wait_ge(s.h, v)


class KB:
    def __init__(self, upto=4):
        self.upto = upto
        nc = bass.Bass("TRN2", target_bir_lowering=False)
        self.nc = nc
        self.es = ExitStack()
        es = self.es

        def din(name, shape, dt=F32):
            return nc.dram_tensor(name, shape, dt, kind="ExternalInput").ap()

        self.x_in = din("x_in", [T, D])
        self.wq_t = din("wq_t", [DCH, 128, DCH, 128])
        self.wk_t = din("wk_t", [DCH, 128, DCH, 128])
        self.wv_t = din("wv_t", [DCH, 128, DCH, 128])
        self.wo_t = din("wo_t", [DCH, 128, DCH, 128])
        self.bq_c = din("bq_c", [128, DCH])
        self.bk_c = din("bk_c", [128, DCH])
        self.bv_c = din("bv_c", [128, DCH])
        self.bo_c = din("bo_c", [128, DCH])
        self.lng_b = din("lng_b", [128, D])
        self.lnb_b = din("lnb_b", [128, D])
        self.wgu = din("wgu", [2 * H_FFN, D])
        self.wp = din("wp", [D, H_FFN])
        self.wr1_t = din("wr1_t", [DCH, 128, DCH, 128])
        self.wk1_t = din("wk1_t", [DCH, 128, DCH, 128])
        self.wv1_t = din("wv1_t", [DCH, 128, DCH, 128])
        self.wo1_t = din("wo1_t", [DCH, 128, DCH, 128])
        self.decay_c = din("decay_c", [128, DCH])
        self.gnw_c = din("gnw_c", [128, DCH])
        self.gnb_c = din("gnb_c", [128, DCH])
        self.gw_t = din("gw_t", [HCH, 128, DCH, 128], I32)
        self.uw_t = din("uw_t", [HCH, 128, DCH, 128], I32)
        self.dw_t = din("dw_t", [DCH, 128, HCH, 128], I32)
        self.gs_c = din("gs_c", [128, HCH])
        self.us_c = din("us_c", [128, HCH])
        self.ds_c = din("ds_c", [128, DCH])
        self.cosf = din("cosf", [128, S])
        self.sinf = din("sinf", [128, S])
        self.ident = din("ident", [128, 128])
        self.mask = din("mask", [128, 128])
        self.iota = din("iota", [128, S])

        self.out = nc.dram_tensor("out", [T, D], F32, kind="ExternalOutput").ap()

        def scr(name, shape, dt=F32):
            return nc.dram_tensor(name, shape, dt, kind="Internal").ap()

        self.x1_s = scr("x1_s", [T, D])
        self.x2_s = scr("x2_s", [T, D])
        self.x3_s = scr("x3_s", [T, D])
        self.y_s = scr("y_s", [T, D])
        self.q_s = scr("q_s", [BPC, NH, 128, S])
        self.k_s = scr("k_s", [BPC, NH, 128, S])
        self.v_s = scr("v_s", [BPC, NH, 128, S // 128, 128])
        self.o_s = scr("o_s", [BPC, NH, 128, S])
        self.att_s = scr("att_s", [BPC, DCH, 128, S])
        self.wgub_s = scr("wgub_s", [2 * H_FFN, D], BF16)
        self.wpb_s = scr("wpb_s", [D, H_FFN], BF16)
        self.wgut_s = scr("wgut_s", [2 * H_FFN // 512, 128, DCH, 512], BF16)
        self.wpt_s = scr("wpt_s", [DCH, 128, HCH, 128], BF16)
        self.gwb_s = scr("gwb_s", [HCH, 128, DCH, 128], BF16)
        self.uwb_s = scr("uwb_s", [HCH, 128, DCH, 128], BF16)
        self.dwb_s = scr("dwb_s", [DCH, 128, HCH, 128], BF16)

        self.sW = Sem(nc, es, "sW")
        self.sA = Sem(nc, es, "sA")
        self.sG = Sem(nc, es, "sG")
        self.sPE = Sem(nc, es, "sPE")
        self.sDV = Sem(nc, es, "sDV")
        self.sAC = Sem(nc, es, "sAC")
        self.sPO = Sem(nc, es, "sPO")
        self.dsem = {"sync": self.sW, "scalar": self.sA, "gpsimd": self.sG}
        self.esem = {"tensor": self.sPE, "vector": self.sDV, "scalar": self.sAC,
                     "gpsimd": self.sPO, "sync": self.sW}

        self.ps_big = es.enter_context(nc.psum_tensor("ps_big", [128, 2048], F32))
        self.PB = [self.ps_big[:, 512 * i:512 * (i + 1)] for i in range(4)]
        self.p4 = es.enter_context(nc.psum_tensor("p4", [128, 512], F32))
        self.p5 = es.enter_context(nc.psum_tensor("p5", [128, 512], F32))
        self.p6 = es.enter_context(nc.psum_tensor("p6", [128, 512], F32))
        self.p7 = es.enter_context(nc.psum_tensor("p7", [128, 512], F32))

        def sbt(name, shape, dt=F32):
            return es.enter_context(nc.sbuf_tensor(name, shape, dt))

        self.sbt = sbt
        self.c_ident = sbt("c_ident", [128, 128])
        self.c_mask = sbt("c_mask", [128, 128])
        self.c_bq = sbt("c_bq", [128, DCH])
        self.c_bk = sbt("c_bk", [128, DCH])
        self.c_bv = sbt("c_bv", [128, DCH])
        self.c_bo = sbt("c_bo", [128, DCH])
        self.c_gnw = sbt("c_gnw", [128, DCH])
        self.c_gnb = sbt("c_gnb", [128, DCH])
        self.c_gs = sbt("c_gs", [128, HCH])
        self.c_us = sbt("c_us", [128, HCH])
        self.c_ds = sbt("c_ds", [128, DCH])
        self.c_logd = sbt("c_logd", [128, DCH])
        self.c_ones1 = sbt("c_ones1", [128, 1])
        self.c_ones = sbt("c_ones", [128, 512])
        self.c_eps = sbt("c_eps", [128, 2])
        self.c_dummy = sbt("c_dummy", [1, 1])
        self.ln_cols = sbt("ln_cols", [128, 16])
        self.ln_slot = 0
        self.ln_prev = [None, None]

        self.ldw_patch = []
        self.block = es.enter_context(nc.Block())

    def on(self, engname, emit, wait=(), inc=1):
        out = {}
        sem = self.esem[engname]

        def f(eng):
            _wait_all(eng, wait)
            ins = emit(eng)
            out["ev"] = sem.inc(ins, inc)

        getattr(self.block, engname)(f)
        return out["ev"]

    def dma(self, q, out_ap, in_ap, wait=(), **kw):
        sem = self.dsem[q]
        res = {}

        def f(eng):
            _wait_all(eng, wait)
            res["ev"] = sem.inc(eng.dma_start(out=out_ap, in_=in_ap, **kw), 16)

        getattr(self.block, q)(f)
        return res["ev"]

    def dma_multi(self, q, pairs, wait=(), transpose=False):
        sem = self.dsem[q]
        res = {}

        def f(eng):
            _wait_all(eng, wait)
            for out_ap, in_ap in pairs:
                if transpose:
                    ins = eng.dma_start_transpose(out=out_ap, in_=in_ap)
                else:
                    ins = eng.dma_start(out=out_ap, in_=in_ap)
                res["ev"] = sem.inc(ins, 16)

        getattr(self.block, q)(f)
        return res["ev"]

    def barrier_ev(self):
        return [(s, s.v) for s in (self.sW, self.sA, self.sG, self.sPE,
                                   self.sDV, self.sAC, self.sPO) if s.v > 0]

    def aid(self, ac, out, in_):
        ac.activation(out=self.c_dummy[:], in_=self.c_eps[0:1, 0:1], func=AF.Exp)
        return ac.activation(out=out, in_=in_, func=AF.Identity, scale=1.0)

    def pin(self, ac, func=None):
        ac.activation(out=self.c_dummy[:], in_=self.c_eps[0:1, 0:1],
                      func=AF.Exp if func is None else func)

    def ln(self, x_ap, y_ap, wait, n=D, eps=EPS_LN):
        slot = self.ln_slot
        self.ln_slot ^= 1
        c = self.ln_cols[:, 8 * slot:8 * (slot + 1)]
        su, negmu, ssq, var0, mu2, rstd, nmr, lnv = (c[:, i:i + 1] for i in range(8))
        dump = y_ap
        inv_n = 1.0 / n
        w0 = list(wait) + ([self.ln_prev[slot]] if self.ln_prev[slot] else [])

        ev_su = self.on("vector", lambda dv: dv.tensor_reduce(
            out=su, in_=x_ap, axis=AX.X, op=ALU.add), wait=w0)

        def e_negmu(ac):
            self.pin(ac)
            ac.activation(out=negmu, in_=su, func=AF.Identity, scale=-inv_n)
            return ac.activation(out=dump, in_=x_ap, func=AF.Square, accum_out=ssq)
        ev_negmu = self.on("scalar", e_negmu, wait=w0 + [ev_su])

        ev_nv = self.on("vector", lambda dv: dv.scalar_tensor_tensor(
            out=var0, in0=negmu, scalar=-1.0, in1=negmu,
            op0=ALU.mult, op1=ALU.mult), wait=[ev_negmu])

        def e3(ac):
            ac.activation(out=mu2, in_=ssq, func=AF.Identity, scale=inv_n, bias=var0)
            ac.activation(out=lnv, in_=mu2, func=AF.Ln, bias=self.c_eps[:, 0:1] if eps == EPS_LN else self.c_eps[:, 1:2])
            return ac.activation(out=rstd, in_=lnv, func=AF.Exp, scale=-0.5)
        ev_rstd = self.on("scalar", e3, wait=[ev_nv])

        ev_nmr = self.on("vector", lambda dv: dv.tensor_tensor(
            out=nmr, in0=negmu, in1=rstd, op=ALU.mult), wait=[ev_rstd])

        def e_y(ac):
            self.pin(ac)
            return ac.activation(out=y_ap, in_=x_ap, func=AF.Identity,
                                 scale=rstd, bias=nmr)
        ev_done = self.on("scalar", e_y, wait=[ev_nmr])
        self.ln_prev[slot] = ev_done
        return ev_done

    def transpose_mac(self, src_subs, src_evs, dst, dst_cols, pp, pp_free):
        evs = []
        nch = len(dst_cols)
        for j in range(nch):
            slot = j % 2
            pb = pp[slot]

            def em(pe, j=j, pb=pb):
                for s in range(NSUB):
                    ins = pe.transpose(pb[:, 128 * s:128 * (s + 1)],
                                       src_subs[s][:, 128 * j:128 * (j + 1)], self.c_ident[:])
                return ins
            ev_tp = self.on("tensor", em, wait=list(src_evs) + [pp_free.get(slot)])

            eng = "vector" if j % 2 == 0 else "scalar"
            if eng == "vector":
                ev_ev = self.on(eng, lambda dv, j=j, pb=pb: dv.tensor_copy(
                    out=dst[:, dst_cols[j]:dst_cols[j] + 512], in_=pb[:]), wait=[ev_tp])
            else:
                ev_ev = self.on(eng, lambda ac, j=j, pb=pb: self.aid(
                    ac, dst[:, dst_cols[j]:dst_cols[j] + 512], pb[:]), wait=[ev_tp])
            pp_free[slot] = ev_ev
            evs.append(ev_ev)
        return evs


def emit_prologue(kb: KB):
    loads = [
        (kb.c_ident, kb.ident), (kb.c_mask, kb.mask),
        (kb.c_bq, kb.bq_c), (kb.c_bk, kb.bk_c), (kb.c_bv, kb.bv_c),
        (kb.c_bo, kb.bo_c), (kb.c_gnw, kb.gnw_c), (kb.c_gnb, kb.gnb_c),
        (kb.c_gs, kb.gs_c), (kb.c_us, kb.us_c), (kb.c_ds, kb.ds_c),
    ]
    kb.c_dect = kb.sbt("c_dect", [128, DCH])
    ev_ld = kb.dma_multi("sync", [(d[:], s[:]) for d, s in loads])
    ev_dec = kb.dma("sync", kb.c_dect[:], kb.decay_c[:])

    def e_ones(dv):
        dv.memset(kb.c_ones1[:], 1.0)
        dv.memset(kb.c_eps[:, 0:1], EPS_LN)
        dv.memset(kb.c_eps[:, 1:2], EPS_RMS)
        return dv.memset(kb.c_ones[:], 1.0)
    ev_ones = kb.on("vector", e_ones)

    ev_sig = kb.on("scalar", lambda ac: ac.activation(
        out=kb.c_dect[:], in_=kb.c_dect[:], func=AF.Sigmoid), wait=[ev_dec])
    ev_clip = kb.on("vector", lambda dv: dv.tensor_scalar_max(
        kb.c_dect[:], kb.c_dect[:], 1e-7), wait=[ev_sig])
    ev_logd = kb.on("scalar", lambda ac: ac.activation(
        out=kb.c_logd[:], in_=kb.c_dect[:], func=AF.Ln), wait=[ev_clip])

    kb.ev_consts = ev_ld
    kb.ev_logd = ev_logd
    kb.ev_ones = ev_ones


def emit_sb1(kb: KB, qs=None):
    nc = kb.nc
    entry = [kb.ev_consts, kb.ev_ones]

    with ExitStack() as st:
        def sbt(n, sh, dt=F32):
            return st.enter_context(nc.sbuf_tensor(n, sh, dt))
        kb.c_cos = sbt("c_cos", [128, S])
        kb.c_sin = sbt("c_sin", [128, S])
        kb.c_lng = sbt("c_lng", [128, D])
        kb.c_lnb = sbt("c_lnb", [128, D])
        ev_c1 = kb.dma_multi("sync", [(kb.c_cos[:], kb.cosf[:]), (kb.c_sin[:], kb.sinf[:]),
                                      (kb.c_lng[:], kb.lng_b[:]), (kb.c_lnb[:], kb.lnb_b[:])])
        entry = entry + [ev_c1]
        z_mac = sbt("z1_mac", [128, NSUB * D])
        zT = sbt("z1T", [128, DCH * 512])
        xbuf = [sbt(f"x1ld{i}", [128, D]) for i in range(2)]
        wbuf = [sbt(f"w1buf{i}", [128, DCH * 128]) for i in range(3)]
        stg = [sbt(f"stg1_{i}", [128, 512]) for i in range(2)]
        rop = [sbt(f"rop1_{i}", [128, 512]) for i in range(2)]
        rtmp = sbt("rtmp1", [128, 512])
        ru = sbt("ru1", [128, 512])
        vblk = [sbt(f"vblk1_{i}", [128, 512]) for i in range(2)]

        xbuf_free = [entry, entry]
        wbuf_free = [entry, entry, entry]
        pbank_free = [None, None]
        stg_free = [entry, entry]
        rop_free = [entry, entry]
        vblk_free = [entry, entry]
        vtp_free = [None, None]
        tpp_free = {}
        prev_tp_last = None

        for m in range(NMAC):
            b, tw = divmod(m, MPB)
            t0 = tw * MAC
            r0 = b * S + t0
            aff_evs = []
            for s in range(NSUB):
                buf = xbuf[s % 2]
                ev_x = kb.dma("scalar", buf[:], kb.x_in[r0 + 128 * s:r0 + 128 * (s + 1), :],
                              wait=[xbuf_free[s % 2]])
                yv = z_mac[:, D * s:D * (s + 1)]
                ev_ln = kb.ln(buf[:], yv, [ev_x, prev_tp_last])
                ev_ys = kb.dma("gpsimd", kb.y_s[r0 + 128 * s:r0 + 128 * (s + 1), :], yv,
                               wait=[ev_ln])
                xbuf_free[s % 2] = ev_ln

                def e_aff(dv, yv=yv):
                    dv.tensor_tensor(out=yv, in0=yv, in1=kb.c_lng[:], op=ALU.mult)
                    return dv.tensor_tensor(out=yv, in0=yv, in1=kb.c_lnb[:], op=ALU.add)
                aff_evs.append(kb.on("vector", e_aff, wait=[ev_ys]))

            src_subs = [z_mac[:, D * s:D * (s + 1)] for s in range(NSUB)]
            tp_evs = kb.transpose_mac(src_subs, aff_evs, zT,
                                      [512 * j for j in range(DCH)],
                                      (kb.PB[2], kb.PB[3]), tpp_free)
            zT_ready = tp_evs

            projs = [("q", kb.wq_t, kb.c_bq, kb.q_s), ("k", kb.wk_t, kb.c_bk, kb.k_s),
                     ("v", kb.wv_t, kb.c_bv, None)]
            for pi, (pname, wt, bias_c, dst_s) in enumerate(projs):
                for i in range(DCH):
                    g = pi * DCH + i
                    ws = g % 3
                    pb = kb.PB[g % 2]
                    ev_w = kb.dma("sync", wbuf[ws][:], wt[i].flatten_outer_dims(),
                                  wait=[wbuf_free[ws]])

                    def e_mm(pe, ws=ws, pb=pb):
                        for j in range(DCH):
                            ins = pe.matmul(pb, wbuf[ws][:, 128 * j:128 * (j + 1)],
                                            zT[:, 512 * j:512 * (j + 1)],
                                            start=(j == 0), stop=(j == DCH - 1))
                        return ins
                    ev_mm = kb.on("tensor", e_mm,
                                  wait=[ev_w, pbank_free[g % 2]] + list(zT_ready))
                    wbuf_free[ws] = ev_mm
                    prev_tp_last = ev_mm

                    sl = g % 2
                    if pname in ("q", "k"):
                        def e_ev(ac, pb=pb, sl=sl, i=i, bias_c=bias_c):
                            kb.pin(ac)
                            return ac.activation(out=stg[sl][:], in_=pb, func=AF.Identity,
                                                 bias=bias_c[:, i:i + 1])
                        ev_evac = kb.on("scalar", e_ev, wait=[ev_mm, stg_free[sl]])
                        pbank_free[g % 2] = ev_evac

                        def e_rope(dv, sl=sl, t0=t0):
                            cs = kb.c_cos[:, t0:t0 + 512]
                            sn = kb.c_sin[:, t0:t0 + 512]
                            dv.tensor_tensor(out=ru[:], in0=stg[sl][:], in1=sn, op=ALU.mult)
                            dv.tensor_copy(out=rtmp[0:64, :], in_=ru[64:128, :])
                            dv.tensor_copy(out=rtmp[64:128, :], in_=ru[0:64, :])
                            dv.tensor_tensor(out=rop[sl][:], in0=stg[sl][:], in1=cs, op=ALU.mult)
                            return dv.tensor_tensor(out=rop[sl][:], in0=rop[sl][:], in1=rtmp[:],
                                                    op=ALU.add)
                        ev_rope = kb.on("vector", e_rope, wait=[ev_evac, rop_free[sl]])
                        stg_free[sl] = ev_rope
                        ev_dma = kb.dma("gpsimd", dst_s[b, i, :, t0:t0 + 512], rop[sl][:],
                                        wait=[ev_rope])
                        rop_free[sl] = ev_dma
                    else:
                        def e_ev(ac, pb=pb, sl=sl, i=i):
                            kb.pin(ac)
                            return ac.activation(out=stg[sl][:], in_=pb, func=AF.Identity,
                                                 bias=kb.c_bv[:, i:i + 1])
                        ev_evac = kb.on("scalar", e_ev, wait=[ev_mm, stg_free[sl]])
                        pbank_free[g % 2] = ev_evac

                        pv = (kb.p4, kb.p5)[sl]

                        def e_vtp(pe, sl=sl, pv=pv):
                            for s in range(NSUB):
                                ins = pe.transpose(pv[:, 128 * s:128 * (s + 1)],
                                                   stg[sl][:, 128 * s:128 * (s + 1)],
                                                   kb.c_ident[:])
                            return ins
                        ev_vtp = kb.on("tensor", e_vtp, wait=[ev_evac, vtp_free[sl]])
                        stg_free[sl] = ev_vtp
                        ev_vcp = kb.on("vector", lambda dv, sl=sl, pv=pv: dv.tensor_copy(
                            out=vblk[sl][:], in_=pv[:]), wait=[ev_vtp, vblk_free[sl]])
                        vtp_free[sl] = ev_vcp
                        ev_vdma = kb.dma_multi("gpsimd", [
                            (kb.v_s[b, i, :, NSUB * tw + s, :], vblk[sl][:, 128 * s:128 * (s + 1)])
                            for s in range(NSUB)], wait=[ev_vcp])
                        vblk_free[sl] = ev_vdma

    barrier1 = kb.barrier_ev()

    with ExitStack() as st:
        def sbt(n, sh, dt=F32):
            return st.enter_context(nc.sbuf_tensor(n, sh, dt))
        qq2 = [sbt(f"qq1_{i}", [128, S]) for i in range(2)]
        kk2 = [sbt(f"kk1_{i}", [128, S]) for i in range(2)]
        vv2 = [sbt(f"vv1_{i}", [128, S]) for i in range(2)]
        p_sb = sbt("p_sb1", [128, S])
        pt_sb = [sbt(f"pt1_{i}", [128, 512]) for i in range(2)]
        o_stage = [sbt(f"ostg1_{i}", [128, 128]) for i in range(2)]
        acols = sbt("acols1", [128, 4])
        mxc, nmx, rsc, rrc = (acols[:, i:i + 1] for i in range(4))

        ld_free = [barrier1, barrier1]
        for b in range(BPC):
            for h in range(NH):
                if qs is not None:
                    qs.emit_next((len(qs.jobs) + BPC * NH - 1) // (BPC * NH))
                hsl = (b * NH + h) % 2
                qq, kk, vv = qq2[hsl], kk2[hsl], vv2[hsl]
                ev_ld = kb.dma_multi("sync", [
                    (qq[:], kb.q_s[b, h].flatten_outer_dims()),
                    (kk[:], kb.k_s[b, h].flatten_outer_dims()),
                    (vv[:], kb.v_s[b, h].flatten_outer_dims()),
                ], wait=[ld_free[hsl]])

                ev_scores_free = None
                ev_psb_free = barrier1
                prev_oev = None
                ostg_free = [barrier1, barrier1]
                ptp_free = {0: barrier1, 1: barrier1}
                ptsb_free = {0: barrier1, 1: barrier1}
                for i in range(S // 128):
                    ncols = 128 * (i + 1)

                    def e_sc(pe, i=i, ncols=ncols):
                        c0 = 0
                        while c0 < ncols:
                            nn = min(512, ncols - c0)
                            ins = pe.matmul(kb.ps_big[:, c0:c0 + nn],
                                            qq[:, 128 * i:128 * (i + 1)],
                                            kk[:, c0:c0 + nn], start=True, stop=True)
                            c0 += nn
                        return ins
                    ev_sc = kb.on("tensor", e_sc, wait=[ev_ld, ev_scores_free])

                    def e_mx(dv, i=i, ncols=ncols):
                        dv.tensor_tensor(out=kb.ps_big[:, 128 * i:128 * (i + 1)],
                                         in0=kb.ps_big[:, 128 * i:128 * (i + 1)],
                                         in1=kb.c_mask[:], op=ALU.add)
                        return dv.tensor_reduce(out=mxc, in_=kb.ps_big[:, 0:ncols],
                                                axis=AX.X, op=ALU.max)
                    ev_mx = kb.on("vector", e_mx, wait=[ev_sc])

                    ev_nmx = kb.on("gpsimd", lambda po: po.tensor_scalar_mul(
                        nmx, mxc, -SM_SCALE), wait=[ev_mx])

                    ev_exp = kb.on("scalar", lambda ac, ncols=ncols: ac.activation(
                        out=p_sb[:, 0:ncols], in_=kb.ps_big[:, 0:ncols], func=AF.Exp,
                        scale=SM_SCALE, bias=nmx, accum_out=rsc),
                        wait=[ev_nmx, ev_psb_free])
                    ev_scores_free = ev_exp

                    ev_rr = kb.on("vector", lambda dv: dv.reciprocal(out=rrc, in_=rsc),
                                  wait=[ev_exp])
                    def e_pn(ac, ncols=ncols):
                        kb.pin(ac)
                        return ac.activation(out=p_sb[:, 0:ncols], in_=p_sb[:, 0:ncols],
                                             func=AF.Identity, scale=rrc)
                    ev_pn = kb.on("scalar", e_pn, wait=[ev_rr])

                    ngrp = (i + 4) // 4
                    last_tp = None
                    ev_av = None
                    for gidx in range(ngrp):
                        k0 = 4 * gidx
                        kn = min(4, i + 1 - k0)
                        sl = gidx % 2
                        pv = (kb.p4, kb.p5)[sl]

                        def e_tp(pe, k0=k0, kn=kn, pv=pv):
                            for t in range(kn):
                                ins = pe.transpose(pv[:, 128 * t:128 * (t + 1)],
                                                   p_sb[:, 128 * (k0 + t):128 * (k0 + t + 1)],
                                                   kb.c_ident[:])
                            return ins
                        ev_tp = kb.on("tensor", e_tp, wait=[ev_pn, ptp_free[sl]])
                        last_tp = ev_tp

                        if sl == 0:
                            ev_cp = kb.on("vector", lambda dv, sl=sl, pv=pv, kn=kn:
                                          dv.tensor_copy(out=pt_sb[sl][:, 0:128 * kn],
                                                         in_=pv[:, 0:128 * kn]),
                                          wait=[ev_tp, ptsb_free[sl]])
                        else:
                            ev_cp = kb.on("scalar", lambda ac, sl=sl, pv=pv, kn=kn:
                                          kb.aid(ac, pt_sb[sl][:, 0:128 * kn],
                                                 pv[:, 0:128 * kn]),
                                          wait=[ev_tp, ptsb_free[sl]])
                        ptp_free[sl] = ev_cp

                        def e_av(pe, k0=k0, kn=kn, sl=sl, i=i):
                            for t in range(kn):
                                kc = k0 + t
                                ins = pe.matmul(kb.p6[:, 0:128], vv[:, 128 * kc:128 * (kc + 1)],
                                                pt_sb[sl][:, 128 * t:128 * (t + 1)],
                                                start=(kc == 0), stop=(kc == i))
                            return ins
                        ev_av = kb.on("tensor", e_av,
                                      wait=[ev_cp] + ([prev_oev] if gidx == 0 else []))
                        ptsb_free[sl] = ev_av

                    ev_psb_free = last_tp
                    osl = i % 2
                    ev_oev = kb.on("scalar", lambda ac, osl=osl: kb.aid(
                        ac, o_stage[osl][:], kb.p6[:, 0:128]),
                        wait=[ev_av, ostg_free[osl]])
                    prev_oev = ev_oev
                    ev_odma = kb.dma("gpsimd", kb.o_s[b, h, :, 128 * i:128 * (i + 1)],
                                     o_stage[osl][:], wait=[ev_oev])
                    ostg_free[osl] = ev_odma
                ld_free[hsl] = ev_odma

    barrier2 = kb.barrier_ev()

    with ExitStack() as st:
        def sbt(n, sh, dt=F32):
            return st.enter_context(nc.sbuf_tensor(n, sh, dt))
        obuf = sbt("obuf1", [128, DCH * 512])
        att_tm = sbt("att1_tm", [128, NSUB * D])
        wbuf = [sbt(f"w1cbuf{i}", [128, DCH * 128]) for i in range(2)]
        wo_stage = [sbt(f"wostg{i}", [128, 512]) for i in range(2)]
        ybuf = sbt("ybuf1", [128, D])
        x0buf = sbt("x0buf1", [128, D])

        wbuf_free = [barrier2, barrier2]
        pbank_free = [barrier2, barrier2]
        wostg_free = [barrier2, barrier2]
        tpp5_free = {0: barrier2, 1: barrier2}
        obuf_free = barrier2
        att_free = barrier2
        epi_prev = barrier2
        x1_evs = []

        for m in range(NMAC):
            b, tw = divmod(m, MPB)
            t0 = tw * MAC
            r0 = b * S + t0
            ev_old = kb.dma_multi("sync", [
                (obuf[:, 512 * hc:512 * (hc + 1)], kb.o_s[b, hc, :, t0:t0 + 512])
                for hc in range(DCH)], wait=[obuf_free])

            att_done = []
            for i in range(DCH):
                ws = i % 2
                pb = kb.PB[i % 2]
                ev_w = kb.dma("sync", wbuf[ws][:], kb.wo_t[i].flatten_outer_dims(),
                              wait=[wbuf_free[ws]])

                def e_mm(pe, ws=ws, pb=pb):
                    for j in range(DCH):
                        ins = pe.matmul(pb, wbuf[ws][:, 128 * j:128 * (j + 1)],
                                        obuf[:, 512 * j:512 * (j + 1)],
                                        start=(j == 0), stop=(j == DCH - 1))
                    return ins
                ev_mm = kb.on("tensor", e_mm, wait=[ev_w, ev_old, pbank_free[i % 2]])
                wbuf_free[ws] = ev_mm

                sl = i % 2
                ev_evac = kb.on("scalar", lambda ac, pb=pb, i=i, sl=sl: ac.activation(
                    out=wo_stage[sl][:], in_=pb, func=AF.Identity, bias=kb.c_bo[:, i:i + 1]),
                    wait=[ev_mm, wostg_free[sl]])
                pbank_free[i % 2] = ev_evac

                pv = (kb.p4, kb.p5)[sl]

                def e_tp(pe, sl=sl, pv=pv):
                    for s in range(NSUB):
                        ins = pe.transpose(pv[:, 128 * s:128 * (s + 1)],
                                           wo_stage[sl][:, 128 * s:128 * (s + 1)],
                                           kb.c_ident[:])
                    return ins
                ev_tp = kb.on("tensor", e_tp, wait=[ev_evac, tpp5_free[sl], att_free])
                wostg_free[sl] = ev_tp

                def e_cp(dv, sl=sl, pv=pv, i=i):
                    for s in range(NSUB):
                        ins = dv.tensor_copy(
                            out=att_tm[:, D * s + 128 * i:D * s + 128 * (i + 1)],
                            in_=pv[:, 128 * s:128 * (s + 1)])
                    return ins
                ev_cp = kb.on("vector", e_cp, wait=[ev_tp])
                tpp5_free[sl] = ev_cp
                att_done.append(ev_cp)
            obuf_free = ev_mm

            for s in range(NSUB):
                ev_yld = kb.dma_multi("scalar", [
                    (ybuf[:], kb.y_s[r0 + 128 * s:r0 + 128 * (s + 1), :]),
                    (x0buf[:], kb.x_in[r0 + 128 * s:r0 + 128 * (s + 1), :]),
                ], wait=[epi_prev])
                av = att_tm[:, D * s:D * (s + 1)]
                ev_fout = kb.on("vector", lambda dv, av=av: dv.tensor_tensor(
                    out=av, in0=av, in1=ybuf[:], op=ALU.add), wait=[ev_yld] + att_done)
                ev_ln = kb.ln(av, ybuf[:], [ev_fout])
                ev_x1 = kb.on("vector", lambda dv: dv.tensor_tensor(
                    out=x0buf[:], in0=x0buf[:], in1=ybuf[:], op=ALU.add), wait=[ev_ln])
                ev_st = kb.dma("scalar", kb.x1_s[r0 + 128 * s:r0 + 128 * (s + 1), :],
                               x0buf[:], wait=[ev_x1])
                epi_prev = ev_st
                x1_evs.append(ev_st)
            att_free = x1_evs[-1]

    return kb.x1_s, [x1_evs[-1]]


class QuantStream:

    def __init__(self, kb, st):
        nc = kb.nc
        self.kb = kb

        def sbt(n, sh, dt=F32):
            return st.enter_context(nc.sbuf_tensor(n, sh, dt))

        self.wt_in = [sbt(f"qw_in{i}", [128, D]) for i in range(2)]
        self.wb = sbt("qw_wb", [128, D], BF16)
        self.quot = sbt("qw_quot", [128, D], BF16)
        self.rq = sbt("qw_rq", [128, D])
        self.what = sbt("qw_hat", [128, D], BF16)
        qc = sbt("qw_cols", [128, 160])
        self.asum = qc[:, 0:32]
        self.sbf = qc[:, 32:64].bitcast(BF16)[:, 0:32]
        self.smax = qc[:, 64:96]
        self.thr = qc[:, 96:128]
        self.nthr = qc[:, 128:160]
        self.cv = [sbt(f"cv_in{i}", [128, D], BF16) for i in range(2)]
        self.tb = [sbt(f"qt_tb{i}", [128, 512], BF16) for i in range(2)]

        self.jobs = []
        for rt in range(2 * H_FFN // 128):
            self.jobs.append(("q", kb.wgu[128 * rt:128 * (rt + 1), :],
                              kb.wgub_s[128 * rt:128 * (rt + 1), :]))
        for rt in range(DCH):
            for cs in range(4):
                self.jobs.append(("q", kb.wp[128 * rt:128 * (rt + 1), 2048 * cs:2048 * (cs + 1)],
                                  kb.wpb_s[128 * rt:128 * (rt + 1), 2048 * cs:2048 * (cs + 1)]))
        for g in range(2 * H_FFN // 512):
            for j in range(DCH):
                self.jobs.append(("t", kb.wgub_s[512 * g:512 * (g + 1), 128 * j:128 * (j + 1)],
                                  kb.wgut_s[g, :, j, :]))
        for jw in range(HCH):
            for dg in range(4):
                self.jobs.append(("t4", kb.wpb_s[512 * dg:512 * (dg + 1), 128 * jw:128 * (jw + 1)],
                                  (jw, dg)))
        for ic in range(HCH):
            self.jobs.append(("c", kb.gw_t[ic].rearrange("p j c -> p (j c)"),
                              kb.gwb_s[ic].rearrange("p j c -> p (j c)")))
            self.jobs.append(("c", kb.uw_t[ic].rearrange("p j c -> p (j c)"),
                              kb.uwb_s[ic].rearrange("p j c -> p (j c)")))
        for dc in range(DCH):
            for hq in range(4):
                self.jobs.append(("c", kb.dw_t[dc].rearrange("p j c -> p (j c)")[:, 2048 * hq:2048 * (hq + 1)],
                                  kb.dwb_s[dc].rearrange("p j c -> p (j c)")[:, 2048 * hq:2048 * (hq + 1)]))
        self.idx = 0
        self.in_free = [None, None]
        self.cv_free = [None, None]
        self.pipe_free = None
        self.ev_done = None

    def emit_next(self, n):
        kb = self.kb
        end = min(self.idx + n, len(self.jobs))
        while self.idx < end:
            idx = self.idx
            kind, srcap, dstap = self.jobs[idx]
            self.idx += 1
            if kind == "q":
                sl = idx % 2
                wt_in, wb, quot, rq, what = self.wt_in, self.wb, self.quot, self.rq, self.what
                asum, sbf, smax = self.asum, self.sbf, self.smax
                thr, nthr = self.thr, self.nthr
                ev_in = kb.dma("scalar", wt_in[sl][:], srcap, wait=[self.in_free[sl]])

                def e_p1(dv, sl=sl):
                    dv.tensor_copy(out=wb[:], in_=wt_in[sl][:])
                    return dv.tensor_reduce(out=asum,
                                            in_=wb[:].rearrange("p (g k) -> p g k", k=64),
                                            axis=AX.X, op=ALU.add, apply_absolute_value=True)
                ev_p1 = kb.on("vector", e_p1, wait=[ev_in, self.pipe_free])
                self.in_free[sl] = ev_p1

                ev_s1 = kb.on("scalar", lambda ac: ac.activation(
                    out=sbf, in_=asum, func=AF.Identity, scale=1.0 / 64.0), wait=[ev_p1])
                ev_s2 = kb.on("vector", lambda dv: dv.tensor_scalar_max(
                    smax, sbf, BF16_CLIP_LO), wait=[ev_s1])

                def e_thr(ac):
                    ac.activation(out=thr, in_=smax, func=AF.Identity, scale=BF16_HALF_UP)
                    return ac.activation(out=nthr, in_=smax, func=AF.Identity,
                                         scale=-BF16_HALF_UP)
                ev_s3 = kb.on("scalar", e_thr, wait=[ev_s2])

                def e_cmp(dv):
                    dv.tensor_tensor(out=rq[:].rearrange("p (g k) -> p g k", k=64),
                                     in0=wb[:].rearrange("p (g k) -> p g k", k=64),
                                     in1=thr.unsqueeze(2).broadcast_to([128, 32, 64]),
                                     op=ALU.is_gt)
                    return dv.tensor_tensor(out=quot[:].rearrange("p (g k) -> p g k", k=64),
                                            in0=wb[:].rearrange("p (g k) -> p g k", k=64),
                                            in1=nthr.unsqueeze(2).broadcast_to([128, 32, 64]),
                                            op=ALU.is_lt)
                ev_cmp = kb.on("vector", e_cmp, wait=[ev_s3])

                def e_p2(dv):
                    dv.tensor_tensor(out=rq[:], in0=rq[:], in1=quot[:], op=ALU.subtract)
                    return dv.tensor_tensor(out=what[:].rearrange("p (g k) -> p g k", k=64),
                                            in0=rq[:].rearrange("p (g k) -> p g k", k=64),
                                            in1=smax.unsqueeze(2).broadcast_to([128, 32, 64]),
                                            op=ALU.mult)
                ev_p2 = kb.on("vector", e_p2, wait=[ev_cmp])
                self.ev_done = kb.dma("scalar", dstap, what[:], wait=[ev_p2])
                self.pipe_free = self.ev_done
            elif kind == "t":
                sl = idx % 2
                ev_in = kb.dma("scalar", self.tb[sl][:], srcap,
                               wait=[self.cv_free[sl], self.pipe_free], transpose=True)
                self.ev_done = kb.dma("scalar", dstap, self.tb[sl][:], wait=[ev_in])
                self.cv_free[sl] = self.ev_done
            elif kind == "t4":
                jw, dg = dstap
                sl = idx % 2
                ev_in = kb.dma("scalar", self.tb[sl][:], srcap,
                               wait=[self.cv_free[sl], self.pipe_free], transpose=True)
                ev_o = None
                for q in range(4):
                    ev_o = kb.dma("scalar", kb.wpt_s[4 * dg + q, :, jw, :],
                                  self.tb[sl][:, 128 * q:128 * (q + 1)],
                                  wait=[ev_in])
                self.ev_done = ev_o
                self.cv_free[sl] = ev_o
            else:
                sl = idx % 2
                cv = self.cv
                ev_in = kb.dma("gpsimd", cv[sl][:], srcap, wait=[self.cv_free[sl]])
                ev_sub = kb.on("vector", lambda dv, sl=sl: dv.tensor_scalar_add(
                    cv[sl][:], cv[sl][:], -128.0), wait=[ev_in])
                self.ev_done = kb.dma("gpsimd", dstap, cv[sl][:], wait=[ev_sub])
                self.cv_free[sl] = self.ev_done

    def finish(self):
        self.emit_next(len(self.jobs))
        return self.ev_done


def emit_sb2(kb: KB, x_src, in_evs):
    nc = kb.nc
    entry = kb.barrier_ev()

    with ExitStack() as stp:
        def sbtp(n, sh, dt=F32):
            return stp.enter_context(nc.sbuf_tensor(n, sh, dt))
        hhi = sbtp("hh2hi", [128, HCH * 512], BF16)
        hlo = sbtp("hh2lo", [128, HCH * 512], BF16)


        x2_evs = [entry]
        ffn_free = entry
        epi_prev = entry
        hh_free = entry
        p7_free = entry
        rb_free = entry
        prev_pe = entry

        for m in range(NMAC):
            b, tw = divmod(m, MPB)
            t0 = tw * MAC
            r0 = b * S + t0

            with ExitStack() as stA:
                def sbtA(n, sh, dt=F32):
                    return stA.enter_context(nc.sbuf_tensor(f"{n}_m{m}", sh, dt))
                yhi = sbtA("y2hi", [128, DCH * 512], BF16)
                ylo = sbtA("y2lo", [128, DCH * 512], BF16)
                stA1 = stA.enter_context(ExitStack())
                def sbtA1(n, sh, dt=F32):
                    return stA1.enter_context(nc.sbuf_tensor(f"{n}_m{m}", sh, dt))
                z_mac = sbtA1("z2", [128, 2 * D])
                xbuf = sbtA1("x2ld", [128, D])

                macro_gate = [epi_prev, prev_pe]
                tpp_free = {0: macro_gate, 1: macro_gate}
                xbuf_free = macro_gate
                last_lo = None
                z_free = [prev_pe, epi_prev]
                for sp in range(2):
                    ln_evs = []
                    for half in range(2):
                        s = 2 * sp + half
                        ev_x = kb.dma("scalar", xbuf[:],
                                      x_src[r0 + 128 * s:r0 + 128 * (s + 1), :],
                                      wait=[xbuf_free] + (in_evs if m == 0 and s == 0 else []))
                        ev_ln = kb.ln(xbuf[:], z_mac[:, D * half:D * (half + 1)],
                                      [ev_x] + z_free)
                        xbuf_free = ev_ln
                        ln_evs.append(ev_ln)
                    for j in range(DCH):
                        pv = (kb.PB[2], kb.PB[3])[j % 2]

                        def e_tp(pe, j=j, pv=pv):
                            for half in range(2):
                                ins = pe.transpose(pv[:, 128 * half:128 * (half + 1)],
                                                   z_mac[:, D * half + 128 * j:D * half + 128 * (j + 1)],
                                                   kb.c_ident[:])
                            return ins
                        ev_tp = kb.on("tensor", e_tp, wait=ln_evs + [tpp_free[j % 2]])

                        col = 512 * j + 256 * sp
                        ev_hi = kb.on("scalar", lambda ac, pv=pv, col=col: kb.aid(
                            ac, yhi[:, col:col + 256], pv[:, 0:256]), wait=[ev_tp])
                        ev_lo = kb.on("vector", lambda dv, pv=pv, col=col: dv.tensor_tensor(
                            out=ylo[:, col:col + 256], in0=pv[:, 0:256],
                            in1=yhi[:, col:col + 256], op=ALU.subtract), wait=[ev_hi])
                        tpp_free[j % 2] = ev_lo
                        last_lo = ev_lo
                        last_tp = ev_tp
                    z_free = [last_tp]
                y_ready = [last_lo]
                stA1.close()
                wgbuf = sbtA("wg2", [128, DCH * 512], BF16)
                wubuf = sbtA("wu2", [128, DCH * 512], BF16)
                sg = sbtA("sg2", [128, 512])
                hhf = sbtA("hhf2", [128, 512])
                hsq = sbtA("hsq2", [128, 512])

                wg_free = [last_tp] + macro_gate
                wu_free = [last_tp] + macro_gate
                sg_free = macro_gate
                hhf_free = macro_gate
                hsq_free = macro_gate
                pAB_free = {0: macro_gate, 1: macro_gate, 2: macro_gate, 3: macro_gate}
                ev_ssum = None
                for g in range(16):
                    ev_wg = kb.dma("sync", wgbuf[:], kb.wgut_s[g].rearrange("p j c -> p (j c)"),
                                   wait=[wg_free])
                    ev_wu = kb.dma("sync", wubuf[:], kb.wgut_s[16 + g].rearrange("p j c -> p (j c)"),
                                   wait=[wu_free])

                    for isub in range(4):
                        ic = 4 * g + isub
                        pg = kb.PB[2 * (ic % 2)]
                        pu = kb.PB[2 * (ic % 2) + 1]

                        def e_mm(pe, isub=isub, pg=pg, pu=pu):
                            for j in range(DCH):
                                wgt = wgbuf[:, 512 * j + 128 * isub:512 * j + 128 * (isub + 1)]
                                wut = wubuf[:, 512 * j + 128 * isub:512 * j + 128 * (isub + 1)]
                                pe.matmul(pg, wgt, yhi[:, 512 * j:512 * (j + 1)],
                                          start=(j == 0), stop=False)
                                kb.ldw_patch.append(pe.matmul(pg, wgt, ylo[:, 512 * j:512 * (j + 1)],
                                          start=False, stop=(j == DCH - 1)).ins)
                                pe.matmul(pu, wut, yhi[:, 512 * j:512 * (j + 1)],
                                          start=(j == 0), stop=False)
                                ins = pe.matmul(pu, wut, ylo[:, 512 * j:512 * (j + 1)],
                                                start=False, stop=(j == DCH - 1))
                                kb.ldw_patch.append(ins.ins)
                            return ins
                        ev_mm = kb.on("tensor", e_mm,
                                      wait=[ev_wg, ev_wu, pAB_free[2 * (ic % 2)],
                                            pAB_free[2 * (ic % 2) + 1]] + y_ready)
                        prev_pe = ev_mm
                        if isub == 3:
                            wg_free = ev_mm
                            wu_free = ev_mm

                        ev_sg = kb.on("scalar", lambda ac, pg=pg: ac.activation(
                            out=sg[:], in_=pg, func=AF.Silu), wait=[ev_mm, sg_free])
                        pAB_free[2 * (ic % 2)] = ev_sg

                        def e_fuse(dv, pu=pu, ic=ic):
                            dv.tensor_tensor(out=hhf[:], in0=sg[:], in1=pu, op=ALU.mult)
                            dv.tensor_copy(out=hhi[:, 512 * ic:512 * (ic + 1)], in_=hhf[:])
                            dv.tensor_tensor(out=hsq[:], in0=hhf[:], in1=hhf[:], op=ALU.mult)
                            return dv.tensor_tensor(
                                out=hlo[:, 512 * ic:512 * (ic + 1)], in0=hhf[:],
                                in1=hhi[:, 512 * ic:512 * (ic + 1)], op=ALU.subtract)
                        ev_lo2 = kb.on("vector", e_fuse,
                                       wait=[ev_sg, hhf_free, hsq_free, hh_free])
                        pAB_free[2 * (ic % 2) + 1] = ev_lo2
                        sg_free = ev_lo2
                        hhf_free = ev_lo2

                        def e_ssum(pe, ic=ic):
                            return pe.matmul(kb.p7[0:1, :], kb.c_ones1[:], hsq[:],
                                             start=(ic == 0), stop=(ic == HCH - 1))
                        ev_ssum = kb.on("tensor", e_ssum,
                                        wait=[ev_lo2] + ([p7_free] if ic == 0 else []))
                        hsq_free = ev_ssum
                        prev_pe = ev_ssum
                hh_last = [ev_lo2]

            with ExitStack() as stB:
                def sbtB(n, sh, dt=F32):
                    return stB.enter_context(nc.sbuf_tensor(f"{n}_m{m}", sh, dt))
                wpbuf = sbtB("wp2_0", [128, HCH * 128], BF16)
                rrow = sbtB("rrow2", [1, 512])
                rb = sbtB("rb2", [128, 512])
                ostg = [sbtB(f"ostg2_{i}", [128, 512]) for i in range(2)]
                ffn_tm = sbtB("ffn2", [128, NSUB * D])
                ybuf = sbtB("yb2", [128, D])
                x1buf = sbtB("x1b2", [128, D])

                def e_rr(ac):
                    ac.activation(out=rrow[0:1, :], in_=kb.p7[0:1, :], func=AF.Identity,
                                  scale=1.0 / H_FFN)
                    return ac.activation(out=rrow[0:1, :], in_=rrow[0:1, :], func=AF.Ln,
                                         bias=kb.c_eps[0:1, 1:2])
                ev_r1 = kb.on("scalar", e_rr, wait=[ev_ssum])
                p7_free = ev_r1
                ev_r2 = kb.on("scalar", lambda ac: ac.activation(
                    out=rrow[0:1, :], in_=rrow[0:1, :], func=AF.Exp, scale=-0.5),
                    wait=[ev_r1])

                ev_bc = kb.on("tensor", lambda pe: pe.matmul(
                    kb.p6[:, 0:512], kb.c_ones[0:1, 0:128], rrow[0:1, :],
                    start=True, stop=True), wait=[ev_r2, rb_free])
                ev_rb = kb.on("vector", lambda dv: dv.tensor_copy(
                    out=rb[:], in_=kb.p6[:, 0:512]), wait=[ev_bc])
                prev_pe = ev_bc

                HH2 = HCH // 2
                wh_free = [[prev_pe] + hh_last, [prev_pe] + hh_last]
                pd_free = [entry, entry]
                ostg_free = [entry, entry]
                tpB_free = {0: entry, 1: entry}
                tp_done = []
                rb_users = []
                for dc in range(DCH):
                    ev_wh = []
                    for h in range(2):
                        ev = kb.dma(
                            "sync", wpbuf[:, 128 * HH2 * h:128 * HH2 * (h + 1)],
                            kb.wpt_s[dc][:, HH2 * h:HH2 * (h + 1), :]
                            .rearrange("p j c -> p (j c)"),
                            wait=[wh_free[h]])
                        ev_wh.append(ev)

                    pd = (kb.PB[0], kb.PB[1])[dc % 2]

                    def e_dmmA(pe, pd=pd):
                        for j in range(HH2):
                            pe.matmul(pd, wpbuf[:, 128 * j:128 * (j + 1)],
                                      hhi[:, 512 * j:512 * (j + 1)],
                                      start=(j == 0), stop=False)
                            ins = pe.matmul(pd, wpbuf[:, 128 * j:128 * (j + 1)],
                                            hlo[:, 512 * j:512 * (j + 1)],
                                            start=False, stop=False)
                            kb.ldw_patch.append(ins.ins)
                        return ins
                    ev_dA = kb.on("tensor", e_dmmA,
                                  wait=[ev_wh[0], pd_free[dc % 2]] + hh_last)
                    wh_free[0] = ev_dA

                    def e_dmmB(pe, pd=pd):
                        for j in range(HH2, HCH):
                            pe.matmul(pd, wpbuf[:, 128 * j:128 * (j + 1)],
                                      hhi[:, 512 * j:512 * (j + 1)],
                                      start=False, stop=False)
                            ins = pe.matmul(pd, wpbuf[:, 128 * j:128 * (j + 1)],
                                            hlo[:, 512 * j:512 * (j + 1)],
                                            start=False, stop=(j == HCH - 1))
                            kb.ldw_patch.append(ins.ins)
                        return ins
                    ev_dmm = kb.on("tensor", e_dmmB, wait=[ev_wh[1]])
                    wh_free[1] = ev_dmm
                    prev_pe = ev_dmm

                    osl = dc % 2
                    ev_oev = kb.on("vector", lambda dv, pd=pd, osl=osl: dv.tensor_tensor(
                        out=ostg[osl][:], in0=pd, in1=rb[:], op=ALU.mult),
                        wait=[ev_dmm, ostg_free[osl], ev_rb])
                    pd_free[dc % 2] = ev_oev
                    rb_users.append(ev_oev)

                    pv = (kb.p4, kb.p5)[osl]

                    def e_tp(pe, pv=pv, osl=osl):
                        for s in range(NSUB):
                            ins = pe.transpose(pv[:, 128 * s:128 * (s + 1)],
                                               ostg[osl][:, 128 * s:128 * (s + 1)],
                                               kb.c_ident[:])
                        return ins
                    ev_tp = kb.on("tensor", e_tp, wait=[ev_oev, tpB_free[osl], ffn_free])
                    ostg_free[osl] = ev_tp
                    prev_pe = ev_tp

                    def e_cp(ac, pv=pv, dc=dc):
                        for s in range(NSUB):
                            ins = kb.aid(ac, ffn_tm[:, D * s + 128 * dc:D * s + 128 * (dc + 1)],
                                         pv[:, 128 * s:128 * (s + 1)])
                        return ins
                    ev_cp = kb.on("scalar", e_cp, wait=[ev_tp])
                    tpB_free[osl] = ev_cp
                    tp_done.append(ev_cp)
                hh_free = prev_pe
                rb_free = rb_users[-1]

                for s in range(NSUB):
                    ev_x1 = kb.dma("scalar", x1buf[:],
                                   x_src[r0 + 128 * s:r0 + 128 * (s + 1), :],
                                   wait=[epi_prev])
                    fv = ffn_tm[:, D * s:D * (s + 1)]
                    ev_ln = kb.ln(fv, ybuf[:], [ev_x1] + tp_done)
                    ev_x2 = kb.on("vector", lambda dv: dv.tensor_tensor(
                        out=x1buf[:], in0=x1buf[:], in1=ybuf[:], op=ALU.add), wait=[ev_ln])
                    ev_st = kb.dma("scalar", kb.x2_s[r0 + 128 * s:r0 + 128 * (s + 1), :],
                                   x1buf[:], wait=[ev_x2])
                    epi_prev = ev_st
                    x2_evs.append(ev_st)
                ffn_free = x2_evs[-1]

    return kb.x2_s, [x2_evs[-1]]




def emit_sb3(kb: KB, x_src, in_evs):
    nc = kb.nc
    entry = kb.barrier_ev()

    with ExitStack() as stp:
        def sbtp(n, sh, dt=F32):
            return stp.enter_context(nc.sbuf_tensor(n, sh, dt))
        kb.c_iota = sbtp("c_iota", [128, S])
        ev_iota = kb.dma("sync", kb.c_iota[:], kb.iota[:], wait=entry)
        carry = sbtp("carry3", [128, DCH])
        acc_s = sbtp("acc3_s", [128, DCH])
        acc_q = sbtp("acc3_q", [128, DCH])
        srow = sbtp("srow3", [1, 4])
        stc = sbtp("stc3", [128, 8])
        A_col = sbtp("A3", [128, DCH])
        B_col = sbtp("B3", [128, DCH])

        x3_evs = [entry]
        epi_prev = entry
        prev_pe = entry
        att_rd_free = entry
        ab_users = entry

        for b in range(BPC):
            def e_rst(po):
                po.memset(carry[:], 0.0)
                po.memset(acc_s[:], 0.0)
                return po.memset(acc_q[:], 0.0)
            ev_rst = kb.on("gpsimd", e_rst, wait=[ab_users])

            with ExitStack() as stA:
                def sbtA(n, sh, dt=F32):
                    return stA.enter_context(nc.sbuf_tensor(f"{n}_b{b}", sh, dt))
                z_mac = sbtA("z3", [128, NSUB * D])
                xbuf = sbtA("x3ld", [128, D])
                zT = sbtA("z3T", [128, DCH * 512])
                r_all = sbtA("r3", [128, DCH * 512])
                kv_all = sbtA("kv3", [128, DCH * 512])
                wbuf = [sbtA(f"w3buf{i}", [128, DCH * 128]) for i in range(3)]
                scl = [sbtA(f"scl3_{i}", [128, 512]) for i in range(2)]
                arg = sbtA("arg3", [128, 512])
                clp = sbtA("clp3", [128, 512])
                inv = sbtA("inv3", [128, 512])
                scd = sbtA("scd3", [128, 512])
                cs = [sbtA(f"cs3_{i}", [128, 512]) for i in range(2)]
                stt_ = [sbtA(f"st3_{i}", [128, 512]) for i in range(2)]
                attst = [sbtA(f"att3_{i}", [128, 512]) for i in range(2)]
                sqd = sbtA("sqd3", [128, 512])

                macro_gate = [epi_prev, prev_pe, ev_rst]
                xbuf_free = macro_gate
                wbuf_free = [macro_gate, macro_gate, macro_gate]
                pbank_free = {0: macro_gate, 1: macro_gate}
                tpp_free = {0: macro_gate, 1: macro_gate}
                kcol_free = {}
                scl_free = [macro_gate, macro_gate]
                cs_free = [macro_gate, macro_gate]
                st_free = [macro_gate, macro_gate]
                att_free = [macro_gate, macro_gate]
                arg_free = macro_gate
                clp_free = macro_gate
                inv_free = macro_gate
                scd_free = macro_gate
                sqd_free = macro_gate
                racc_prev = macro_gate
                qacc_prev = macro_gate
                carry_rd = {c: [ev_rst] for c in range(DCH)}
                r_free = {c: macro_gate for c in range(DCH)}
                kv_free = {c: macro_gate for c in range(DCH)}

                for tw in range(MPB):
                    t0 = tw * MAC
                    r0 = b * S + t0
                    ln_evs = []
                    for s in range(NSUB):
                        ev_x = kb.dma("scalar", xbuf[:],
                                      x_src[r0 + 128 * s:r0 + 128 * (s + 1), :],
                                      wait=[xbuf_free] + (in_evs if b == 0 and tw == 0 and s == 0 else []))
                        ev_ln = kb.ln(xbuf[:], z_mac[:, D * s:D * (s + 1)],
                                      [ev_x] + macro_gate)
                        xbuf_free = ev_ln
                        ln_evs.append(ev_ln)
                    tp_evs = kb.transpose_mac(
                        [z_mac[:, D * s:D * (s + 1)] for s in range(NSUB)],
                        ln_evs, zT, [512 * j for j in range(DCH)],
                        (kb.PB[2], kb.PB[3]), tpp_free)
                    zT_ready = [tp_evs[-1]]
                    prev_pe = tp_evs[-1]

                    kv_done = {}
                    r_done = {}
                    for pi, wt in enumerate([kb.wr1_t, kb.wk1_t, kb.wv1_t]):
                        for i in range(DCH):
                            g = pi * DCH + i
                            ws = g % 3
                            pb = kb.PB[g % 2]
                            ev_w = kb.dma("sync", wbuf[ws][:], wt[i].flatten_outer_dims(),
                                          wait=[wbuf_free[ws]])

                            def e_mm(pe, ws=ws, pb=pb):
                                for j in range(DCH):
                                    ins = pe.matmul(pb, wbuf[ws][:, 128 * j:128 * (j + 1)],
                                                    zT[:, 512 * j:512 * (j + 1)],
                                                    start=(j == 0), stop=(j == DCH - 1))
                                return ins
                            ev_mm = kb.on("tensor", e_mm,
                                          wait=[ev_w, pbank_free[g % 2]] + zT_ready)
                            wbuf_free[ws] = ev_mm
                            prev_pe = ev_mm

                            if pi == 0:
                                ev_ev = kb.on("scalar", lambda ac, pb=pb, i=i: ac.activation(
                                    out=r_all[:, 512 * i:512 * (i + 1)], in_=pb,
                                    func=AF.Sigmoid), wait=[ev_mm, r_free[i]])
                                r_done[i] = ev_ev
                            elif pi == 1:
                                ev_ev = kb.on("scalar", lambda ac, pb=pb, i=i: kb.aid(
                                    ac, kv_all[:, 512 * i:512 * (i + 1)], pb),
                                    wait=[ev_mm, kv_free[i]])
                                kcol_free[i] = ev_ev
                            else:
                                ev_ev = kb.on("vector", lambda dv, pb=pb, i=i: dv.tensor_tensor(
                                    out=kv_all[:, 512 * i:512 * (i + 1)],
                                    in0=kv_all[:, 512 * i:512 * (i + 1)],
                                    in1=pb, op=ALU.mult), wait=[ev_mm, kcol_free[i]])
                                kv_done[i] = ev_ev
                            pbank_free[g % 2] = ev_ev

                    for c in range(DCH):
                        sl = c % 2
                        it = kb.c_iota[:, t0:t0 + 512]
                        ld_col = kb.c_logd[:, c:c + 1]

                        def e_arg(dv, it=it, ld_col=ld_col):
                            dv.tensor_scalar_mul(arg[:], it, ld_col)
                            return dv.tensor_scalar(out=clp[:], in0=it, scalar1=ld_col,
                                                    scalar2=LN_CLIP_LO,
                                                    op0=ALU.mult, op1=ALU.max)
                        ev_arg = kb.on("vector", e_arg,
                                       wait=[ev_iota, kb.ev_logd, arg_free, clp_free])

                        def e_exp(ac, sl=sl):
                            ac.activation(out=scl[sl][:], in_=arg[:], func=AF.Exp)
                            return ac.activation(out=inv[:], in_=clp[:], func=AF.Exp,
                                                 scale=-1.0)
                        ev_scl = kb.on("scalar", e_exp,
                                       wait=[ev_arg, scl_free[sl], inv_free])
                        arg_free = ev_scl
                        clp_free = ev_scl

                        def e_scan(dv, sl=sl, c=c):
                            dv.tensor_tensor(out=scd[:], in0=kv_all[:, 512 * c:512 * (c + 1)],
                                             in1=inv[:], op=ALU.mult)
                            return dv.tensor_tensor_scan(
                                out=cs[sl][:], data0=kb.c_ones[:], data1=scd[:],
                                initial=carry[:, c:c + 1], op0=ALU.mult, op1=ALU.add)
                        ev_cs = kb.on("vector", e_scan,
                                      wait=[ev_scl, kv_done[c], scd_free, cs_free[sl]]
                                      + carry_rd[c] + [kb.ev_ones])
                        inv_free = ev_cs
                        scd_free = ev_cs
                        kv_free[c] = ev_cs
                        ev_cw = kb.on("scalar", lambda ac, sl=sl, c=c: kb.aid(
                            ac, carry[:, c:c + 1], cs[sl][:, 511:512]), wait=[ev_cs])
                        carry_rd[c] = [ev_cw]

                        def e_att(dv, sl=sl, c=c):
                            dv.tensor_tensor(out=stt_[sl][:], in0=cs[sl][:], in1=scl[sl][:],
                                             op=ALU.mult)
                            return dv.tensor_tensor(out=attst[sl][:], in0=stt_[sl][:],
                                                    in1=r_all[:, 512 * c:512 * (c + 1)],
                                                    op=ALU.mult)
                        ev_att = kb.on("vector", e_att,
                                       wait=[ev_cw, r_done[c], st_free[sl], att_free[sl]])
                        cs_free[sl] = [ev_att, ev_cw]
                        scl_free[sl] = ev_att
                        st_free[sl] = ev_att
                        r_free[c] = ev_att

                        def e_stats(dv, sl=sl):
                            dv.tensor_reduce(out=stc[:, 4:5], in_=attst[sl][:], axis=AX.X,
                                             op=ALU.add)
                            dv.tensor_tensor(out=sqd[:], in0=attst[sl][:], in1=attst[sl][:],
                                             op=ALU.mult)
                            return dv.tensor_reduce(out=stc[:, 5:6], in_=sqd[:], axis=AX.X,
                                                    op=ALU.add)
                        ev_stats = kb.on("vector", e_stats,
                                         wait=[ev_att, sqd_free, racc_prev, qacc_prev])

                        def e_acc(ac, c=c):
                            kb.pin(ac)
                            ac.activation(out=acc_s[:, c:c + 1], in_=acc_s[:, c:c + 1],
                                          func=AF.Identity, bias=stc[:, 4:5])
                            return ac.activation(out=acc_q[:, c:c + 1], in_=acc_q[:, c:c + 1],
                                                 func=AF.Identity, bias=stc[:, 5:6])
                        ev_acc = kb.on("scalar", e_acc, wait=[ev_stats])
                        racc_prev = ev_acc
                        qacc_prev = ev_acc
                        sqd_free = ev_stats
                        ev_dma = kb.dma("gpsimd", kb.att_s[b, c, :, t0:t0 + 512],
                                        attst[sl][:], wait=[ev_att])
                        att_free[sl] = [ev_dma, ev_stats]
                ab_last = [racc_prev, qacc_prev]
                stat_dma = (kb.sW, kb.sW.v)

            def e_sums(pe):
                pe.matmul(kb.p7[0:1, 0:16], kb.c_ones1[:], acc_s[:], start=True, stop=True)
                return pe.matmul(kb.p7[0:1, 16:32], kb.c_ones1[:], acc_q[:],
                                 start=False, stop=True)
            ev_sums = kb.on("tensor", e_sums, wait=ab_last + [prev_pe])
            prev_pe = ev_sums

            def e_red(dv):
                dv.tensor_reduce(out=srow[0:1, 0:1], in_=kb.p7[0:1, 0:16], axis=AX.X,
                                 op=ALU.add)
                return dv.tensor_reduce(out=srow[0:1, 1:2], in_=kb.p7[0:1, 16:32],
                                        axis=AX.X, op=ALU.add)
            ev_red = kb.on("vector", e_red, wait=[ev_sums])

            invN = 1.0 / (S * D)

            def e_mm2(ac):
                ac.activation(out=srow[0:1, 2:3], in_=srow[0:1, 0:1], func=AF.Identity,
                              scale=invN)
                return ac.activation(out=srow[0:1, 3:4], in_=srow[0:1, 1:2],
                                     func=AF.Identity, scale=invN)
            ev_m2 = kb.on("scalar", e_mm2, wait=[ev_red])

            ev_nm2 = kb.on("vector", lambda dv: dv.scalar_tensor_tensor(
                out=srow[0:1, 0:1], in0=srow[0:1, 2:3], scalar=-1.0,
                in1=srow[0:1, 2:3], op0=ALU.mult, op1=ALU.mult), wait=[ev_m2])

            def e_rstd(ac):
                ac.activation(out=srow[0:1, 1:2], in_=srow[0:1, 3:4], func=AF.Identity,
                              bias=srow[0:1, 0:1])
                ac.activation(out=srow[0:1, 1:2], in_=srow[0:1, 1:2], func=AF.Ln,
                              bias=kb.c_eps[0:1, 0:1])
                return ac.activation(out=srow[0:1, 1:2], in_=srow[0:1, 1:2],
                                     func=AF.Exp, scale=-0.5)
            ev_rstd = kb.on("scalar", e_rstd, wait=[ev_nm2])

            def e_bc(pe):
                return pe.matmul(kb.p7[:, 32:34], kb.c_ones[0:1, 0:128], srow[0:1, 1:3],
                                 start=True, stop=True)
            ev_bc = kb.on("tensor", e_bc, wait=[ev_rstd])
            prev_pe = ev_bc

            ev_bcc = kb.on("vector", lambda dv: dv.tensor_copy(
                out=stc[:, 0:2], in_=kb.p7[:, 32:34]), wait=[ev_bc])

            ev_A = kb.on("scalar", lambda ac: ac.activation(
                out=A_col[:], in_=kb.c_gnw[:], func=AF.Identity, scale=stc[:, 0:1]),
                wait=[ev_bcc])
            ev_ngm = kb.on("vector", lambda dv: dv.tensor_scalar_mul(
                stc[:, 2:3], stc[:, 1:2], -1.0), wait=[ev_A])
            ev_t1 = kb.on("gpsimd", lambda po: po.tensor_scalar_mul(
                B_col[:], A_col[:], stc[:, 2:3]), wait=[ev_ngm])
            ev_B = kb.on("vector", lambda dv: dv.tensor_tensor(
                out=B_col[:], in0=B_col[:], in1=kb.c_gnb[:], op=ALU.add), wait=[ev_t1])
            ab_users = None

            with ExitStack() as stC:
                def sbtC(n, sh, dt=F32):
                    return stC.enter_context(nc.sbuf_tensor(f"{n}_b{b}", sh, dt))
                att_all = sbtC("attld3", [128, DCH * 512])
                wbuf = [sbtC(f"w3cbuf{i}", [128, DCH * 128]) for i in range(2)]
                wo_stage = [sbtC(f"wostg3_{i}", [128, 512]) for i in range(2)]
                ffn_tm = sbtC("att3_tm", [128, NSUB * D])
                ybuf = sbtC("yb3", [128, D])
                x2buf = sbtC("x2b3", [128, D])

                cgate = [prev_pe, epi_prev, ev_B, stat_dma]
                wbuf_free = [cgate, cgate]
                pbank_free = [cgate, cgate]
                wostg_free = [cgate, cgate]
                tpc_free = {0: cgate, 1: cgate}
                att_ld_free = cgate
                ffn_free = cgate
                ab_user_evs = []

                for tw in range(MPB):
                    t0 = tw * MAC
                    r0 = b * S + t0
                    ev_ald = kb.dma_multi("scalar", [
                        (att_all[:, 512 * c:512 * (c + 1)], kb.att_s[b, c, :, t0:t0 + 512])
                        for c in range(DCH)], wait=[att_ld_free])

                    nrm_evs = []
                    for c in range(DCH):
                        ev_n = kb.on("vector", lambda dv, c=c: dv.tensor_scalar(
                            out=att_all[:, 512 * c:512 * (c + 1)],
                            in0=att_all[:, 512 * c:512 * (c + 1)],
                            scalar1=A_col[:, c:c + 1], scalar2=B_col[:, c:c + 1],
                            op0=ALU.mult, op1=ALU.add), wait=[ev_ald, ev_B])
                        nrm_evs.append(ev_n)
                        ab_user_evs.append(ev_n)

                    tp_done = []
                    for i in range(DCH):
                        ws = i % 2
                        pb = kb.PB[i % 2]
                        ev_w = kb.dma("sync", wbuf[ws][:], kb.wo1_t[i].flatten_outer_dims(),
                                      wait=[wbuf_free[ws]])

                        def e_mm(pe, ws=ws, pb=pb):
                            for j in range(DCH):
                                ins = pe.matmul(pb, wbuf[ws][:, 128 * j:128 * (j + 1)],
                                                att_all[:, 512 * j:512 * (j + 1)],
                                                start=(j == 0), stop=(j == DCH - 1))
                            return ins
                        ev_mm = kb.on("tensor", e_mm,
                                      wait=[ev_w, pbank_free[i % 2]] + nrm_evs)
                        wbuf_free[ws] = ev_mm
                        prev_pe = ev_mm

                        sl = i % 2
                        ev_evac = kb.on("scalar", lambda ac, pb=pb, sl=sl: kb.aid(
                            ac, wo_stage[sl][:], pb), wait=[ev_mm, wostg_free[sl]])
                        pbank_free[i % 2] = ev_evac

                        pv = (kb.p4, kb.p5)[sl]

                        def e_tp(pe, sl=sl, pv=pv):
                            for s in range(NSUB):
                                ins = pe.transpose(pv[:, 128 * s:128 * (s + 1)],
                                                   wo_stage[sl][:, 128 * s:128 * (s + 1)],
                                                   kb.c_ident[:])
                            return ins
                        ev_tp = kb.on("tensor", e_tp,
                                      wait=[ev_evac, tpc_free[sl], ffn_free])
                        wostg_free[sl] = ev_tp
                        prev_pe = ev_tp

                        def e_cp(dv, pv=pv, i=i):
                            for s in range(NSUB):
                                ins = dv.tensor_copy(
                                    out=ffn_tm[:, D * s + 128 * i:D * s + 128 * (i + 1)],
                                    in_=pv[:, 128 * s:128 * (s + 1)])
                            return ins
                        ev_cp = kb.on("vector", e_cp, wait=[ev_tp])
                        tpc_free[sl] = ev_cp
                        tp_done.append(ev_cp)
                    att_ld_free = ev_mm

                    for s in range(NSUB):
                        ev_x2 = kb.dma("scalar", x2buf[:],
                                       x_src[r0 + 128 * s:r0 + 128 * (s + 1), :],
                                       wait=[epi_prev])
                        fv = ffn_tm[:, D * s:D * (s + 1)]
                        ev_ln = kb.ln(fv, ybuf[:], [ev_x2] + tp_done)
                        ev_x3 = kb.on("vector", lambda dv: dv.tensor_tensor(
                            out=x2buf[:], in0=x2buf[:], in1=ybuf[:], op=ALU.add),
                            wait=[ev_ln])
                        ev_st = kb.dma("scalar", kb.x3_s[r0 + 128 * s:r0 + 128 * (s + 1), :],
                                       x2buf[:], wait=[ev_x3])
                        epi_prev = ev_st
                        x3_evs.append(ev_st)
                    ffn_free = x3_evs[-1]
                ab_users = ab_user_evs[-1]

    return kb.x3_s, [x3_evs[-1]]




def emit_sb4(kb: KB, x_src, in_evs):
    nc = kb.nc
    entry = kb.barrier_ev()

    with ExitStack() as stp:
        def sbtp(n, sh, dt=F32):
            return stp.enter_context(nc.sbuf_tensor(n, sh, dt))
        hhi = sbtp("hh4hi", [128, HCH * 512], BF16)
        hlo = sbtp("hh4lo", [128, HCH * 512], BF16)
        wdbuf0 = sbtp("wd4p", [128, HCH * 128], BF16)

        out_evs = [entry]
        epi_prev = entry
        hh_free = entry
        prev_pe = entry
        ffn_free = entry

        for m in range(NMAC):
            b, tw = divmod(m, MPB)
            t0 = tw * MAC
            r0 = b * S + t0

            with ExitStack() as stA:
                def sbtA(n, sh, dt=F32):
                    return stA.enter_context(nc.sbuf_tensor(f"{n}_m{m}", sh, dt))
                yhi = sbtA("y4hi", [128, DCH * 512], BF16)
                ylo = sbtA("y4lo", [128, DCH * 512], BF16)
                stA1 = stA.enter_context(ExitStack())

                def sbtA1(n, sh, dt=F32):
                    return stA1.enter_context(nc.sbuf_tensor(f"{n}_m{m}", sh, dt))
                z_mac = sbtA1("z4", [128, 2 * D])
                xbuf = sbtA1("x4ld", [128, D])

                macro_gate = [epi_prev, prev_pe]
                tpp_free = {0: macro_gate, 1: macro_gate}
                xbuf_free = macro_gate
                last_lo = None
                last_tp = None
                z_free = macro_gate
                for sp in range(2):
                    ln_evs = []
                    for half in range(2):
                        s = 2 * sp + half
                        ev_x = kb.dma("scalar", xbuf[:],
                                      x_src[r0 + 128 * s:r0 + 128 * (s + 1), :],
                                      wait=[xbuf_free] + (in_evs if m == 0 and s == 0 else []))
                        ev_ln = kb.ln(xbuf[:], z_mac[:, D * half:D * (half + 1)],
                                      [ev_x] + z_free)
                        xbuf_free = ev_ln
                        ln_evs.append(ev_ln)
                    for j in range(DCH):
                        pv = (kb.PB[2], kb.PB[3])[j % 2]

                        def e_tp(pe, j=j, pv=pv):
                            for half in range(2):
                                ins = pe.transpose(pv[:, 128 * half:128 * (half + 1)],
                                                   z_mac[:, D * half + 128 * j:D * half + 128 * (j + 1)],
                                                   kb.c_ident[:])
                            return ins
                        ev_tp = kb.on("tensor", e_tp, wait=ln_evs + [tpp_free[j % 2]])

                        col = 512 * j + 256 * sp
                        ev_hi = kb.on("scalar", lambda ac, pv=pv, col=col: kb.aid(
                            ac, yhi[:, col:col + 256], pv[:, 0:256]), wait=[ev_tp])
                        ev_lo = kb.on("vector", lambda dv, pv=pv, col=col: dv.tensor_tensor(
                            out=ylo[:, col:col + 256], in0=pv[:, 0:256],
                            in1=yhi[:, col:col + 256], op=ALU.subtract), wait=[ev_hi])
                        tpp_free[j % 2] = ev_lo
                        last_lo = ev_lo
                        last_tp = ev_tp
                    z_free = [last_tp]
                y_ready = [last_lo]
                stA1.close()
                wgbuf = [sbtA(f"wg4_{i}", [128, DCH * 128], BF16) for i in range(2)]
                wubuf = [sbtA(f"wu4_{i}", [128, DCH * 128], BF16) for i in range(2)]
                sgb = sbtA("sgb4", [128, 512])
                ub = sbtA("ub4", [128, 512])
                t1 = sbtA("t1_4", [128, 512])
                hhf = sbtA("hhf4", [128, 512])

                wg_free = [[last_tp] + macro_gate, [last_tp] + macro_gate]
                sgb_free = macro_gate
                ub_free = macro_gate
                t1_free = macro_gate
                hhf_free = macro_gate
                pAB_free = {0: macro_gate, 1: macro_gate, 2: macro_gate, 3: macro_gate}

                for ic in range(HCH):
                    wsl = ic % 2
                    ev_wg = kb.dma("sync", wgbuf[wsl][:],
                                   kb.gwb_s[ic].rearrange("p j c -> p (j c)"),
                                   wait=[wg_free[wsl]])
                    ev_wu = kb.dma("sync", wubuf[wsl][:],
                                   kb.uwb_s[ic].rearrange("p j c -> p (j c)"),
                                   wait=[wg_free[wsl]])
                    pg = kb.PB[2 * (ic % 2)]
                    pu = kb.PB[2 * (ic % 2) + 1]

                    def e_mm(pe, wsl=wsl, pg=pg, pu=pu):
                        for j in range(DCH):
                            wgt = wgbuf[wsl][:, 128 * j:128 * (j + 1)]
                            wut = wubuf[wsl][:, 128 * j:128 * (j + 1)]
                            pe.matmul(pg, wgt, yhi[:, 512 * j:512 * (j + 1)],
                                      start=(j == 0), stop=False)
                            kb.ldw_patch.append(pe.matmul(pg, wgt, ylo[:, 512 * j:512 * (j + 1)],
                                      start=False, stop=(j == DCH - 1)).ins)
                            pe.matmul(pu, wut, yhi[:, 512 * j:512 * (j + 1)],
                                      start=(j == 0), stop=False)
                            ins = pe.matmul(pu, wut, ylo[:, 512 * j:512 * (j + 1)],
                                            start=False, stop=(j == DCH - 1))
                            kb.ldw_patch.append(ins.ins)
                        return ins
                    ev_mm = kb.on("tensor", e_mm,
                                  wait=[ev_wg, ev_wu, pAB_free[2 * (ic % 2)],
                                        pAB_free[2 * (ic % 2) + 1]] + y_ready)
                    prev_pe = ev_mm
                    wg_free[wsl] = ev_mm

                    gs_col = kb.c_gs[:, ic:ic + 1]
                    us_col = kb.c_us[:, ic:ic + 1]
                    ev_sgb = kb.on("scalar", lambda ac, pg=pg, gs_col=gs_col: ac.activation(
                        out=sgb[:], in_=pg, func=AF.Sigmoid, scale=gs_col),
                        wait=[ev_mm, sgb_free])

                    def e_fuse(dv, pg=pg, pu=pu, ic=ic, gs_col=gs_col, us_col=us_col):
                        dv.tensor_tensor(out=t1[:], in0=pg, in1=sgb[:], op=ALU.mult)
                        dv.tensor_scalar_mul(ub[:], pu, us_col)
                        dv.scalar_tensor_tensor(out=hhf[:], in0=t1[:], scalar=gs_col,
                                                in1=ub[:], op0=ALU.mult, op1=ALU.mult)
                        dv.tensor_copy(out=hhi[:, 512 * ic:512 * (ic + 1)], in_=hhf[:])
                        dv.tensor_scalar_mul(ub[:], ub[:], 1.0)
                        return dv.tensor_tensor(
                            out=hlo[:, 512 * ic:512 * (ic + 1)], in0=hhf[:],
                            in1=hhi[:, 512 * ic:512 * (ic + 1)], op=ALU.subtract)
                    ev_lo2 = kb.on("vector", e_fuse,
                                   wait=[ev_sgb, t1_free, ub_free, hhf_free, hh_free])
                    pAB_free[2 * (ic % 2)] = ev_lo2
                    pAB_free[2 * (ic % 2) + 1] = ev_lo2
                    sgb_free = ev_lo2
                    t1_free = ev_lo2
                    ub_free = ev_lo2
                    hhf_free = ev_lo2
                hh_last = [ev_lo2]

            with ExitStack() as stB:
                def sbtB(n, sh, dt=F32):
                    return stB.enter_context(nc.sbuf_tensor(f"{n}_m{m}", sh, dt))
                ostg = [sbtB(f"ostg4_{i}", [128, 512]) for i in range(2)]
                ffn_tm = sbtB("ffn4", [128, NSUB * D])
                ybuf = sbtB("yb4", [128, D])
                x3buf = sbtB("x3b4", [128, D])

                HH2 = HCH // 2
                wh_free = [[prev_pe] + hh_last, [prev_pe] + hh_last]
                pd_free = [entry, entry]
                ostg_free = [entry, entry]
                tpB_free = {0: entry, 1: entry}
                tp_done = []
                for dc in range(DCH):
                    ev_wh = []
                    for h in range(2):
                        ev = kb.dma(
                            "sync", wdbuf0[:, 128 * HH2 * h:128 * HH2 * (h + 1)],
                            kb.dwb_s[dc][:, HH2 * h:HH2 * (h + 1), :]
                            .rearrange("p j c -> p (j c)"),
                            wait=[wh_free[h]])
                        ev_wh.append(ev)

                    pd = (kb.PB[0], kb.PB[1])[dc % 2]

                    def e_dmmA(pe, pd=pd):
                        for j in range(HH2):
                            pe.matmul(pd, wdbuf0[:, 128 * j:128 * (j + 1)],
                                      hhi[:, 512 * j:512 * (j + 1)],
                                      start=(j == 0), stop=False)
                            ins = pe.matmul(pd, wdbuf0[:, 128 * j:128 * (j + 1)],
                                            hlo[:, 512 * j:512 * (j + 1)],
                                            start=False, stop=False)
                            kb.ldw_patch.append(ins.ins)
                        return ins
                    ev_dA = kb.on("tensor", e_dmmA,
                                  wait=[ev_wh[0], pd_free[dc % 2]] + hh_last)
                    wh_free[0] = ev_dA

                    def e_dmmB(pe, pd=pd):
                        for j in range(HH2, HCH):
                            pe.matmul(pd, wdbuf0[:, 128 * j:128 * (j + 1)],
                                      hhi[:, 512 * j:512 * (j + 1)],
                                      start=False, stop=False)
                            ins = pe.matmul(pd, wdbuf0[:, 128 * j:128 * (j + 1)],
                                            hlo[:, 512 * j:512 * (j + 1)],
                                            start=False, stop=(j == HCH - 1))
                            kb.ldw_patch.append(ins.ins)
                        return ins
                    ev_dmm = kb.on("tensor", e_dmmB, wait=[ev_wh[1]])
                    wh_free[1] = ev_dmm
                    prev_pe = ev_dmm

                    osl = dc % 2
                    ev_oev = kb.on("scalar", lambda ac, pd=pd, osl=osl, dc=dc: ac.activation(
                        out=ostg[osl][:], in_=pd, func=AF.Identity,
                        scale=kb.c_ds[:, dc:dc + 1]),
                        wait=[ev_dmm, ostg_free[osl]])
                    pd_free[dc % 2] = ev_oev

                    pv = (kb.p4, kb.p5)[osl]

                    def e_tp(pe, pv=pv, osl=osl):
                        for s in range(NSUB):
                            ins = pe.transpose(pv[:, 128 * s:128 * (s + 1)],
                                               ostg[osl][:, 128 * s:128 * (s + 1)],
                                               kb.c_ident[:])
                        return ins
                    ev_tp = kb.on("tensor", e_tp, wait=[ev_oev, tpB_free[osl], ffn_free])
                    ostg_free[osl] = ev_tp
                    prev_pe = ev_tp

                    def e_cp(dv, pv=pv, dc=dc):
                        for s in range(NSUB):
                            ins = dv.tensor_copy(
                                out=ffn_tm[:, D * s + 128 * dc:D * s + 128 * (dc + 1)],
                                in_=pv[:, 128 * s:128 * (s + 1)])
                        return ins
                    ev_cp = kb.on("vector", e_cp, wait=[ev_tp])
                    tpB_free[osl] = ev_cp
                    tp_done.append(ev_cp)
                hh_free = prev_pe

                for s in range(NSUB):
                    ev_x3 = kb.dma("scalar", x3buf[:],
                                   x_src[r0 + 128 * s:r0 + 128 * (s + 1), :],
                                   wait=[epi_prev])
                    fv = ffn_tm[:, D * s:D * (s + 1)]
                    ev_ln = kb.ln(fv, ybuf[:], [ev_x3] + tp_done)
                    ev_x4 = kb.on("vector", lambda dv: dv.tensor_tensor(
                        out=x3buf[:], in0=x3buf[:], in1=ybuf[:], op=ALU.add), wait=[ev_ln])
                    ev_st = kb.dma("scalar", kb.out[r0 + 128 * s:r0 + 128 * (s + 1), :],
                                   x3buf[:], wait=[ev_x4])
                    epi_prev = ev_st
                    out_evs.append(ev_st)
                ffn_free = out_evs[-1]

    return kb.out, [out_evs[-1]]




def emit_finale(kb, src, evs, upto):
    if upto < 4:
        ev = kb.dma("sync", kb.out[:], src[:], wait=kb.barrier_ev() + list(evs))
    else:
        ev = None
    for name in DEBUG_DUMPS:
        scr_ap = getattr(kb, name)
        dbg = kb.nc.dram_tensor("dbg_" + name, list(scr_ap.shape), scr_ap.dtype,
                                kind="ExternalOutput").ap()
        ev = kb.dma("sync", dbg.flatten_outer_dims(), scr_ap.flatten_outer_dims(),
                    wait=[ev] + kb.barrier_ev() + list(evs))

    def f(sy):
        _wait_all(sy, [ev] + list(evs) + kb.barrier_ev())
    kb.block.sync(f)


def _patch_ldweights(kb):
    import copy as _copy
    ids = {id(x) for x in kb.ldw_patch}
    if not ids:
        return
    n = 0
    for blk in kb.nc.main_func.blocks:
        il = blk.instructions
        for idx, ins in enumerate(il):
            if id(ins) in ids:
                il[idx] = _copy.replace(ins, ldweights=False)
                n += 1
    assert n == len(ids), (n, len(ids))


def _build(upto=None):
    if upto is None:
        upto = UPTO
    kb = KB(upto)
    with kb.es:
        emit_prologue(kb)
        with ExitStack() as qscope:
            qs = QuantStream(kb, qscope) if upto >= 2 else None
            src, evs = emit_sb1(kb, qs)
            if qs is not None:
                qs.finish()
        if upto >= 2:
            src, evs = emit_sb2(kb, src, evs)
        if upto >= 3:
            src, evs = emit_sb3(kb, src, evs)
        if upto >= 4:
            src, evs = emit_sb4(kb, src, evs)
        emit_finale(kb, src, evs, upto)
        _patch_ldweights(kb)
    return kb.nc


def kernel(**inputs):
    shared, shards = _prep_inputs(inputs)
    nc = _build()
    in_maps = [dict(shared, x_in=shards[c]) for c in range(NCORES)]
    res = run_bass_kernel_spmd(nc, in_maps, core_ids=list(range(NCORES)))
    out = np.empty((B, S, D), np.float32)
    for c, r in enumerate(res.results):
        out[BPC * c:BPC * (c + 1)] = r["out"].reshape(BPC, S, D)
    return out


# version 17
# speedup vs baseline: 1.1091x; 1.1091x over previous
import numpy as np
import ml_dtypes
from contextlib import ExitStack

import concourse.bass as bass
import concourse.mybir as mybir
from concourse.bass_utils import run_bass_kernel_spmd

F32 = mybir.dt.float32
BF16 = mybir.dt.bfloat16
I32 = mybir.dt.int32
AF = mybir.ActivationFunctionType
ALU = mybir.AluOpType
AX = mybir.AxisListType

B, S, D = 16, 2048, 2048
NCORES = 8
BPC = B // NCORES
T = BPC * S
NH, DK = 16, 128
H_FFN = 8192
MAC = 512
NMAC = T // MAC
MPB = NMAC // BPC
NSUB = MAC // 128
DCH = D // 128
HCH = H_FFN // 128
EPS_LN = 1e-5
EPS_RMS = float(np.finfo(np.float32).eps)
SM_SCALE = float(1.0 / np.sqrt(DK))
NEG_BIG = -1e30
BF16_CLIP_LO = float(np.float32(ml_dtypes.bfloat16(1e-8)))
RNE_C = 12582912.0
LN_CLIP_LO = float(np.log(np.float64(1e-10)))
BF16_HALF_UP = 0.5009765625

UPTO = 4
DEBUG_DUMPS = []



def _rope_tables():
    theta = (1.0 / 10000.0 ** (np.arange(0, DK, 2, dtype=np.float64) / DK))
    pos = np.arange(S, dtype=np.float64)
    fr = pos[:, None] * theta[None, :]
    cos = np.cos(fr)
    sin = np.sin(fr)
    cos_fm = np.empty((DK, S), np.float32)
    sin_fm = np.empty((DK, S), np.float32)
    cos_fm[:64] = cos.T
    cos_fm[64:] = cos.T
    sin_fm[:64] = sin.T
    sin_fm[64:] = -sin.T
    return np.ascontiguousarray(cos_fm), np.ascontiguousarray(sin_fm)


def _tile_lhsT(wT, kdim, fdim):
    kc, fc = kdim // 128, fdim // 128
    t = wT.reshape(kc, 128, fc, 128)
    return np.ascontiguousarray(t.transpose(2, 1, 0, 3))


def _cols(v):
    n = v.shape[0] // 128
    return np.ascontiguousarray(v.reshape(n, 128).T.astype(np.float32))


def _prep_inputs(inp):
    f = np.float32
    d = {}
    d["wq_t"] = _tile_lhsT(np.ascontiguousarray(inp["L0_attn_wq"].T), D, D)
    d["wk_t"] = _tile_lhsT(np.ascontiguousarray(inp["L0_attn_wk"].T), D, D)
    d["wv_t"] = _tile_lhsT(np.ascontiguousarray(inp["L0_attn_wv"].T), D, D)
    d["wo_t"] = _tile_lhsT(np.ascontiguousarray(inp["L0_attn_wo"].T), D, D)
    d["bq_c"] = _cols(inp["L0_attn_bq"])
    d["bk_c"] = _cols(inp["L0_attn_bk"])
    d["bv_c"] = _cols(inp["L0_attn_bv"])
    d["bo_c"] = _cols(inp["L0_attn_bo"])
    d["lng_b"] = np.ascontiguousarray(np.tile(inp["L0_attn_ln_g"].astype(f), (128, 1)))
    d["lnb_b"] = np.ascontiguousarray(np.tile(inp["L0_attn_ln_b"].astype(f), (128, 1)))
    d["wgu"] = np.ascontiguousarray(inp["L0_ffn_w_gate_up"].astype(f))
    d["wp"] = np.ascontiguousarray(inp["L0_ffn_w_proj"].astype(f))
    d["wr1_t"] = _tile_lhsT(np.ascontiguousarray(inp["L1_attn_Wr"].T), D, D)
    d["wk1_t"] = _tile_lhsT(np.ascontiguousarray(inp["L1_attn_Wk"].T), D, D)
    d["wv1_t"] = _tile_lhsT(np.ascontiguousarray(inp["L1_attn_Wv"].T), D, D)
    d["wo1_t"] = _tile_lhsT(np.ascontiguousarray(inp["L1_attn_Wo"].T), D, D)
    d["decay_c"] = _cols(inp["L1_attn_decay"])
    d["gnw_c"] = _cols(inp["L1_attn_gn_weight"])
    d["gnb_c"] = _cols(inp["L1_attn_gn_bias"])
    d["gw_t"] = _tile_lhsT(np.ascontiguousarray(inp["L1_ffn_gate_w"].T), D, H_FFN)
    d["uw_t"] = _tile_lhsT(np.ascontiguousarray(inp["L1_ffn_up_w"].T), D, H_FFN)
    d["dw_t"] = _tile_lhsT(np.ascontiguousarray(inp["L1_ffn_down_w"].T), H_FFN, D)
    d["gs_c"] = _cols(inp["L1_ffn_gate_scale"])
    d["us_c"] = _cols(inp["L1_ffn_up_scale"])
    d["ds_c"] = _cols(inp["L1_ffn_down_scale"])
    cos_fm, sin_fm = _rope_tables()
    d["cosf"] = cos_fm
    d["sinf"] = sin_fm
    d["ident"] = np.eye(128, dtype=f)
    m = np.zeros((128, 128), f)
    m[np.triu_indices(128, k=1)] = NEG_BIG
    d["mask"] = m
    d["iota"] = np.ascontiguousarray(np.tile(np.arange(S, dtype=f), (128, 1)))
    x = np.asarray(inp["L0_attn_h"], dtype=f).reshape(B, S, D)
    shards = [np.ascontiguousarray(x[BPC * c:BPC * (c + 1)].reshape(T, D)) for c in range(NCORES)]
    return d, shards



class Sem:
    def __init__(self, nc, es, name):
        self.h = es.enter_context(nc.semaphore(name))
        self.v = 0

    def inc(self, instr, k=1):
        instr.then_inc(self.h, k)
        self.v += k
        return (self, self.v)


def _flat(evs, out):
    for ev in evs:
        if ev is None:
            continue
        if isinstance(ev, list):
            _flat(ev, out)
        else:
            out.append(ev)


def _wait_all(eng, evs):
    flat = []
    _flat(list(evs), flat)
    best = {}
    for s, v in flat:
        cur = best.get(id(s))
        if cur is None or cur[1] < v:
            best[id(s)] = (s, v)
    for s, v in best.values():
        eng.wait_ge(s.h, v)


class KB:
    def __init__(self, upto=4):
        self.upto = upto
        nc = bass.Bass("TRN2", target_bir_lowering=False)
        self.nc = nc
        self.es = ExitStack()
        es = self.es

        def din(name, shape, dt=F32):
            return nc.dram_tensor(name, shape, dt, kind="ExternalInput").ap()

        self.x_in = din("x_in", [T, D])
        self.wq_t = din("wq_t", [DCH, 128, DCH, 128])
        self.wk_t = din("wk_t", [DCH, 128, DCH, 128])
        self.wv_t = din("wv_t", [DCH, 128, DCH, 128])
        self.wo_t = din("wo_t", [DCH, 128, DCH, 128])
        self.bq_c = din("bq_c", [128, DCH])
        self.bk_c = din("bk_c", [128, DCH])
        self.bv_c = din("bv_c", [128, DCH])
        self.bo_c = din("bo_c", [128, DCH])
        self.lng_b = din("lng_b", [128, D])
        self.lnb_b = din("lnb_b", [128, D])
        self.wgu = din("wgu", [2 * H_FFN, D])
        self.wp = din("wp", [D, H_FFN])
        self.wr1_t = din("wr1_t", [DCH, 128, DCH, 128])
        self.wk1_t = din("wk1_t", [DCH, 128, DCH, 128])
        self.wv1_t = din("wv1_t", [DCH, 128, DCH, 128])
        self.wo1_t = din("wo1_t", [DCH, 128, DCH, 128])
        self.decay_c = din("decay_c", [128, DCH])
        self.gnw_c = din("gnw_c", [128, DCH])
        self.gnb_c = din("gnb_c", [128, DCH])
        self.gw_t = din("gw_t", [HCH, 128, DCH, 128], I32)
        self.uw_t = din("uw_t", [HCH, 128, DCH, 128], I32)
        self.dw_t = din("dw_t", [DCH, 128, HCH, 128], I32)
        self.gs_c = din("gs_c", [128, HCH])
        self.us_c = din("us_c", [128, HCH])
        self.ds_c = din("ds_c", [128, DCH])
        self.cosf = din("cosf", [128, S])
        self.sinf = din("sinf", [128, S])
        self.ident = din("ident", [128, 128])
        self.mask = din("mask", [128, 128])
        self.iota = din("iota", [128, S])

        self.out = nc.dram_tensor("out", [T, D], F32, kind="ExternalOutput").ap()

        def scr(name, shape, dt=F32):
            return nc.dram_tensor(name, shape, dt, kind="Internal").ap()

        self.x1_s = scr("x1_s", [T, D])
        self.x2_s = scr("x2_s", [T, D])
        self.x3_s = scr("x3_s", [T, D])
        self.y_s = scr("y_s", [T, D])
        self.q_s = scr("q_s", [BPC, NH, 128, S])
        self.k_s = scr("k_s", [BPC, NH, 128, S])
        self.v_s = scr("v_s", [BPC, NH, 128, S // 128, 128])
        self.o_s = scr("o_s", [BPC, NH, 128, S])
        self.att_s = scr("att_s", [BPC, DCH, 128, S])
        self.wgub_s = scr("wgub_s", [2 * H_FFN, D], BF16)
        self.wpb_s = scr("wpb_s", [D, H_FFN], BF16)
        self.wgut_s = scr("wgut_s", [2 * H_FFN // 512, 128, DCH, 512], BF16)
        self.wpt_s = scr("wpt_s", [DCH, 128, HCH, 128], BF16)
        self.gwb_s = scr("gwb_s", [HCH, 128, DCH, 128], BF16)
        self.uwb_s = scr("uwb_s", [HCH, 128, DCH, 128], BF16)
        self.dwb_s = scr("dwb_s", [DCH, 128, HCH, 128], BF16)

        self.sW = Sem(nc, es, "sW")
        self.sA = Sem(nc, es, "sA")
        self.sG = Sem(nc, es, "sG")
        self.sPE = Sem(nc, es, "sPE")
        self.sDV = Sem(nc, es, "sDV")
        self.sAC = Sem(nc, es, "sAC")
        self.sPO = Sem(nc, es, "sPO")
        self.dsem = {"sync": self.sW, "scalar": self.sA, "gpsimd": self.sG}
        self.esem = {"tensor": self.sPE, "vector": self.sDV, "scalar": self.sAC,
                     "gpsimd": self.sPO, "sync": self.sW}

        self.ps_big = es.enter_context(nc.psum_tensor("ps_big", [128, 2048], F32))
        self.PB = [self.ps_big[:, 512 * i:512 * (i + 1)] for i in range(4)]
        self.p4 = es.enter_context(nc.psum_tensor("p4", [128, 512], F32))
        self.p5 = es.enter_context(nc.psum_tensor("p5", [128, 512], F32))
        self.p6 = es.enter_context(nc.psum_tensor("p6", [128, 512], F32))
        self.p7 = es.enter_context(nc.psum_tensor("p7", [128, 512], F32))

        def sbt(name, shape, dt=F32):
            return es.enter_context(nc.sbuf_tensor(name, shape, dt))

        self.sbt = sbt
        self.c_ident = sbt("c_ident", [128, 128])
        self.c_mask = sbt("c_mask", [128, 128])
        self.c_bq = sbt("c_bq", [128, DCH])
        self.c_bk = sbt("c_bk", [128, DCH])
        self.c_bv = sbt("c_bv", [128, DCH])
        self.c_bo = sbt("c_bo", [128, DCH])
        self.c_gnw = sbt("c_gnw", [128, DCH])
        self.c_gnb = sbt("c_gnb", [128, DCH])
        self.c_gs = sbt("c_gs", [128, HCH])
        self.c_us = sbt("c_us", [128, HCH])
        self.c_ds = sbt("c_ds", [128, DCH])
        self.c_logd = sbt("c_logd", [128, DCH])
        self.c_ones1 = sbt("c_ones1", [128, 1])
        self.c_ones = sbt("c_ones", [128, 512])
        self.c_eps = sbt("c_eps", [128, 2])
        self.c_dummy = sbt("c_dummy", [1, 1])
        self.ln_cols = sbt("ln_cols", [128, 16])
        self.ln_slot = 0
        self.ln_prev = [None, None]

        self.ldw_patch = []
        self.block = es.enter_context(nc.Block())

    def on(self, engname, emit, wait=(), inc=1):
        out = {}
        sem = self.esem[engname]

        def f(eng):
            _wait_all(eng, wait)
            ins = emit(eng)
            out["ev"] = sem.inc(ins, inc)

        getattr(self.block, engname)(f)
        return out["ev"]

    def dma(self, q, out_ap, in_ap, wait=(), **kw):
        sem = self.dsem[q]
        res = {}

        def f(eng):
            _wait_all(eng, wait)
            res["ev"] = sem.inc(eng.dma_start(out=out_ap, in_=in_ap, **kw), 16)

        getattr(self.block, q)(f)
        return res["ev"]

    def dma_multi(self, q, pairs, wait=(), transpose=False):
        sem = self.dsem[q]
        res = {}

        def f(eng):
            _wait_all(eng, wait)
            for out_ap, in_ap in pairs:
                if transpose:
                    ins = eng.dma_start_transpose(out=out_ap, in_=in_ap)
                else:
                    ins = eng.dma_start(out=out_ap, in_=in_ap)
                res["ev"] = sem.inc(ins, 16)

        getattr(self.block, q)(f)
        return res["ev"]

    def barrier_ev(self):
        return [(s, s.v) for s in (self.sW, self.sA, self.sG, self.sPE,
                                   self.sDV, self.sAC, self.sPO) if s.v > 0]

    def aid(self, ac, out, in_):
        ac.activation(out=self.c_dummy[:], in_=self.c_eps[0:1, 0:1], func=AF.Exp)
        return ac.activation(out=out, in_=in_, func=AF.Identity, scale=1.0)

    def pin(self, ac, func=None):
        ac.activation(out=self.c_dummy[:], in_=self.c_eps[0:1, 0:1],
                      func=AF.Exp if func is None else func)

    def ln(self, x_ap, y_ap, wait, n=D, eps=EPS_LN):
        slot = self.ln_slot
        self.ln_slot ^= 1
        c = self.ln_cols[:, 8 * slot:8 * (slot + 1)]
        su, negmu, ssq, var0, mu2, rstd, nmr, lnv = (c[:, i:i + 1] for i in range(8))
        dump = y_ap
        inv_n = 1.0 / n
        w0 = list(wait) + ([self.ln_prev[slot]] if self.ln_prev[slot] else [])

        ev_su = self.on("vector", lambda dv: dv.tensor_reduce(
            out=su, in_=x_ap, axis=AX.X, op=ALU.add), wait=w0)

        def e_negmu(ac):
            self.pin(ac)
            ac.activation(out=negmu, in_=su, func=AF.Identity, scale=-inv_n)
            return ac.activation(out=dump, in_=x_ap, func=AF.Square, accum_out=ssq)
        ev_negmu = self.on("scalar", e_negmu, wait=w0 + [ev_su])

        ev_nv = self.on("vector", lambda dv: dv.scalar_tensor_tensor(
            out=var0, in0=negmu, scalar=-1.0, in1=negmu,
            op0=ALU.mult, op1=ALU.mult), wait=[ev_negmu])

        def e3(ac):
            ac.activation(out=mu2, in_=ssq, func=AF.Identity, scale=inv_n, bias=var0)
            ac.activation(out=lnv, in_=mu2, func=AF.Ln, bias=self.c_eps[:, 0:1] if eps == EPS_LN else self.c_eps[:, 1:2])
            return ac.activation(out=rstd, in_=lnv, func=AF.Exp, scale=-0.5)
        ev_rstd = self.on("scalar", e3, wait=[ev_nv])

        ev_nmr = self.on("vector", lambda dv: dv.tensor_tensor(
            out=nmr, in0=negmu, in1=rstd, op=ALU.mult), wait=[ev_rstd])

        def e_y(ac):
            self.pin(ac)
            return ac.activation(out=y_ap, in_=x_ap, func=AF.Identity,
                                 scale=rstd, bias=nmr)
        ev_done = self.on("scalar", e_y, wait=[ev_nmr])
        self.ln_prev[slot] = ev_done
        return ev_done

    def transpose_mac(self, src_subs, src_evs, dst, dst_cols, pp, pp_free):
        evs = []
        nch = len(dst_cols)
        for j in range(nch):
            slot = j % 2
            pb = pp[slot]

            def em(pe, j=j, pb=pb):
                for s in range(NSUB):
                    ins = pe.transpose(pb[:, 128 * s:128 * (s + 1)],
                                       src_subs[s][:, 128 * j:128 * (j + 1)], self.c_ident[:])
                return ins
            ev_tp = self.on("tensor", em, wait=list(src_evs) + [pp_free.get(slot)])

            eng = "vector" if j % 2 == 0 else "scalar"
            if eng == "vector":
                ev_ev = self.on(eng, lambda dv, j=j, pb=pb: dv.tensor_copy(
                    out=dst[:, dst_cols[j]:dst_cols[j] + 512], in_=pb[:]), wait=[ev_tp])
            else:
                ev_ev = self.on(eng, lambda ac, j=j, pb=pb: self.aid(
                    ac, dst[:, dst_cols[j]:dst_cols[j] + 512], pb[:]), wait=[ev_tp])
            pp_free[slot] = ev_ev
            evs.append(ev_ev)
        return evs


def emit_prologue(kb: KB):
    loads = [
        (kb.c_ident, kb.ident), (kb.c_mask, kb.mask),
        (kb.c_bq, kb.bq_c), (kb.c_bk, kb.bk_c), (kb.c_bv, kb.bv_c),
        (kb.c_bo, kb.bo_c), (kb.c_gnw, kb.gnw_c), (kb.c_gnb, kb.gnb_c),
        (kb.c_gs, kb.gs_c), (kb.c_us, kb.us_c), (kb.c_ds, kb.ds_c),
    ]
    kb.c_dect = kb.sbt("c_dect", [128, DCH])
    ev_ld = kb.dma_multi("sync", [(d[:], s[:]) for d, s in loads])
    ev_dec = kb.dma("sync", kb.c_dect[:], kb.decay_c[:])

    def e_ones(dv):
        dv.memset(kb.c_ones1[:], 1.0)
        dv.memset(kb.c_eps[:, 0:1], EPS_LN)
        dv.memset(kb.c_eps[:, 1:2], EPS_RMS)
        return dv.memset(kb.c_ones[:], 1.0)
    ev_ones = kb.on("vector", e_ones)

    ev_sig = kb.on("scalar", lambda ac: ac.activation(
        out=kb.c_dect[:], in_=kb.c_dect[:], func=AF.Sigmoid), wait=[ev_dec])
    ev_clip = kb.on("vector", lambda dv: dv.tensor_scalar_max(
        kb.c_dect[:], kb.c_dect[:], 1e-7), wait=[ev_sig])
    ev_logd = kb.on("scalar", lambda ac: ac.activation(
        out=kb.c_logd[:], in_=kb.c_dect[:], func=AF.Ln), wait=[ev_clip])

    kb.ev_consts = ev_ld
    kb.ev_logd = ev_logd
    kb.ev_ones = ev_ones


def emit_sb1(kb: KB, qs=None):
    nc = kb.nc
    entry = [kb.ev_consts, kb.ev_ones]

    with ExitStack() as st:
        def sbt(n, sh, dt=F32):
            return st.enter_context(nc.sbuf_tensor(n, sh, dt))
        kb.c_cos = sbt("c_cos", [128, S])
        kb.c_sin = sbt("c_sin", [128, S])
        kb.c_lng = sbt("c_lng", [128, D])
        kb.c_lnb = sbt("c_lnb", [128, D])
        ev_c1 = kb.dma_multi("sync", [(kb.c_cos[:], kb.cosf[:]), (kb.c_sin[:], kb.sinf[:]),
                                      (kb.c_lng[:], kb.lng_b[:]), (kb.c_lnb[:], kb.lnb_b[:])])
        entry = entry + [ev_c1]
        z_mac = sbt("z1_mac", [128, NSUB * D])
        zT = sbt("z1T", [128, DCH * 512])
        xbuf = [sbt(f"x1ld{i}", [128, D]) for i in range(2)]
        wbuf = [sbt(f"w1buf{i}", [128, DCH * 128]) for i in range(3)]
        stg = [sbt(f"stg1_{i}", [128, 512]) for i in range(2)]
        rop = [sbt(f"rop1_{i}", [128, 512]) for i in range(2)]
        rtmp = sbt("rtmp1", [128, 512])
        ru = sbt("ru1", [128, 512])
        vblk = [sbt(f"vblk1_{i}", [128, 512]) for i in range(2)]

        xbuf_free = [entry, entry]
        wbuf_free = [entry, entry, entry]
        pbank_free = [None, None]
        stg_free = [entry, entry]
        rop_free = [entry, entry]
        vblk_free = [entry, entry]
        vtp_free = [None, None]
        tpp_free = {}
        prev_tp_last = None

        for m in range(NMAC):
            b, tw = divmod(m, MPB)
            t0 = tw * MAC
            r0 = b * S + t0
            aff_evs = []
            for s in range(NSUB):
                buf = xbuf[s % 2]
                ev_x = kb.dma("scalar", buf[:], kb.x_in[r0 + 128 * s:r0 + 128 * (s + 1), :],
                              wait=[xbuf_free[s % 2]])
                yv = z_mac[:, D * s:D * (s + 1)]
                ev_ln = kb.ln(buf[:], yv, [ev_x, prev_tp_last])
                ev_ys = kb.dma("gpsimd", kb.y_s[r0 + 128 * s:r0 + 128 * (s + 1), :], yv,
                               wait=[ev_ln])
                xbuf_free[s % 2] = ev_ln

                def e_aff(dv, yv=yv):
                    dv.tensor_tensor(out=yv, in0=yv, in1=kb.c_lng[:], op=ALU.mult)
                    return dv.tensor_tensor(out=yv, in0=yv, in1=kb.c_lnb[:], op=ALU.add)
                aff_evs.append(kb.on("vector", e_aff, wait=[ev_ys]))

            src_subs = [z_mac[:, D * s:D * (s + 1)] for s in range(NSUB)]
            tp_evs = kb.transpose_mac(src_subs, aff_evs, zT,
                                      [512 * j for j in range(DCH)],
                                      (kb.PB[2], kb.PB[3]), tpp_free)
            zT_ready = tp_evs

            projs = [("q", kb.wq_t, kb.c_bq, kb.q_s), ("k", kb.wk_t, kb.c_bk, kb.k_s),
                     ("v", kb.wv_t, kb.c_bv, None)]
            for pi, (pname, wt, bias_c, dst_s) in enumerate(projs):
                for i in range(DCH):
                    g = pi * DCH + i
                    ws = g % 3
                    pb = kb.PB[g % 2]
                    ev_w = kb.dma("sync", wbuf[ws][:], wt[i].flatten_outer_dims(),
                                  wait=[wbuf_free[ws]])

                    def e_mm(pe, ws=ws, pb=pb):
                        for j in range(DCH):
                            ins = pe.matmul(pb, wbuf[ws][:, 128 * j:128 * (j + 1)],
                                            zT[:, 512 * j:512 * (j + 1)],
                                            start=(j == 0), stop=(j == DCH - 1))
                        return ins
                    ev_mm = kb.on("tensor", e_mm,
                                  wait=[ev_w, pbank_free[g % 2]] + list(zT_ready))
                    wbuf_free[ws] = ev_mm
                    prev_tp_last = ev_mm

                    sl = g % 2
                    if pname in ("q", "k"):
                        def e_ev(ac, pb=pb, sl=sl, i=i, bias_c=bias_c):
                            kb.pin(ac)
                            return ac.activation(out=stg[sl][:], in_=pb, func=AF.Identity,
                                                 bias=bias_c[:, i:i + 1])
                        ev_evac = kb.on("scalar", e_ev, wait=[ev_mm, stg_free[sl]])
                        pbank_free[g % 2] = ev_evac

                        def e_rope(dv, sl=sl, t0=t0):
                            cs = kb.c_cos[:, t0:t0 + 512]
                            sn = kb.c_sin[:, t0:t0 + 512]
                            dv.tensor_tensor(out=ru[:], in0=stg[sl][:], in1=sn, op=ALU.mult)
                            dv.tensor_copy(out=rtmp[0:64, :], in_=ru[64:128, :])
                            dv.tensor_copy(out=rtmp[64:128, :], in_=ru[0:64, :])
                            dv.tensor_tensor(out=rop[sl][:], in0=stg[sl][:], in1=cs, op=ALU.mult)
                            return dv.tensor_tensor(out=rop[sl][:], in0=rop[sl][:], in1=rtmp[:],
                                                    op=ALU.add)
                        ev_rope = kb.on("vector", e_rope, wait=[ev_evac, rop_free[sl]])
                        stg_free[sl] = ev_rope
                        ev_dma = kb.dma("gpsimd", dst_s[b, i, :, t0:t0 + 512], rop[sl][:],
                                        wait=[ev_rope])
                        rop_free[sl] = ev_dma
                    else:
                        def e_ev(ac, pb=pb, sl=sl, i=i):
                            kb.pin(ac)
                            return ac.activation(out=stg[sl][:], in_=pb, func=AF.Identity,
                                                 bias=kb.c_bv[:, i:i + 1])
                        ev_evac = kb.on("scalar", e_ev, wait=[ev_mm, stg_free[sl]])
                        pbank_free[g % 2] = ev_evac

                        pv = (kb.p4, kb.p5)[sl]

                        def e_vtp(pe, sl=sl, pv=pv):
                            for s in range(NSUB):
                                ins = pe.transpose(pv[:, 128 * s:128 * (s + 1)],
                                                   stg[sl][:, 128 * s:128 * (s + 1)],
                                                   kb.c_ident[:])
                            return ins
                        ev_vtp = kb.on("tensor", e_vtp, wait=[ev_evac, vtp_free[sl]])
                        stg_free[sl] = ev_vtp
                        ev_vcp = kb.on("vector", lambda dv, sl=sl, pv=pv: dv.tensor_copy(
                            out=vblk[sl][:], in_=pv[:]), wait=[ev_vtp, vblk_free[sl]])
                        vtp_free[sl] = ev_vcp
                        ev_vdma = kb.dma_multi("gpsimd", [
                            (kb.v_s[b, i, :, NSUB * tw + s, :], vblk[sl][:, 128 * s:128 * (s + 1)])
                            for s in range(NSUB)], wait=[ev_vcp])
                        vblk_free[sl] = ev_vdma

    barrier1 = kb.barrier_ev()

    with ExitStack() as st:
        def sbt(n, sh, dt=F32):
            return st.enter_context(nc.sbuf_tensor(n, sh, dt))
        qq2 = [sbt(f"qq1_{i}", [128, S]) for i in range(2)]
        kk2 = [sbt(f"kk1_{i}", [128, S]) for i in range(2)]
        vv2 = [sbt(f"vv1_{i}", [128, S]) for i in range(2)]
        p_sb = sbt("p_sb1", [128, S])
        pt_sb = [sbt(f"pt1_{i}", [128, 512]) for i in range(2)]
        o_stage = [sbt(f"ostg1_{i}", [128, 128]) for i in range(2)]
        o_tq = sbt("otq1", [128, 128])
        acols = sbt("acols1", [128, 5])
        mxc, nmx, rsc = (acols[:, i:i + 1] for i in range(3))
        rrc2 = [acols[:, 3 + i:4 + i] for i in range(2)]

        ld_free = [barrier1, barrier1]
        otq_free = barrier1
        otp_hist = [barrier1, barrier1]
        oev_hist = [None, None]
        ostg_free = [barrier1, barrier1]
        for b in range(BPC):
            for h in range(NH):
                if qs is not None:
                    qs.emit_next((len(qs.jobs) + BPC * NH - 1) // (BPC * NH))
                hsl = (b * NH + h) % 2
                qq, kk, vv = qq2[hsl], kk2[hsl], vv2[hsl]
                ev_ld = kb.dma_multi("sync", [
                    (qq[:], kb.q_s[b, h].flatten_outer_dims()),
                    (kk[:], kb.k_s[b, h].flatten_outer_dims()),
                    (vv[:], kb.v_s[b, h].flatten_outer_dims()),
                ], wait=[ld_free[hsl]])

                ev_scores_free = None
                ev_psb_free = barrier1
                prev_oev = None
                ev_rr = None
                ptp_free = {0: barrier1, 1: barrier1}
                ptsb_free = {0: barrier1, 1: barrier1}
                for i in range(S // 128):
                    ncols = 128 * (i + 1)

                    def e_sc(pe, i=i, ncols=ncols):
                        c0 = 0
                        while c0 < ncols:
                            nn = min(512, ncols - c0)
                            ins = pe.matmul(kb.ps_big[:, c0:c0 + nn],
                                            qq[:, 128 * i:128 * (i + 1)],
                                            kk[:, c0:c0 + nn], start=True, stop=True)
                            c0 += nn
                        return ins
                    ev_sc = kb.on("tensor", e_sc, wait=[ev_ld, ev_scores_free])

                    def e_mx(dv, i=i, ncols=ncols):
                        dv.tensor_tensor(out=kb.ps_big[:, 128 * i:128 * (i + 1)],
                                         in0=kb.ps_big[:, 128 * i:128 * (i + 1)],
                                         in1=kb.c_mask[:], op=ALU.add)
                        return dv.tensor_reduce(out=mxc, in_=kb.ps_big[:, 0:ncols],
                                                axis=AX.X, op=ALU.max)
                    ev_mx = kb.on("vector", e_mx, wait=[ev_sc])

                    ev_nmx = kb.on("gpsimd", lambda po: po.tensor_scalar_mul(
                        nmx, mxc, -SM_SCALE), wait=[ev_mx])

                    ev_exp = kb.on("scalar", lambda ac, ncols=ncols: ac.activation(
                        out=p_sb[:, 0:ncols], in_=kb.ps_big[:, 0:ncols], func=AF.Exp,
                        scale=SM_SCALE, bias=nmx, accum_out=rsc),
                        wait=[ev_nmx, ev_psb_free, ev_rr])
                    ev_scores_free = ev_exp

                    rcol = rrc2[i % 2]
                    ev_rr = kb.on("vector", lambda dv, rcol=rcol: dv.reciprocal(
                        out=rcol, in_=rsc), wait=[ev_exp, oev_hist[i % 2]])

                    ngrp = (i + 4) // 4
                    last_tp = None
                    ev_av = None
                    for gidx in range(ngrp):
                        k0 = 4 * gidx
                        kn = min(4, i + 1 - k0)
                        sl = gidx % 2
                        pv = (kb.p4, kb.p5)[sl]

                        def e_tp(pe, k0=k0, kn=kn, pv=pv):
                            for t in range(kn):
                                ins = pe.transpose(pv[:, 128 * t:128 * (t + 1)],
                                                   p_sb[:, 128 * (k0 + t):128 * (k0 + t + 1)],
                                                   kb.c_ident[:])
                            return ins
                        ev_tp = kb.on("tensor", e_tp, wait=[ev_exp, ptp_free[sl]])
                        last_tp = ev_tp

                        ev_cp = kb.on("vector", lambda dv, sl=sl, pv=pv, kn=kn:
                                      dv.tensor_copy(out=pt_sb[sl][:, 0:128 * kn],
                                                     in_=pv[:, 0:128 * kn]),
                                      wait=[ev_tp, ptsb_free[sl]])
                        ptp_free[sl] = ev_cp

                        def e_av(pe, k0=k0, kn=kn, sl=sl, i=i):
                            for t in range(kn):
                                kc = k0 + t
                                ins = pe.matmul(kb.p6[:, 0:128],
                                                pt_sb[sl][:, 128 * t:128 * (t + 1)],
                                                vv[:, 128 * kc:128 * (kc + 1)],
                                                start=(kc == 0), stop=(kc == i))
                            return ins
                        ev_av = kb.on("tensor", e_av,
                                      wait=[ev_cp] + ([prev_oev] if gidx == 0 else []))
                        ptsb_free[sl] = ev_av

                    ev_psb_free = last_tp
                    osl = i % 2

                    def e_oev(ac, osl=osl, rcol=rcol):
                        kb.pin(ac)
                        return ac.activation(out=o_tq[:], in_=kb.p6[:, 0:128],
                                             func=AF.Identity, scale=rcol)
                    ev_oev = kb.on("scalar", e_oev,
                                   wait=[ev_av, otq_free, ev_rr])
                    prev_oev = ev_oev
                    oev_hist[i % 2] = ev_oev

                    ev_otp = kb.on("tensor", lambda pe, osl=osl: pe.transpose(
                        kb.p7[:, 128 * osl:128 * (osl + 1)], o_tq[:], kb.c_ident[:]),
                        wait=[ev_oev, otp_hist[osl]])
                    otq_free = ev_otp

                    ev_oev2 = kb.on("vector", lambda dv, osl=osl: dv.tensor_copy(
                        out=o_stage[osl][:], in_=kb.p7[:, 128 * osl:128 * (osl + 1)]),
                        wait=[ev_otp, ostg_free[osl]])
                    otp_hist[osl] = ev_oev2
                    ev_odma = kb.dma("gpsimd", kb.o_s[b, h, :, 128 * i:128 * (i + 1)],
                                     o_stage[osl][:], wait=[ev_oev2])
                    ostg_free[osl] = ev_odma
                ld_free[hsl] = ev_odma

    barrier2 = kb.barrier_ev()

    with ExitStack() as st:
        def sbt(n, sh, dt=F32):
            return st.enter_context(nc.sbuf_tensor(n, sh, dt))
        obuf = sbt("obuf1", [128, DCH * 512])
        att_tm = sbt("att1_tm", [128, NSUB * D])
        wbuf = [sbt(f"w1cbuf{i}", [128, DCH * 128]) for i in range(2)]
        wo_stage = [sbt(f"wostg{i}", [128, 512]) for i in range(2)]
        ybuf = sbt("ybuf1", [128, D])
        x0buf = sbt("x0buf1", [128, D])

        wbuf_free = [barrier2, barrier2]
        pbank_free = [barrier2, barrier2]
        wostg_free = [barrier2, barrier2]
        tpp5_free = {0: barrier2, 1: barrier2}
        obuf_free = barrier2
        att_free = barrier2
        epi_prev = barrier2
        x1_evs = []

        for m in range(NMAC):
            b, tw = divmod(m, MPB)
            t0 = tw * MAC
            r0 = b * S + t0
            ev_old = kb.dma_multi("sync", [
                (obuf[:, 512 * hc:512 * (hc + 1)], kb.o_s[b, hc, :, t0:t0 + 512])
                for hc in range(DCH)], wait=[obuf_free])

            att_done = []
            for i in range(DCH):
                ws = i % 2
                pb = kb.PB[i % 2]
                ev_w = kb.dma("sync", wbuf[ws][:], kb.wo_t[i].flatten_outer_dims(),
                              wait=[wbuf_free[ws]])

                def e_mm(pe, ws=ws, pb=pb):
                    for j in range(DCH):
                        ins = pe.matmul(pb, wbuf[ws][:, 128 * j:128 * (j + 1)],
                                        obuf[:, 512 * j:512 * (j + 1)],
                                        start=(j == 0), stop=(j == DCH - 1))
                    return ins
                ev_mm = kb.on("tensor", e_mm, wait=[ev_w, ev_old, pbank_free[i % 2]])
                wbuf_free[ws] = ev_mm

                sl = i % 2
                ev_evac = kb.on("scalar", lambda ac, pb=pb, i=i, sl=sl: ac.activation(
                    out=wo_stage[sl][:], in_=pb, func=AF.Identity, bias=kb.c_bo[:, i:i + 1]),
                    wait=[ev_mm, wostg_free[sl]])
                pbank_free[i % 2] = ev_evac

                pv = (kb.p4, kb.p5)[sl]

                def e_tp(pe, sl=sl, pv=pv):
                    for s in range(NSUB):
                        ins = pe.transpose(pv[:, 128 * s:128 * (s + 1)],
                                           wo_stage[sl][:, 128 * s:128 * (s + 1)],
                                           kb.c_ident[:])
                    return ins
                ev_tp = kb.on("tensor", e_tp, wait=[ev_evac, tpp5_free[sl], att_free])
                wostg_free[sl] = ev_tp

                def e_cp(dv, sl=sl, pv=pv, i=i):
                    for s in range(NSUB):
                        ins = dv.tensor_copy(
                            out=att_tm[:, D * s + 128 * i:D * s + 128 * (i + 1)],
                            in_=pv[:, 128 * s:128 * (s + 1)])
                    return ins
                ev_cp = kb.on("vector", e_cp, wait=[ev_tp])
                tpp5_free[sl] = ev_cp
                att_done.append(ev_cp)
            obuf_free = ev_mm

            for s in range(NSUB):
                ev_yld = kb.dma_multi("scalar", [
                    (ybuf[:], kb.y_s[r0 + 128 * s:r0 + 128 * (s + 1), :]),
                    (x0buf[:], kb.x_in[r0 + 128 * s:r0 + 128 * (s + 1), :]),
                ], wait=[epi_prev])
                av = att_tm[:, D * s:D * (s + 1)]
                ev_fout = kb.on("vector", lambda dv, av=av: dv.tensor_tensor(
                    out=av, in0=av, in1=ybuf[:], op=ALU.add), wait=[ev_yld] + att_done)
                ev_ln = kb.ln(av, ybuf[:], [ev_fout])
                ev_x1 = kb.on("vector", lambda dv: dv.tensor_tensor(
                    out=x0buf[:], in0=x0buf[:], in1=ybuf[:], op=ALU.add), wait=[ev_ln])
                ev_st = kb.dma("scalar", kb.x1_s[r0 + 128 * s:r0 + 128 * (s + 1), :],
                               x0buf[:], wait=[ev_x1])
                epi_prev = ev_st
                x1_evs.append(ev_st)
            att_free = x1_evs[-1]

    return kb.x1_s, [x1_evs[-1]]


class QuantStream:

    def __init__(self, kb, st):
        nc = kb.nc
        self.kb = kb

        def sbt(n, sh, dt=F32):
            return st.enter_context(nc.sbuf_tensor(n, sh, dt))

        self.wt_in = [sbt(f"qw_in{i}", [128, D]) for i in range(2)]
        self.wb = sbt("qw_wb", [128, D], BF16)
        self.quot = sbt("qw_quot", [128, D], BF16)
        self.rq = sbt("qw_rq", [128, D])
        self.what = sbt("qw_hat", [128, D], BF16)
        qc = sbt("qw_cols", [128, 160])
        self.asum = qc[:, 0:32]
        self.sbf = qc[:, 32:64].bitcast(BF16)[:, 0:32]
        self.smax = qc[:, 64:96]
        self.thr = qc[:, 96:128]
        self.nthr = qc[:, 128:160]
        self.cv = [sbt(f"cv_in{i}", [128, D], BF16) for i in range(2)]
        self.tb = [sbt(f"qt_tb{i}", [128, 512], BF16) for i in range(2)]

        qw_jobs, qp_jobs, t_jobs, t4_jobs, c_jobs = [], [], [], [], []
        for rt in range(2 * H_FFN // 128):
            qw_jobs.append(("q", kb.wgu[128 * rt:128 * (rt + 1), :],
                            kb.wgub_s[128 * rt:128 * (rt + 1), :]))
        for rt in range(DCH):
            for cs in range(4):
                qp_jobs.append(("q", kb.wp[128 * rt:128 * (rt + 1), 2048 * cs:2048 * (cs + 1)],
                                kb.wpb_s[128 * rt:128 * (rt + 1), 2048 * cs:2048 * (cs + 1)]))
        for g in range(2 * H_FFN // 512):
            for j in range(DCH):
                t_jobs.append(("t", kb.wgub_s[512 * g:512 * (g + 1), 128 * j:128 * (j + 1)],
                               kb.wgut_s[g, :, j, :]))
        for dg in range(4):
            for jw in range(HCH):
                t4_jobs.append(("t4", kb.wpb_s[512 * dg:512 * (dg + 1), 128 * jw:128 * (jw + 1)],
                                (jw, dg)))
        for ic in range(HCH):
            c_jobs.append(("c", kb.gw_t[ic].rearrange("p j c -> p (j c)"),
                           kb.gwb_s[ic].rearrange("p j c -> p (j c)")))
            c_jobs.append(("c", kb.uw_t[ic].rearrange("p j c -> p (j c)"),
                           kb.uwb_s[ic].rearrange("p j c -> p (j c)")))
        for dc in range(DCH):
            for hq in range(4):
                c_jobs.append(("c", kb.dw_t[dc].rearrange("p j c -> p (j c)")[:, 2048 * hq:2048 * (hq + 1)],
                               kb.dwb_s[dc].rearrange("p j c -> p (j c)")[:, 2048 * hq:2048 * (hq + 1)]))
        self.jobs = []
        nqw = nqp = pt = pt4 = pc = 0
        for k in range(32):
            for kind, qn in (("qw", 4), ("qp", 2), ("c", 6), ("t", 16), ("t4", 8)):
                for _ in range(qn):
                    if kind == "qw" and nqw < len(qw_jobs):
                        self.jobs.append(qw_jobs[nqw]); nqw += 1
                    elif kind == "qp" and nqp < len(qp_jobs):
                        self.jobs.append(qp_jobs[nqp]); nqp += 1
                    elif kind == "c" and pc < len(c_jobs):
                        self.jobs.append(c_jobs[pc]); pc += 1
                    elif kind == "t" and pt < len(t_jobs) and nqw >= 4 * (pt // 16) + 4:
                        self.jobs.append(t_jobs[pt]); pt += 1
                    elif kind == "t4" and pt4 < len(t4_jobs) and nqp >= 16 * (pt4 // 64) + 16:
                        self.jobs.append(t4_jobs[pt4]); pt4 += 1
        self.jobs += t_jobs[pt:] + t4_jobs[pt4:] + c_jobs[pc:]
        assert len(self.jobs) == (len(qw_jobs) + len(qp_jobs) + len(t_jobs)
                                  + len(t4_jobs) + len(c_jobs))
        self.idx = 0
        self.in_free = [None, None]
        self.cv_free = [None, None]
        self.pipe_free = None
        self.ev_done = None

    def emit_next(self, n):
        kb = self.kb
        end = min(self.idx + n, len(self.jobs))
        while self.idx < end:
            idx = self.idx
            kind, srcap, dstap = self.jobs[idx]
            self.idx += 1
            if kind == "q":
                sl = idx % 2
                wt_in, wb, quot, rq, what = self.wt_in, self.wb, self.quot, self.rq, self.what
                asum, sbf, smax = self.asum, self.sbf, self.smax
                thr, nthr = self.thr, self.nthr
                ev_in = kb.dma("scalar", wt_in[sl][:], srcap, wait=[self.in_free[sl]])

                def e_p1(dv, sl=sl):
                    dv.tensor_copy(out=wb[:], in_=wt_in[sl][:])
                    return dv.tensor_reduce(out=asum,
                                            in_=wb[:].rearrange("p (g k) -> p g k", k=64),
                                            axis=AX.X, op=ALU.add, apply_absolute_value=True)
                ev_p1 = kb.on("vector", e_p1, wait=[ev_in, self.pipe_free])
                self.in_free[sl] = ev_p1

                ev_s1 = kb.on("scalar", lambda ac: ac.activation(
                    out=sbf, in_=asum, func=AF.Identity, scale=1.0 / 64.0), wait=[ev_p1])
                ev_s2 = kb.on("vector", lambda dv: dv.tensor_scalar_max(
                    smax, sbf, BF16_CLIP_LO), wait=[ev_s1])

                def e_thr(ac):
                    ac.activation(out=thr, in_=smax, func=AF.Identity, scale=BF16_HALF_UP)
                    return ac.activation(out=nthr, in_=smax, func=AF.Identity,
                                         scale=-BF16_HALF_UP)
                ev_s3 = kb.on("scalar", e_thr, wait=[ev_s2])

                def e_cmp(dv):
                    dv.tensor_tensor(out=rq[:].rearrange("p (g k) -> p g k", k=64),
                                     in0=wb[:].rearrange("p (g k) -> p g k", k=64),
                                     in1=thr.unsqueeze(2).broadcast_to([128, 32, 64]),
                                     op=ALU.is_gt)
                    return dv.tensor_tensor(out=quot[:].rearrange("p (g k) -> p g k", k=64),
                                            in0=wb[:].rearrange("p (g k) -> p g k", k=64),
                                            in1=nthr.unsqueeze(2).broadcast_to([128, 32, 64]),
                                            op=ALU.is_lt)
                ev_cmp = kb.on("vector", e_cmp, wait=[ev_s3])

                def e_p2(dv):
                    dv.tensor_tensor(out=rq[:], in0=rq[:], in1=quot[:], op=ALU.subtract)
                    return dv.tensor_tensor(out=what[:].rearrange("p (g k) -> p g k", k=64),
                                            in0=rq[:].rearrange("p (g k) -> p g k", k=64),
                                            in1=smax.unsqueeze(2).broadcast_to([128, 32, 64]),
                                            op=ALU.mult)
                ev_p2 = kb.on("vector", e_p2, wait=[ev_cmp])
                self.ev_done = kb.dma("scalar", dstap, what[:], wait=[ev_p2])
                self.pipe_free = self.ev_done
            elif kind == "t":
                sl = idx % 2
                ev_in = kb.dma("scalar", self.tb[sl][:], srcap,
                               wait=[self.cv_free[sl], self.pipe_free], transpose=True)
                self.ev_done = kb.dma("scalar", dstap, self.tb[sl][:], wait=[ev_in])
                self.cv_free[sl] = self.ev_done
            elif kind == "t4":
                jw, dg = dstap
                sl = idx % 2
                ev_in = kb.dma("scalar", self.tb[sl][:], srcap,
                               wait=[self.cv_free[sl], self.pipe_free], transpose=True)
                ev_o = None
                for q in range(4):
                    ev_o = kb.dma("scalar", kb.wpt_s[4 * dg + q, :, jw, :],
                                  self.tb[sl][:, 128 * q:128 * (q + 1)],
                                  wait=[ev_in])
                self.ev_done = ev_o
                self.cv_free[sl] = ev_o
            else:
                sl = idx % 2
                cv = self.cv
                ev_in = kb.dma("gpsimd", cv[sl][:], srcap, wait=[self.cv_free[sl]])
                ev_sub = kb.on("vector", lambda dv, sl=sl: dv.tensor_scalar_add(
                    cv[sl][:], cv[sl][:], -128.0), wait=[ev_in])
                self.ev_done = kb.dma("gpsimd", dstap, cv[sl][:], wait=[ev_sub])
                self.cv_free[sl] = self.ev_done

    def finish(self):
        self.emit_next(len(self.jobs))
        return self.ev_done


def emit_sb2(kb: KB, x_src, in_evs):
    nc = kb.nc
    entry = kb.barrier_ev()

    with ExitStack() as stp:
        def sbtp(n, sh, dt=F32):
            return stp.enter_context(nc.sbuf_tensor(n, sh, dt))
        hhi = sbtp("hh2hi", [128, HCH * 512], BF16)
        hlo = sbtp("hh2lo", [128, HCH * 512], BF16)


        x2_evs = [entry]
        ffn_free = entry
        epi_prev = entry
        hh_free = entry
        p7_free = entry
        rb_free = entry
        prev_pe = entry

        for m in range(NMAC):
            b, tw = divmod(m, MPB)
            t0 = tw * MAC
            r0 = b * S + t0

            with ExitStack() as stA:
                def sbtA(n, sh, dt=F32):
                    return stA.enter_context(nc.sbuf_tensor(f"{n}_m{m}", sh, dt))
                yhi = sbtA("y2hi", [128, DCH * 512], BF16)
                ylo = sbtA("y2lo", [128, DCH * 512], BF16)
                stA1 = stA.enter_context(ExitStack())
                def sbtA1(n, sh, dt=F32):
                    return stA1.enter_context(nc.sbuf_tensor(f"{n}_m{m}", sh, dt))
                z_mac = sbtA1("z2", [128, 2 * D])
                xbuf = sbtA1("x2ld", [128, D])

                macro_gate = [epi_prev, prev_pe]
                tpp_free = {0: macro_gate, 1: macro_gate}
                xbuf_free = macro_gate
                last_lo = None
                z_free = [prev_pe, epi_prev]
                for sp in range(2):
                    ln_evs = []
                    for half in range(2):
                        s = 2 * sp + half
                        ev_x = kb.dma("scalar", xbuf[:],
                                      x_src[r0 + 128 * s:r0 + 128 * (s + 1), :],
                                      wait=[xbuf_free] + (in_evs if m == 0 and s == 0 else []))
                        ev_ln = kb.ln(xbuf[:], z_mac[:, D * half:D * (half + 1)],
                                      [ev_x] + z_free)
                        xbuf_free = ev_ln
                        ln_evs.append(ev_ln)
                    for j in range(DCH):
                        pv = (kb.PB[2], kb.PB[3])[j % 2]

                        def e_tp(pe, j=j, pv=pv):
                            for half in range(2):
                                ins = pe.transpose(pv[:, 128 * half:128 * (half + 1)],
                                                   z_mac[:, D * half + 128 * j:D * half + 128 * (j + 1)],
                                                   kb.c_ident[:])
                            return ins
                        ev_tp = kb.on("tensor", e_tp, wait=ln_evs + [tpp_free[j % 2]])

                        col = 512 * j + 256 * sp
                        ev_hi = kb.on("scalar", lambda ac, pv=pv, col=col: kb.aid(
                            ac, yhi[:, col:col + 256], pv[:, 0:256]), wait=[ev_tp])
                        ev_lo = kb.on("vector", lambda dv, pv=pv, col=col: dv.tensor_tensor(
                            out=ylo[:, col:col + 256], in0=pv[:, 0:256],
                            in1=yhi[:, col:col + 256], op=ALU.subtract), wait=[ev_hi])
                        tpp_free[j % 2] = ev_lo
                        last_lo = ev_lo
                        last_tp = ev_tp
                    z_free = [last_tp]
                y_ready = [last_lo]
                stA1.close()
                wgbuf = sbtA("wg2", [128, DCH * 512], BF16)
                wubuf = sbtA("wu2", [128, DCH * 512], BF16)
                sg = sbtA("sg2", [128, 512])
                hhf = sbtA("hhf2", [128, 512])
                hsq = sbtA("hsq2", [128, 512])

                wg_free = [last_tp] + macro_gate
                wu_free = [last_tp] + macro_gate
                sg_free = macro_gate
                hhf_free = macro_gate
                hsq_free = macro_gate
                pAB_free = {0: macro_gate, 1: macro_gate, 2: macro_gate, 3: macro_gate}
                ev_ssum = None
                for g in range(16):
                    ev_wg = kb.dma("sync", wgbuf[:], kb.wgut_s[g].rearrange("p j c -> p (j c)"),
                                   wait=[wg_free])
                    ev_wu = kb.dma("sync", wubuf[:], kb.wgut_s[16 + g].rearrange("p j c -> p (j c)"),
                                   wait=[wu_free])

                    for isub in range(4):
                        ic = 4 * g + isub
                        pg = kb.PB[2 * (ic % 2)]
                        pu = kb.PB[2 * (ic % 2) + 1]

                        def e_mm(pe, isub=isub, pg=pg, pu=pu):
                            for j in range(DCH):
                                wgt = wgbuf[:, 512 * j + 128 * isub:512 * j + 128 * (isub + 1)]
                                wut = wubuf[:, 512 * j + 128 * isub:512 * j + 128 * (isub + 1)]
                                pe.matmul(pg, wgt, yhi[:, 512 * j:512 * (j + 1)],
                                          start=(j == 0), stop=False)
                                kb.ldw_patch.append(pe.matmul(pg, wgt, ylo[:, 512 * j:512 * (j + 1)],
                                          start=False, stop=(j == DCH - 1)).ins)
                                pe.matmul(pu, wut, yhi[:, 512 * j:512 * (j + 1)],
                                          start=(j == 0), stop=False)
                                ins = pe.matmul(pu, wut, ylo[:, 512 * j:512 * (j + 1)],
                                                start=False, stop=(j == DCH - 1))
                                kb.ldw_patch.append(ins.ins)
                            return ins
                        ev_mm = kb.on("tensor", e_mm,
                                      wait=[ev_wg, ev_wu, pAB_free[2 * (ic % 2)],
                                            pAB_free[2 * (ic % 2) + 1]] + y_ready)
                        prev_pe = ev_mm
                        if isub == 3:
                            wg_free = ev_mm
                            wu_free = ev_mm

                        ev_sg = kb.on("scalar", lambda ac, pg=pg: ac.activation(
                            out=sg[:], in_=pg, func=AF.Silu), wait=[ev_mm, sg_free])
                        pAB_free[2 * (ic % 2)] = ev_sg

                        def e_fuse(dv, pu=pu):
                            dv.tensor_tensor(out=hhf[:], in0=sg[:], in1=pu, op=ALU.mult)
                            return dv.tensor_tensor(out=hsq[:], in0=hhf[:],
                                                    in1=hhf[:], op=ALU.mult)
                        ev_hf = kb.on("vector", e_fuse,
                                      wait=[ev_sg, hhf_free, hsq_free])
                        ev_hi2 = kb.on("gpsimd", lambda po, ic=ic: po.tensor_copy(
                            out=hhi[:, 512 * ic:512 * (ic + 1)], in_=hhf[:]),
                            wait=[ev_hf, hh_free])
                        ev_lo2 = kb.on("vector", lambda dv, ic=ic: dv.tensor_tensor(
                            out=hlo[:, 512 * ic:512 * (ic + 1)], in0=hhf[:],
                            in1=hhi[:, 512 * ic:512 * (ic + 1)], op=ALU.subtract),
                            wait=[ev_hi2])
                        pAB_free[2 * (ic % 2) + 1] = ev_hf
                        sg_free = ev_hf
                        hhf_free = ev_lo2

                        def e_ssum(pe, ic=ic):
                            return pe.matmul(kb.p7[0:1, :], kb.c_ones1[:], hsq[:],
                                             start=(ic == 0), stop=(ic == HCH - 1))
                        ev_ssum = kb.on("tensor", e_ssum,
                                        wait=[ev_hf] + ([p7_free] if ic == 0 else []))
                        hsq_free = ev_ssum
                        prev_pe = ev_ssum
                hh_last = [ev_lo2]

            with ExitStack() as stB:
                def sbtB(n, sh, dt=F32):
                    return stB.enter_context(nc.sbuf_tensor(f"{n}_m{m}", sh, dt))
                wpbuf = sbtB("wp2_0", [128, HCH * 128], BF16)
                rrow = sbtB("rrow2", [1, 512])
                rb = sbtB("rb2", [128, 512])
                ostg = [sbtB(f"ostg2_{i}", [128, 512]) for i in range(2)]
                ffn_tm = sbtB("ffn2", [128, NSUB * D])
                ybuf = sbtB("yb2", [128, D])
                x1buf = sbtB("x1b2", [128, D])

                def e_rr(ac):
                    ac.activation(out=rrow[0:1, :], in_=kb.p7[0:1, :], func=AF.Identity,
                                  scale=1.0 / H_FFN)
                    return ac.activation(out=rrow[0:1, :], in_=rrow[0:1, :], func=AF.Ln,
                                         bias=kb.c_eps[0:1, 1:2])
                ev_r1 = kb.on("scalar", e_rr, wait=[ev_ssum])
                p7_free = ev_r1
                ev_r2 = kb.on("scalar", lambda ac: ac.activation(
                    out=rrow[0:1, :], in_=rrow[0:1, :], func=AF.Exp, scale=-0.5),
                    wait=[ev_r1])

                ev_bc = kb.on("tensor", lambda pe: pe.matmul(
                    kb.p6[:, 0:512], kb.c_ones[0:1, 0:128], rrow[0:1, :],
                    start=True, stop=True), wait=[ev_r2, rb_free])
                ev_rb = kb.on("vector", lambda dv: dv.tensor_copy(
                    out=rb[:], in_=kb.p6[:, 0:512]), wait=[ev_bc])
                prev_pe = ev_bc

                HH2 = HCH // 2
                wh_free = [[prev_pe] + hh_last, [prev_pe] + hh_last]
                pd_free = [entry, entry]
                ostg_free = [entry, entry]
                tpB_free = {0: entry, 1: entry}
                tp_done = []
                rb_users = []
                for dc in range(DCH):
                    ev_wh = []
                    for h in range(2):
                        ev = kb.dma(
                            "sync", wpbuf[:, 128 * HH2 * h:128 * HH2 * (h + 1)],
                            kb.wpt_s[dc][:, HH2 * h:HH2 * (h + 1), :]
                            .rearrange("p j c -> p (j c)"),
                            wait=[wh_free[h]])
                        ev_wh.append(ev)

                    pd = (kb.PB[0], kb.PB[1])[dc % 2]

                    def e_dmmA(pe, pd=pd):
                        for j in range(HH2):
                            pe.matmul(pd, wpbuf[:, 128 * j:128 * (j + 1)],
                                      hhi[:, 512 * j:512 * (j + 1)],
                                      start=(j == 0), stop=False)
                            ins = pe.matmul(pd, wpbuf[:, 128 * j:128 * (j + 1)],
                                            hlo[:, 512 * j:512 * (j + 1)],
                                            start=False, stop=False)
                            kb.ldw_patch.append(ins.ins)
                        return ins
                    ev_dA = kb.on("tensor", e_dmmA,
                                  wait=[ev_wh[0], pd_free[dc % 2]] + hh_last)
                    wh_free[0] = ev_dA

                    def e_dmmB(pe, pd=pd):
                        for j in range(HH2, HCH):
                            pe.matmul(pd, wpbuf[:, 128 * j:128 * (j + 1)],
                                      hhi[:, 512 * j:512 * (j + 1)],
                                      start=False, stop=False)
                            ins = pe.matmul(pd, wpbuf[:, 128 * j:128 * (j + 1)],
                                            hlo[:, 512 * j:512 * (j + 1)],
                                            start=False, stop=(j == HCH - 1))
                            kb.ldw_patch.append(ins.ins)
                        return ins
                    ev_dmm = kb.on("tensor", e_dmmB, wait=[ev_wh[1]])
                    wh_free[1] = ev_dmm
                    prev_pe = ev_dmm

                    osl = dc % 2
                    ev_oev = kb.on("vector", lambda dv, pd=pd, osl=osl: dv.tensor_tensor(
                        out=ostg[osl][:], in0=pd, in1=rb[:], op=ALU.mult),
                        wait=[ev_dmm, ostg_free[osl], ev_rb])
                    pd_free[dc % 2] = ev_oev
                    rb_users.append(ev_oev)

                    pv = (kb.p4, kb.p5)[osl]

                    def e_tp(pe, pv=pv, osl=osl):
                        for s in range(NSUB):
                            ins = pe.transpose(pv[:, 128 * s:128 * (s + 1)],
                                               ostg[osl][:, 128 * s:128 * (s + 1)],
                                               kb.c_ident[:])
                        return ins
                    ev_tp = kb.on("tensor", e_tp, wait=[ev_oev, tpB_free[osl], ffn_free])
                    ostg_free[osl] = ev_tp
                    prev_pe = ev_tp

                    def e_cp(ac, pv=pv, dc=dc):
                        for s in range(NSUB):
                            ins = kb.aid(ac, ffn_tm[:, D * s + 128 * dc:D * s + 128 * (dc + 1)],
                                         pv[:, 128 * s:128 * (s + 1)])
                        return ins
                    ev_cp = kb.on("scalar", e_cp, wait=[ev_tp])
                    tpB_free[osl] = ev_cp
                    tp_done.append(ev_cp)
                hh_free = prev_pe
                rb_free = rb_users[-1]

                for s in range(NSUB):
                    ev_x1 = kb.dma("scalar", x1buf[:],
                                   x_src[r0 + 128 * s:r0 + 128 * (s + 1), :],
                                   wait=[epi_prev])
                    fv = ffn_tm[:, D * s:D * (s + 1)]
                    ev_ln = kb.ln(fv, ybuf[:], [ev_x1] + tp_done)
                    ev_x2 = kb.on("vector", lambda dv: dv.tensor_tensor(
                        out=x1buf[:], in0=x1buf[:], in1=ybuf[:], op=ALU.add), wait=[ev_ln])
                    ev_st = kb.dma("scalar", kb.x2_s[r0 + 128 * s:r0 + 128 * (s + 1), :],
                                   x1buf[:], wait=[ev_x2])
                    epi_prev = ev_st
                    x2_evs.append(ev_st)
                ffn_free = x2_evs[-1]

    return kb.x2_s, [x2_evs[-1]]




def emit_sb3(kb: KB, x_src, in_evs):
    nc = kb.nc
    entry = kb.barrier_ev()

    with ExitStack() as stp:
        def sbtp(n, sh, dt=F32):
            return stp.enter_context(nc.sbuf_tensor(n, sh, dt))
        kb.c_iota = sbtp("c_iota", [128, S])
        ev_iota = kb.dma("sync", kb.c_iota[:], kb.iota[:], wait=entry)
        carry = sbtp("carry3", [128, DCH])
        acc_s = sbtp("acc3_s", [128, DCH])
        acc_q = sbtp("acc3_q", [128, DCH])
        srow = sbtp("srow3", [1, 4])
        stc = sbtp("stc3", [128, 8])
        A_col = sbtp("A3", [128, DCH])
        B_col = sbtp("B3", [128, DCH])

        x3_evs = [entry]
        epi_prev = entry
        prev_pe = entry
        att_rd_free = entry
        ab_users = entry

        for b in range(BPC):
            def e_rst(po):
                po.memset(carry[:], 0.0)
                po.memset(acc_s[:], 0.0)
                return po.memset(acc_q[:], 0.0)
            ev_rst = kb.on("gpsimd", e_rst, wait=[ab_users])

            with ExitStack() as stA:
                def sbtA(n, sh, dt=F32):
                    return stA.enter_context(nc.sbuf_tensor(f"{n}_b{b}", sh, dt))
                z_mac = sbtA("z3", [128, NSUB * D])
                xbuf = sbtA("x3ld", [128, D])
                zT = sbtA("z3T", [128, DCH * 512])
                r_all = sbtA("r3", [128, DCH * 512])
                kv_all = sbtA("kv3", [128, DCH * 512])
                wbuf = [sbtA(f"w3buf{i}", [128, DCH * 128]) for i in range(3)]
                scl = [sbtA(f"scl3_{i}", [128, 512]) for i in range(2)]
                arg = sbtA("arg3", [128, 512])
                clp = sbtA("clp3", [128, 512])
                inv = sbtA("inv3", [128, 512])
                scd = sbtA("scd3", [128, 512])
                cs = [sbtA(f"cs3_{i}", [128, 512]) for i in range(2)]
                stt_ = [sbtA(f"st3_{i}", [128, 512]) for i in range(2)]
                attst = [sbtA(f"att3_{i}", [128, 512]) for i in range(2)]
                sqd = sbtA("sqd3", [128, 512])

                macro_gate = [epi_prev, prev_pe, ev_rst]
                xbuf_free = macro_gate
                wbuf_free = [macro_gate, macro_gate, macro_gate]
                pbank_free = {0: macro_gate, 1: macro_gate}
                tpp_free = {0: macro_gate, 1: macro_gate}
                kcol_free = {}
                scl_free = [macro_gate, macro_gate]
                cs_free = [macro_gate, macro_gate]
                st_free = [macro_gate, macro_gate]
                att_free = [macro_gate, macro_gate]
                arg_free = macro_gate
                clp_free = macro_gate
                inv_free = macro_gate
                scd_free = macro_gate
                sqd_free = macro_gate
                racc_prev = macro_gate
                qacc_prev = macro_gate
                carry_rd = {c: [ev_rst] for c in range(DCH)}
                r_free = {c: macro_gate for c in range(DCH)}
                kv_free = {c: macro_gate for c in range(DCH)}

                for tw in range(MPB):
                    t0 = tw * MAC
                    r0 = b * S + t0
                    ln_evs = []
                    for s in range(NSUB):
                        ev_x = kb.dma("scalar", xbuf[:],
                                      x_src[r0 + 128 * s:r0 + 128 * (s + 1), :],
                                      wait=[xbuf_free] + (in_evs if b == 0 and tw == 0 and s == 0 else []))
                        ev_ln = kb.ln(xbuf[:], z_mac[:, D * s:D * (s + 1)],
                                      [ev_x] + macro_gate)
                        xbuf_free = ev_ln
                        ln_evs.append(ev_ln)
                    tp_evs = kb.transpose_mac(
                        [z_mac[:, D * s:D * (s + 1)] for s in range(NSUB)],
                        ln_evs, zT, [512 * j for j in range(DCH)],
                        (kb.PB[2], kb.PB[3]), tpp_free)
                    zT_ready = [tp_evs[-1]]
                    prev_pe = tp_evs[-1]

                    kv_done = {}
                    r_done = {}
                    for pi, wt in enumerate([kb.wr1_t, kb.wk1_t, kb.wv1_t]):
                        for i in range(DCH):
                            g = pi * DCH + i
                            ws = g % 3
                            pb = kb.PB[g % 2]
                            ev_w = kb.dma("sync", wbuf[ws][:], wt[i].flatten_outer_dims(),
                                          wait=[wbuf_free[ws]])

                            def e_mm(pe, ws=ws, pb=pb):
                                for j in range(DCH):
                                    ins = pe.matmul(pb, wbuf[ws][:, 128 * j:128 * (j + 1)],
                                                    zT[:, 512 * j:512 * (j + 1)],
                                                    start=(j == 0), stop=(j == DCH - 1))
                                return ins
                            ev_mm = kb.on("tensor", e_mm,
                                          wait=[ev_w, pbank_free[g % 2]] + zT_ready)
                            wbuf_free[ws] = ev_mm
                            prev_pe = ev_mm

                            if pi == 0:
                                ev_ev = kb.on("scalar", lambda ac, pb=pb, i=i: ac.activation(
                                    out=r_all[:, 512 * i:512 * (i + 1)], in_=pb,
                                    func=AF.Sigmoid), wait=[ev_mm, r_free[i]])
                                r_done[i] = ev_ev
                            elif pi == 1:
                                ev_ev = kb.on("scalar", lambda ac, pb=pb, i=i: kb.aid(
                                    ac, kv_all[:, 512 * i:512 * (i + 1)], pb),
                                    wait=[ev_mm, kv_free[i]])
                                kcol_free[i] = ev_ev
                            else:
                                ev_ev = kb.on("vector", lambda dv, pb=pb, i=i: dv.tensor_tensor(
                                    out=kv_all[:, 512 * i:512 * (i + 1)],
                                    in0=kv_all[:, 512 * i:512 * (i + 1)],
                                    in1=pb, op=ALU.mult), wait=[ev_mm, kcol_free[i]])
                                kv_done[i] = ev_ev
                            pbank_free[g % 2] = ev_ev

                    for c in range(DCH):
                        sl = c % 2
                        it = kb.c_iota[:, t0:t0 + 512]
                        ld_col = kb.c_logd[:, c:c + 1]

                        def e_arg(dv, it=it, ld_col=ld_col):
                            dv.tensor_scalar_mul(arg[:], it, ld_col)
                            return dv.tensor_scalar(out=clp[:], in0=it, scalar1=ld_col,
                                                    scalar2=LN_CLIP_LO,
                                                    op0=ALU.mult, op1=ALU.max)
                        ev_arg = kb.on("vector", e_arg,
                                       wait=[ev_iota, kb.ev_logd, arg_free, clp_free])

                        def e_exp(ac, sl=sl):
                            ac.activation(out=scl[sl][:], in_=arg[:], func=AF.Exp)
                            return ac.activation(out=inv[:], in_=clp[:], func=AF.Exp,
                                                 scale=-1.0)
                        ev_scl = kb.on("scalar", e_exp,
                                       wait=[ev_arg, scl_free[sl], inv_free])
                        arg_free = ev_scl
                        clp_free = ev_scl

                        def e_scan(dv, sl=sl, c=c):
                            dv.tensor_tensor(out=scd[:], in0=kv_all[:, 512 * c:512 * (c + 1)],
                                             in1=inv[:], op=ALU.mult)
                            return dv.tensor_tensor_scan(
                                out=cs[sl][:], data0=kb.c_ones[:], data1=scd[:],
                                initial=carry[:, c:c + 1], op0=ALU.mult, op1=ALU.add)
                        ev_cs = kb.on("vector", e_scan,
                                      wait=[ev_scl, kv_done[c], scd_free, cs_free[sl]]
                                      + carry_rd[c] + [kb.ev_ones])
                        inv_free = ev_cs
                        scd_free = ev_cs
                        kv_free[c] = ev_cs
                        ev_cw = kb.on("scalar", lambda ac, sl=sl, c=c: kb.aid(
                            ac, carry[:, c:c + 1], cs[sl][:, 511:512]), wait=[ev_cs])
                        carry_rd[c] = [ev_cw]

                        def e_att(dv, sl=sl, c=c):
                            dv.tensor_tensor(out=stt_[sl][:], in0=cs[sl][:], in1=scl[sl][:],
                                             op=ALU.mult)
                            return dv.tensor_tensor(out=attst[sl][:], in0=stt_[sl][:],
                                                    in1=r_all[:, 512 * c:512 * (c + 1)],
                                                    op=ALU.mult)
                        ev_att = kb.on("vector", e_att,
                                       wait=[ev_cw, r_done[c], st_free[sl], att_free[sl]])
                        cs_free[sl] = [ev_att, ev_cw]
                        scl_free[sl] = ev_att
                        st_free[sl] = ev_att
                        r_free[c] = ev_att

                        def e_stats(dv, sl=sl):
                            dv.tensor_reduce(out=stc[:, 4:5], in_=attst[sl][:], axis=AX.X,
                                             op=ALU.add)
                            dv.tensor_tensor(out=sqd[:], in0=attst[sl][:], in1=attst[sl][:],
                                             op=ALU.mult)
                            return dv.tensor_reduce(out=stc[:, 5:6], in_=sqd[:], axis=AX.X,
                                                    op=ALU.add)
                        ev_stats = kb.on("vector", e_stats,
                                         wait=[ev_att, sqd_free, racc_prev, qacc_prev])

                        def e_acc(ac, c=c):
                            kb.pin(ac)
                            ac.activation(out=acc_s[:, c:c + 1], in_=acc_s[:, c:c + 1],
                                          func=AF.Identity, bias=stc[:, 4:5])
                            return ac.activation(out=acc_q[:, c:c + 1], in_=acc_q[:, c:c + 1],
                                                 func=AF.Identity, bias=stc[:, 5:6])
                        ev_acc = kb.on("scalar", e_acc, wait=[ev_stats])
                        racc_prev = ev_acc
                        qacc_prev = ev_acc
                        sqd_free = ev_stats
                        ev_dma = kb.dma("gpsimd", kb.att_s[b, c, :, t0:t0 + 512],
                                        attst[sl][:], wait=[ev_att])
                        att_free[sl] = [ev_dma, ev_stats]
                ab_last = [racc_prev, qacc_prev]
                stat_dma = (kb.sW, kb.sW.v)

            def e_sums(pe):
                pe.matmul(kb.p7[0:1, 0:16], kb.c_ones1[:], acc_s[:], start=True, stop=True)
                return pe.matmul(kb.p7[0:1, 16:32], kb.c_ones1[:], acc_q[:],
                                 start=False, stop=True)
            ev_sums = kb.on("tensor", e_sums, wait=ab_last + [prev_pe])
            prev_pe = ev_sums

            def e_red(dv):
                dv.tensor_reduce(out=srow[0:1, 0:1], in_=kb.p7[0:1, 0:16], axis=AX.X,
                                 op=ALU.add)
                return dv.tensor_reduce(out=srow[0:1, 1:2], in_=kb.p7[0:1, 16:32],
                                        axis=AX.X, op=ALU.add)
            ev_red = kb.on("vector", e_red, wait=[ev_sums])

            invN = 1.0 / (S * D)

            def e_mm2(ac):
                ac.activation(out=srow[0:1, 2:3], in_=srow[0:1, 0:1], func=AF.Identity,
                              scale=invN)
                return ac.activation(out=srow[0:1, 3:4], in_=srow[0:1, 1:2],
                                     func=AF.Identity, scale=invN)
            ev_m2 = kb.on("scalar", e_mm2, wait=[ev_red])

            ev_nm2 = kb.on("vector", lambda dv: dv.scalar_tensor_tensor(
                out=srow[0:1, 0:1], in0=srow[0:1, 2:3], scalar=-1.0,
                in1=srow[0:1, 2:3], op0=ALU.mult, op1=ALU.mult), wait=[ev_m2])

            def e_rstd(ac):
                ac.activation(out=srow[0:1, 1:2], in_=srow[0:1, 3:4], func=AF.Identity,
                              bias=srow[0:1, 0:1])
                ac.activation(out=srow[0:1, 1:2], in_=srow[0:1, 1:2], func=AF.Ln,
                              bias=kb.c_eps[0:1, 0:1])
                return ac.activation(out=srow[0:1, 1:2], in_=srow[0:1, 1:2],
                                     func=AF.Exp, scale=-0.5)
            ev_rstd = kb.on("scalar", e_rstd, wait=[ev_nm2])

            def e_bc(pe):
                return pe.matmul(kb.p7[:, 32:34], kb.c_ones[0:1, 0:128], srow[0:1, 1:3],
                                 start=True, stop=True)
            ev_bc = kb.on("tensor", e_bc, wait=[ev_rstd])
            prev_pe = ev_bc

            ev_bcc = kb.on("vector", lambda dv: dv.tensor_copy(
                out=stc[:, 0:2], in_=kb.p7[:, 32:34]), wait=[ev_bc])

            ev_A = kb.on("scalar", lambda ac: ac.activation(
                out=A_col[:], in_=kb.c_gnw[:], func=AF.Identity, scale=stc[:, 0:1]),
                wait=[ev_bcc])
            ev_ngm = kb.on("vector", lambda dv: dv.tensor_scalar_mul(
                stc[:, 2:3], stc[:, 1:2], -1.0), wait=[ev_A])
            ev_t1 = kb.on("gpsimd", lambda po: po.tensor_scalar_mul(
                B_col[:], A_col[:], stc[:, 2:3]), wait=[ev_ngm])
            ev_B = kb.on("vector", lambda dv: dv.tensor_tensor(
                out=B_col[:], in0=B_col[:], in1=kb.c_gnb[:], op=ALU.add), wait=[ev_t1])
            ab_users = None

            with ExitStack() as stC:
                def sbtC(n, sh, dt=F32):
                    return stC.enter_context(nc.sbuf_tensor(f"{n}_b{b}", sh, dt))
                att_all = sbtC("attld3", [128, DCH * 512])
                wbuf = [sbtC(f"w3cbuf{i}", [128, DCH * 128]) for i in range(2)]
                wo_stage = [sbtC(f"wostg3_{i}", [128, 512]) for i in range(2)]
                ffn_tm = sbtC("att3_tm", [128, NSUB * D])
                ybuf = sbtC("yb3", [128, D])
                x2buf = sbtC("x2b3", [128, D])

                cgate = [prev_pe, epi_prev, ev_B, stat_dma]
                wbuf_free = [cgate, cgate]
                pbank_free = [cgate, cgate]
                wostg_free = [cgate, cgate]
                tpc_free = {0: cgate, 1: cgate}
                att_ld_free = cgate
                ffn_free = cgate
                ab_user_evs = []

                for tw in range(MPB):
                    t0 = tw * MAC
                    r0 = b * S + t0
                    ev_ald = kb.dma_multi("scalar", [
                        (att_all[:, 512 * c:512 * (c + 1)], kb.att_s[b, c, :, t0:t0 + 512])
                        for c in range(DCH)], wait=[att_ld_free])

                    nrm_evs = []
                    for c in range(DCH):
                        ev_n = kb.on("vector", lambda dv, c=c: dv.tensor_scalar(
                            out=att_all[:, 512 * c:512 * (c + 1)],
                            in0=att_all[:, 512 * c:512 * (c + 1)],
                            scalar1=A_col[:, c:c + 1], scalar2=B_col[:, c:c + 1],
                            op0=ALU.mult, op1=ALU.add), wait=[ev_ald, ev_B])
                        nrm_evs.append(ev_n)
                        ab_user_evs.append(ev_n)

                    tp_done = []
                    for i in range(DCH):
                        ws = i % 2
                        pb = kb.PB[i % 2]
                        ev_w = kb.dma("sync", wbuf[ws][:], kb.wo1_t[i].flatten_outer_dims(),
                                      wait=[wbuf_free[ws]])

                        def e_mm(pe, ws=ws, pb=pb):
                            for j in range(DCH):
                                ins = pe.matmul(pb, wbuf[ws][:, 128 * j:128 * (j + 1)],
                                                att_all[:, 512 * j:512 * (j + 1)],
                                                start=(j == 0), stop=(j == DCH - 1))
                            return ins
                        ev_mm = kb.on("tensor", e_mm,
                                      wait=[ev_w, pbank_free[i % 2]] + nrm_evs)
                        wbuf_free[ws] = ev_mm
                        prev_pe = ev_mm

                        sl = i % 2
                        ev_evac = kb.on("scalar", lambda ac, pb=pb, sl=sl: kb.aid(
                            ac, wo_stage[sl][:], pb), wait=[ev_mm, wostg_free[sl]])
                        pbank_free[i % 2] = ev_evac

                        pv = (kb.p4, kb.p5)[sl]

                        def e_tp(pe, sl=sl, pv=pv):
                            for s in range(NSUB):
                                ins = pe.transpose(pv[:, 128 * s:128 * (s + 1)],
                                                   wo_stage[sl][:, 128 * s:128 * (s + 1)],
                                                   kb.c_ident[:])
                            return ins
                        ev_tp = kb.on("tensor", e_tp,
                                      wait=[ev_evac, tpc_free[sl], ffn_free])
                        wostg_free[sl] = ev_tp
                        prev_pe = ev_tp

                        def e_cp(dv, pv=pv, i=i):
                            for s in range(NSUB):
                                ins = dv.tensor_copy(
                                    out=ffn_tm[:, D * s + 128 * i:D * s + 128 * (i + 1)],
                                    in_=pv[:, 128 * s:128 * (s + 1)])
                            return ins
                        ev_cp = kb.on("vector", e_cp, wait=[ev_tp])
                        tpc_free[sl] = ev_cp
                        tp_done.append(ev_cp)
                    att_ld_free = ev_mm

                    for s in range(NSUB):
                        ev_x2 = kb.dma("scalar", x2buf[:],
                                       x_src[r0 + 128 * s:r0 + 128 * (s + 1), :],
                                       wait=[epi_prev])
                        fv = ffn_tm[:, D * s:D * (s + 1)]
                        ev_ln = kb.ln(fv, ybuf[:], [ev_x2] + tp_done)
                        ev_x3 = kb.on("vector", lambda dv: dv.tensor_tensor(
                            out=x2buf[:], in0=x2buf[:], in1=ybuf[:], op=ALU.add),
                            wait=[ev_ln])
                        ev_st = kb.dma("scalar", kb.x3_s[r0 + 128 * s:r0 + 128 * (s + 1), :],
                                       x2buf[:], wait=[ev_x3])
                        epi_prev = ev_st
                        x3_evs.append(ev_st)
                    ffn_free = x3_evs[-1]
                ab_users = ab_user_evs[-1]

    return kb.x3_s, [x3_evs[-1]]




def emit_sb4(kb: KB, x_src, in_evs):
    nc = kb.nc
    entry = kb.barrier_ev()

    with ExitStack() as stp:
        def sbtp(n, sh, dt=F32):
            return stp.enter_context(nc.sbuf_tensor(n, sh, dt))
        hhi = sbtp("hh4hi", [128, HCH * 512], BF16)
        hlo = sbtp("hh4lo", [128, HCH * 512], BF16)
        wdbuf0 = sbtp("wd4p", [128, HCH * 128], BF16)

        out_evs = [entry]
        epi_prev = entry
        hh_free = entry
        prev_pe = entry
        ffn_free = entry

        for m in range(NMAC):
            b, tw = divmod(m, MPB)
            t0 = tw * MAC
            r0 = b * S + t0

            with ExitStack() as stA:
                def sbtA(n, sh, dt=F32):
                    return stA.enter_context(nc.sbuf_tensor(f"{n}_m{m}", sh, dt))
                yhi = sbtA("y4hi", [128, DCH * 512], BF16)
                ylo = sbtA("y4lo", [128, DCH * 512], BF16)
                stA1 = stA.enter_context(ExitStack())

                def sbtA1(n, sh, dt=F32):
                    return stA1.enter_context(nc.sbuf_tensor(f"{n}_m{m}", sh, dt))
                z_mac = sbtA1("z4", [128, 2 * D])
                xbuf = sbtA1("x4ld", [128, D])

                macro_gate = [epi_prev, prev_pe]
                tpp_free = {0: macro_gate, 1: macro_gate}
                xbuf_free = macro_gate
                last_lo = None
                last_tp = None
                z_free = macro_gate
                for sp in range(2):
                    ln_evs = []
                    for half in range(2):
                        s = 2 * sp + half
                        ev_x = kb.dma("scalar", xbuf[:],
                                      x_src[r0 + 128 * s:r0 + 128 * (s + 1), :],
                                      wait=[xbuf_free] + (in_evs if m == 0 and s == 0 else []))
                        ev_ln = kb.ln(xbuf[:], z_mac[:, D * half:D * (half + 1)],
                                      [ev_x] + z_free)
                        xbuf_free = ev_ln
                        ln_evs.append(ev_ln)
                    for j in range(DCH):
                        pv = (kb.PB[2], kb.PB[3])[j % 2]

                        def e_tp(pe, j=j, pv=pv):
                            for half in range(2):
                                ins = pe.transpose(pv[:, 128 * half:128 * (half + 1)],
                                                   z_mac[:, D * half + 128 * j:D * half + 128 * (j + 1)],
                                                   kb.c_ident[:])
                            return ins
                        ev_tp = kb.on("tensor", e_tp, wait=ln_evs + [tpp_free[j % 2]])

                        col = 512 * j + 256 * sp
                        ev_hi = kb.on("scalar", lambda ac, pv=pv, col=col: kb.aid(
                            ac, yhi[:, col:col + 256], pv[:, 0:256]), wait=[ev_tp])
                        ev_lo = kb.on("vector", lambda dv, pv=pv, col=col: dv.tensor_tensor(
                            out=ylo[:, col:col + 256], in0=pv[:, 0:256],
                            in1=yhi[:, col:col + 256], op=ALU.subtract), wait=[ev_hi])
                        tpp_free[j % 2] = ev_lo
                        last_lo = ev_lo
                        last_tp = ev_tp
                    z_free = [last_tp]
                y_ready = [last_lo]
                stA1.close()
                wgbuf = [sbtA(f"wg4_{i}", [128, DCH * 128], BF16) for i in range(2)]
                wubuf = [sbtA(f"wu4_{i}", [128, DCH * 128], BF16) for i in range(2)]
                sgb = sbtA("sgb4", [128, 512])
                ub = sbtA("ub4", [128, 512])
                t1 = sbtA("t1_4", [128, 512])
                hhf = sbtA("hhf4", [128, 512])

                wg_free = [[last_tp] + macro_gate, [last_tp] + macro_gate]
                sgb_free = macro_gate
                ub_free = macro_gate
                t1_free = macro_gate
                hhf_free = macro_gate
                pAB_free = {0: macro_gate, 1: macro_gate, 2: macro_gate, 3: macro_gate}

                for ic in range(HCH):
                    wsl = ic % 2
                    ev_wg = kb.dma("sync", wgbuf[wsl][:],
                                   kb.gwb_s[ic].rearrange("p j c -> p (j c)"),
                                   wait=[wg_free[wsl]])
                    ev_wu = kb.dma("sync", wubuf[wsl][:],
                                   kb.uwb_s[ic].rearrange("p j c -> p (j c)"),
                                   wait=[wg_free[wsl]])
                    pg = kb.PB[2 * (ic % 2)]
                    pu = kb.PB[2 * (ic % 2) + 1]

                    def e_mm(pe, wsl=wsl, pg=pg, pu=pu):
                        for j in range(DCH):
                            wgt = wgbuf[wsl][:, 128 * j:128 * (j + 1)]
                            wut = wubuf[wsl][:, 128 * j:128 * (j + 1)]
                            pe.matmul(pg, wgt, yhi[:, 512 * j:512 * (j + 1)],
                                      start=(j == 0), stop=False)
                            kb.ldw_patch.append(pe.matmul(pg, wgt, ylo[:, 512 * j:512 * (j + 1)],
                                      start=False, stop=(j == DCH - 1)).ins)
                            pe.matmul(pu, wut, yhi[:, 512 * j:512 * (j + 1)],
                                      start=(j == 0), stop=False)
                            ins = pe.matmul(pu, wut, ylo[:, 512 * j:512 * (j + 1)],
                                            start=False, stop=(j == DCH - 1))
                            kb.ldw_patch.append(ins.ins)
                        return ins
                    ev_mm = kb.on("tensor", e_mm,
                                  wait=[ev_wg, ev_wu, pAB_free[2 * (ic % 2)],
                                        pAB_free[2 * (ic % 2) + 1]] + y_ready)
                    prev_pe = ev_mm
                    wg_free[wsl] = ev_mm

                    gs_col = kb.c_gs[:, ic:ic + 1]
                    us_col = kb.c_us[:, ic:ic + 1]
                    ev_sgb = kb.on("scalar", lambda ac, pg=pg, gs_col=gs_col: ac.activation(
                        out=sgb[:], in_=pg, func=AF.Sigmoid, scale=gs_col),
                        wait=[ev_mm, sgb_free])

                    def e_fuse(dv, pg=pg, pu=pu, gs_col=gs_col, us_col=us_col):
                        dv.tensor_tensor(out=t1[:], in0=pg, in1=sgb[:], op=ALU.mult)
                        dv.tensor_scalar_mul(ub[:], pu, us_col)
                        return dv.scalar_tensor_tensor(out=hhf[:], in0=t1[:],
                                                       scalar=gs_col, in1=ub[:],
                                                       op0=ALU.mult, op1=ALU.mult)
                    ev_hf = kb.on("vector", e_fuse,
                                  wait=[ev_sgb, t1_free, ub_free, hhf_free])
                    ev_hi2 = kb.on("gpsimd", lambda po, ic=ic: po.tensor_copy(
                        out=hhi[:, 512 * ic:512 * (ic + 1)], in_=hhf[:]),
                        wait=[ev_hf, hh_free])
                    ev_lo2 = kb.on("vector", lambda dv, ic=ic: dv.tensor_tensor(
                        out=hlo[:, 512 * ic:512 * (ic + 1)], in0=hhf[:],
                        in1=hhi[:, 512 * ic:512 * (ic + 1)], op=ALU.subtract),
                        wait=[ev_hi2])
                    pAB_free[2 * (ic % 2)] = ev_hf
                    pAB_free[2 * (ic % 2) + 1] = ev_hf
                    sgb_free = ev_hf
                    t1_free = ev_hf
                    ub_free = ev_hf
                    hhf_free = ev_lo2
                hh_last = [ev_lo2]

            with ExitStack() as stB:
                def sbtB(n, sh, dt=F32):
                    return stB.enter_context(nc.sbuf_tensor(f"{n}_m{m}", sh, dt))
                ostg = [sbtB(f"ostg4_{i}", [128, 512]) for i in range(2)]
                ffn_tm = sbtB("ffn4", [128, NSUB * D])
                ybuf = sbtB("yb4", [128, D])
                x3buf = sbtB("x3b4", [128, D])

                HH2 = HCH // 2
                wh_free = [[prev_pe] + hh_last, [prev_pe] + hh_last]
                pd_free = [entry, entry]
                ostg_free = [entry, entry]
                tpB_free = {0: entry, 1: entry}
                tp_done = []
                for dc in range(DCH):
                    ev_wh = []
                    for h in range(2):
                        ev = kb.dma(
                            "sync", wdbuf0[:, 128 * HH2 * h:128 * HH2 * (h + 1)],
                            kb.dwb_s[dc][:, HH2 * h:HH2 * (h + 1), :]
                            .rearrange("p j c -> p (j c)"),
                            wait=[wh_free[h]])
                        ev_wh.append(ev)

                    pd = (kb.PB[0], kb.PB[1])[dc % 2]

                    def e_dmmA(pe, pd=pd):
                        for j in range(HH2):
                            pe.matmul(pd, wdbuf0[:, 128 * j:128 * (j + 1)],
                                      hhi[:, 512 * j:512 * (j + 1)],
                                      start=(j == 0), stop=False)
                            ins = pe.matmul(pd, wdbuf0[:, 128 * j:128 * (j + 1)],
                                            hlo[:, 512 * j:512 * (j + 1)],
                                            start=False, stop=False)
                            kb.ldw_patch.append(ins.ins)
                        return ins
                    ev_dA = kb.on("tensor", e_dmmA,
                                  wait=[ev_wh[0], pd_free[dc % 2]] + hh_last)
                    wh_free[0] = ev_dA

                    def e_dmmB(pe, pd=pd):
                        for j in range(HH2, HCH):
                            pe.matmul(pd, wdbuf0[:, 128 * j:128 * (j + 1)],
                                      hhi[:, 512 * j:512 * (j + 1)],
                                      start=False, stop=False)
                            ins = pe.matmul(pd, wdbuf0[:, 128 * j:128 * (j + 1)],
                                            hlo[:, 512 * j:512 * (j + 1)],
                                            start=False, stop=(j == HCH - 1))
                            kb.ldw_patch.append(ins.ins)
                        return ins
                    ev_dmm = kb.on("tensor", e_dmmB, wait=[ev_wh[1]])
                    wh_free[1] = ev_dmm
                    prev_pe = ev_dmm

                    osl = dc % 2
                    ev_oev = kb.on("scalar", lambda ac, pd=pd, osl=osl, dc=dc: ac.activation(
                        out=ostg[osl][:], in_=pd, func=AF.Identity,
                        scale=kb.c_ds[:, dc:dc + 1]),
                        wait=[ev_dmm, ostg_free[osl]])
                    pd_free[dc % 2] = ev_oev

                    pv = (kb.p4, kb.p5)[osl]

                    def e_tp(pe, pv=pv, osl=osl):
                        for s in range(NSUB):
                            ins = pe.transpose(pv[:, 128 * s:128 * (s + 1)],
                                               ostg[osl][:, 128 * s:128 * (s + 1)],
                                               kb.c_ident[:])
                        return ins
                    ev_tp = kb.on("tensor", e_tp, wait=[ev_oev, tpB_free[osl], ffn_free])
                    ostg_free[osl] = ev_tp
                    prev_pe = ev_tp

                    def e_cp(dv, pv=pv, dc=dc):
                        for s in range(NSUB):
                            ins = dv.tensor_copy(
                                out=ffn_tm[:, D * s + 128 * dc:D * s + 128 * (dc + 1)],
                                in_=pv[:, 128 * s:128 * (s + 1)])
                        return ins
                    ev_cp = kb.on("vector", e_cp, wait=[ev_tp])
                    tpB_free[osl] = ev_cp
                    tp_done.append(ev_cp)
                hh_free = prev_pe

                for s in range(NSUB):
                    ev_x3 = kb.dma("scalar", x3buf[:],
                                   x_src[r0 + 128 * s:r0 + 128 * (s + 1), :],
                                   wait=[epi_prev])
                    fv = ffn_tm[:, D * s:D * (s + 1)]
                    ev_ln = kb.ln(fv, ybuf[:], [ev_x3] + tp_done)
                    ev_x4 = kb.on("vector", lambda dv: dv.tensor_tensor(
                        out=x3buf[:], in0=x3buf[:], in1=ybuf[:], op=ALU.add), wait=[ev_ln])
                    ev_st = kb.dma("scalar", kb.out[r0 + 128 * s:r0 + 128 * (s + 1), :],
                                   x3buf[:], wait=[ev_x4])
                    epi_prev = ev_st
                    out_evs.append(ev_st)
                ffn_free = out_evs[-1]

    return kb.out, [out_evs[-1]]




def emit_finale(kb, src, evs, upto):
    if upto < 4:
        ev = kb.dma("sync", kb.out[:], src[:], wait=kb.barrier_ev() + list(evs))
    else:
        ev = None
    for name in DEBUG_DUMPS:
        scr_ap = getattr(kb, name)
        dbg = kb.nc.dram_tensor("dbg_" + name, list(scr_ap.shape), scr_ap.dtype,
                                kind="ExternalOutput").ap()
        ev = kb.dma("sync", dbg.flatten_outer_dims(), scr_ap.flatten_outer_dims(),
                    wait=[ev] + kb.barrier_ev() + list(evs))

    def f(sy):
        _wait_all(sy, [ev] + list(evs) + kb.barrier_ev())
    kb.block.sync(f)


def _patch_ldweights(kb):
    import copy as _copy
    ids = {id(x) for x in kb.ldw_patch}
    if not ids:
        return
    n = 0
    for blk in kb.nc.main_func.blocks:
        il = blk.instructions
        for idx, ins in enumerate(il):
            if id(ins) in ids:
                il[idx] = _copy.replace(ins, ldweights=False)
                n += 1
    assert n == len(ids), (n, len(ids))


def _build(upto=None):
    if upto is None:
        upto = UPTO
    kb = KB(upto)
    with kb.es:
        emit_prologue(kb)
        with ExitStack() as qscope:
            qs = QuantStream(kb, qscope) if upto >= 2 else None
            src, evs = emit_sb1(kb, qs)
            if qs is not None:
                qs.finish()
        if upto >= 2:
            src, evs = emit_sb2(kb, src, evs)
        if upto >= 3:
            src, evs = emit_sb3(kb, src, evs)
        if upto >= 4:
            src, evs = emit_sb4(kb, src, evs)
        emit_finale(kb, src, evs, upto)
        _patch_ldweights(kb)
    return kb.nc


def kernel(**inputs):
    shared, shards = _prep_inputs(inputs)
    nc = _build()
    in_maps = [dict(shared, x_in=shards[c]) for c in range(NCORES)]
    res = run_bass_kernel_spmd(nc, in_maps, core_ids=list(range(NCORES)))
    out = np.empty((B, S, D), np.float32)
    for c, r in enumerate(res.results):
        out[BPC * c:BPC * (c + 1)] = r["out"].reshape(BPC, S, D)
    return out


# version 18
# speedup vs baseline: 1.1914x; 1.0742x over previous
import numpy as np
import ml_dtypes
from contextlib import ExitStack

import concourse.bass as bass
import concourse.mybir as mybir
from concourse.bass_utils import run_bass_kernel_spmd

F32 = mybir.dt.float32
BF16 = mybir.dt.bfloat16
I32 = mybir.dt.int32
AF = mybir.ActivationFunctionType
ALU = mybir.AluOpType
AX = mybir.AxisListType

B, S, D = 16, 2048, 2048
NCORES = 8
BPC = B // NCORES
T = BPC * S
NH, DK = 16, 128
H_FFN = 8192
MAC = 512
NMAC = T // MAC
MPB = NMAC // BPC
NSUB = MAC // 128
DCH = D // 128
HCH = H_FFN // 128
EPS_LN = 1e-5
EPS_RMS = float(np.finfo(np.float32).eps)
SM_SCALE = float(1.0 / np.sqrt(DK))
NEG_BIG = -1e30
BF16_CLIP_LO = float(np.float32(ml_dtypes.bfloat16(1e-8)))
RNE_C = 12582912.0
LN_CLIP_LO = float(np.log(np.float64(1e-10)))
BF16_HALF_UP = 0.5009765625

UPTO = 4
DEBUG_DUMPS = []



def _rope_tables():
    theta = (1.0 / 10000.0 ** (np.arange(0, DK, 2, dtype=np.float64) / DK))
    pos = np.arange(S, dtype=np.float64)
    fr = pos[:, None] * theta[None, :]
    cos = np.cos(fr)
    sin = np.sin(fr)
    cos_fm = np.empty((DK, S), np.float32)
    sin_fm = np.empty((DK, S), np.float32)
    cos_fm[:64] = cos.T
    cos_fm[64:] = cos.T
    sin_fm[:64] = sin.T
    sin_fm[64:] = -sin.T
    return np.ascontiguousarray(cos_fm), np.ascontiguousarray(sin_fm)


def _tile_lhsT(wT, kdim, fdim):
    kc, fc = kdim // 128, fdim // 128
    t = wT.reshape(kc, 128, fc, 128)
    return np.ascontiguousarray(t.transpose(2, 1, 0, 3))


def _cols(v):
    n = v.shape[0] // 128
    return np.ascontiguousarray(v.reshape(n, 128).T.astype(np.float32))


def _prep_inputs(inp):
    f = np.float32
    d = {}
    d["wq_t"] = _tile_lhsT(np.ascontiguousarray(inp["L0_attn_wq"].T), D, D)
    d["wk_t"] = _tile_lhsT(np.ascontiguousarray(inp["L0_attn_wk"].T), D, D)
    d["wv_t"] = _tile_lhsT(np.ascontiguousarray(inp["L0_attn_wv"].T), D, D)
    d["wo_t"] = _tile_lhsT(np.ascontiguousarray(inp["L0_attn_wo"].T), D, D)
    d["bq_c"] = _cols(inp["L0_attn_bq"])
    d["bk_c"] = _cols(inp["L0_attn_bk"])
    d["bv_c"] = _cols(inp["L0_attn_bv"])
    d["bo_c"] = _cols(inp["L0_attn_bo"])
    d["lng_b"] = np.ascontiguousarray(np.tile(inp["L0_attn_ln_g"].astype(f), (128, 1)))
    d["lnb_b"] = np.ascontiguousarray(np.tile(inp["L0_attn_ln_b"].astype(f), (128, 1)))
    d["wgu"] = np.ascontiguousarray(inp["L0_ffn_w_gate_up"].astype(f))
    d["wp"] = np.ascontiguousarray(inp["L0_ffn_w_proj"].astype(f))
    d["wr1_t"] = _tile_lhsT(np.ascontiguousarray(inp["L1_attn_Wr"].T), D, D)
    d["wk1_t"] = _tile_lhsT(np.ascontiguousarray(inp["L1_attn_Wk"].T), D, D)
    d["wv1_t"] = _tile_lhsT(np.ascontiguousarray(inp["L1_attn_Wv"].T), D, D)
    d["wo1_t"] = _tile_lhsT(np.ascontiguousarray(inp["L1_attn_Wo"].T), D, D)
    d["decay_c"] = _cols(inp["L1_attn_decay"])
    d["gnw_c"] = _cols(inp["L1_attn_gn_weight"])
    d["gnb_c"] = _cols(inp["L1_attn_gn_bias"])
    d["gw_t"] = _tile_lhsT(np.ascontiguousarray(inp["L1_ffn_gate_w"].T), D, H_FFN)
    d["uw_t"] = _tile_lhsT(np.ascontiguousarray(inp["L1_ffn_up_w"].T), D, H_FFN)
    d["dw_t"] = _tile_lhsT(np.ascontiguousarray(inp["L1_ffn_down_w"].T), H_FFN, D)
    d["gs_c"] = _cols(inp["L1_ffn_gate_scale"])
    d["us_c"] = _cols(inp["L1_ffn_up_scale"])
    d["ds_c"] = _cols(inp["L1_ffn_down_scale"])
    cos_fm, sin_fm = _rope_tables()
    d["cosf"] = cos_fm
    d["sinf"] = sin_fm
    d["ident"] = np.eye(128, dtype=f)
    m = np.zeros((128, 128), f)
    m[np.triu_indices(128, k=1)] = NEG_BIG
    d["mask"] = m
    d["iota"] = np.ascontiguousarray(np.tile(np.arange(S, dtype=f), (128, 1)))
    x = np.asarray(inp["L0_attn_h"], dtype=f).reshape(B, S, D)
    shards = [np.ascontiguousarray(x[BPC * c:BPC * (c + 1)].reshape(T, D)) for c in range(NCORES)]
    return d, shards



class Sem:
    def __init__(self, nc, es, name):
        self.h = es.enter_context(nc.semaphore(name))
        self.v = 0

    def inc(self, instr, k=1):
        instr.then_inc(self.h, k)
        self.v += k
        return (self, self.v)


def _flat(evs, out):
    for ev in evs:
        if ev is None:
            continue
        if isinstance(ev, list):
            _flat(ev, out)
        else:
            out.append(ev)


def _wait_all(eng, evs):
    flat = []
    _flat(list(evs), flat)
    best = {}
    for s, v in flat:
        cur = best.get(id(s))
        if cur is None or cur[1] < v:
            best[id(s)] = (s, v)
    for s, v in best.values():
        eng.wait_ge(s.h, v)


class KB:
    def __init__(self, upto=4):
        self.upto = upto
        nc = bass.Bass("TRN2", target_bir_lowering=False)
        self.nc = nc
        self.es = ExitStack()
        es = self.es

        def din(name, shape, dt=F32):
            return nc.dram_tensor(name, shape, dt, kind="ExternalInput").ap()

        self.x_in = din("x_in", [T, D])
        self.wq_t = din("wq_t", [DCH, 128, DCH, 128])
        self.wk_t = din("wk_t", [DCH, 128, DCH, 128])
        self.wv_t = din("wv_t", [DCH, 128, DCH, 128])
        self.wo_t = din("wo_t", [DCH, 128, DCH, 128])
        self.bq_c = din("bq_c", [128, DCH])
        self.bk_c = din("bk_c", [128, DCH])
        self.bv_c = din("bv_c", [128, DCH])
        self.bo_c = din("bo_c", [128, DCH])
        self.lng_b = din("lng_b", [128, D])
        self.lnb_b = din("lnb_b", [128, D])
        self.wgu = din("wgu", [2 * H_FFN, D])
        self.wp = din("wp", [D, H_FFN])
        self.wr1_t = din("wr1_t", [DCH, 128, DCH, 128])
        self.wk1_t = din("wk1_t", [DCH, 128, DCH, 128])
        self.wv1_t = din("wv1_t", [DCH, 128, DCH, 128])
        self.wo1_t = din("wo1_t", [DCH, 128, DCH, 128])
        self.decay_c = din("decay_c", [128, DCH])
        self.gnw_c = din("gnw_c", [128, DCH])
        self.gnb_c = din("gnb_c", [128, DCH])
        self.gw_t = din("gw_t", [HCH, 128, DCH, 128], I32)
        self.uw_t = din("uw_t", [HCH, 128, DCH, 128], I32)
        self.dw_t = din("dw_t", [DCH, 128, HCH, 128], I32)
        self.gs_c = din("gs_c", [128, HCH])
        self.us_c = din("us_c", [128, HCH])
        self.ds_c = din("ds_c", [128, DCH])
        self.cosf = din("cosf", [128, S])
        self.sinf = din("sinf", [128, S])
        self.ident = din("ident", [128, 128])
        self.mask = din("mask", [128, 128])
        self.iota = din("iota", [128, S])

        self.out = nc.dram_tensor("out", [T, D], F32, kind="ExternalOutput").ap()

        def scr(name, shape, dt=F32):
            return nc.dram_tensor(name, shape, dt, kind="Internal").ap()

        self.x1_s = scr("x1_s", [T, D])
        self.x2_s = scr("x2_s", [T, D])
        self.x3_s = scr("x3_s", [T, D])
        self.y_s = scr("y_s", [T, D])
        self.q_s = scr("q_s", [BPC, NH, 128, S])
        self.k_s = scr("k_s", [BPC, NH, 128, S])
        self.v_s = scr("v_s", [BPC, NH, 128, S // 128, 128])
        self.o_s = scr("o_s", [BPC, NH, 128, S])
        self.att_s = scr("att_s", [BPC, DCH, 128, S])
        self.wgub_s = scr("wgub_s", [2 * H_FFN, D], BF16)
        self.wpb_s = scr("wpb_s", [D, H_FFN], BF16)
        self.wgut_s = scr("wgut_s", [2 * H_FFN // 512, 128, DCH, 512], BF16)
        self.wpt_s = scr("wpt_s", [DCH, 128, HCH, 128], BF16)
        self.gwb_s = scr("gwb_s", [HCH, 128, DCH, 128], BF16)
        self.uwb_s = scr("uwb_s", [HCH, 128, DCH, 128], BF16)
        self.dwb_s = scr("dwb_s", [DCH, 128, HCH, 128], BF16)

        self.sW = Sem(nc, es, "sW")
        self.sA = Sem(nc, es, "sA")
        self.sG = Sem(nc, es, "sG")
        self.sPE = Sem(nc, es, "sPE")
        self.sDV = Sem(nc, es, "sDV")
        self.sAC = Sem(nc, es, "sAC")
        self.sPO = Sem(nc, es, "sPO")
        self.dsem = {"sync": self.sW, "scalar": self.sA, "gpsimd": self.sG}
        self.esem = {"tensor": self.sPE, "vector": self.sDV, "scalar": self.sAC,
                     "gpsimd": self.sPO, "sync": self.sW}

        self.ps_big = es.enter_context(nc.psum_tensor("ps_big", [128, 2048], F32))
        self.PB = [self.ps_big[:, 512 * i:512 * (i + 1)] for i in range(4)]
        self.p4 = es.enter_context(nc.psum_tensor("p4", [128, 512], F32))
        self.p5 = es.enter_context(nc.psum_tensor("p5", [128, 512], F32))
        self.p6 = es.enter_context(nc.psum_tensor("p6", [128, 512], F32))
        self.p7 = es.enter_context(nc.psum_tensor("p7", [128, 512], F32))

        def sbt(name, shape, dt=F32):
            return es.enter_context(nc.sbuf_tensor(name, shape, dt))

        self.sbt = sbt
        self.c_ident = sbt("c_ident", [128, 128])
        self.c_mask = sbt("c_mask", [128, 128])
        self.c_bq = sbt("c_bq", [128, DCH])
        self.c_bk = sbt("c_bk", [128, DCH])
        self.c_bv = sbt("c_bv", [128, DCH])
        self.c_bo = sbt("c_bo", [128, DCH])
        self.c_gnw = sbt("c_gnw", [128, DCH])
        self.c_gnb = sbt("c_gnb", [128, DCH])
        self.c_gs = sbt("c_gs", [128, HCH])
        self.c_us = sbt("c_us", [128, HCH])
        self.c_ds = sbt("c_ds", [128, DCH])
        self.c_logd = sbt("c_logd", [128, DCH])
        self.c_ones1 = sbt("c_ones1", [128, 1])
        self.c_ones = sbt("c_ones", [128, 512])
        self.c_eps = sbt("c_eps", [128, 2])
        self.c_dummy = sbt("c_dummy", [1, 1])
        self.ln_cols = sbt("ln_cols", [128, 16])
        self.ln_slot = 0
        self.ln_prev = [None, None]

        self.ldw_patch = []
        self.block = es.enter_context(nc.Block())

    def on(self, engname, emit, wait=(), inc=1):
        out = {}
        sem = self.esem[engname]

        def f(eng):
            _wait_all(eng, wait)
            ins = emit(eng)
            out["ev"] = sem.inc(ins, inc)

        getattr(self.block, engname)(f)
        return out["ev"]

    def dma(self, q, out_ap, in_ap, wait=(), **kw):
        sem = self.dsem[q]
        res = {}

        def f(eng):
            _wait_all(eng, wait)
            res["ev"] = sem.inc(eng.dma_start(out=out_ap, in_=in_ap, **kw), 16)

        getattr(self.block, q)(f)
        return res["ev"]

    def dma_multi(self, q, pairs, wait=(), transpose=False):
        sem = self.dsem[q]
        res = {}

        def f(eng):
            _wait_all(eng, wait)
            for out_ap, in_ap in pairs:
                if transpose:
                    ins = eng.dma_start_transpose(out=out_ap, in_=in_ap)
                else:
                    ins = eng.dma_start(out=out_ap, in_=in_ap)
                res["ev"] = sem.inc(ins, 16)

        getattr(self.block, q)(f)
        return res["ev"]

    def barrier_ev(self):
        return [(s, s.v) for s in (self.sW, self.sA, self.sG, self.sPE,
                                   self.sDV, self.sAC, self.sPO) if s.v > 0]

    def aid(self, ac, out, in_):
        ac.activation(out=self.c_dummy[:], in_=self.c_eps[0:1, 0:1], func=AF.Exp)
        return ac.activation(out=out, in_=in_, func=AF.Identity, scale=1.0)

    def pin(self, ac, func=None):
        ac.activation(out=self.c_dummy[:], in_=self.c_eps[0:1, 0:1],
                      func=AF.Exp if func is None else func)

    def ln(self, x_ap, y_ap, wait, n=D, eps=EPS_LN):
        slot = self.ln_slot
        self.ln_slot ^= 1
        c = self.ln_cols[:, 8 * slot:8 * (slot + 1)]
        su, negmu, ssq, var0, mu2, rstd, nmr, lnv = (c[:, i:i + 1] for i in range(8))
        dump = y_ap
        inv_n = 1.0 / n
        w0 = list(wait) + ([self.ln_prev[slot]] if self.ln_prev[slot] else [])

        ev_su = self.on("vector", lambda dv: dv.tensor_reduce(
            out=su, in_=x_ap, axis=AX.X, op=ALU.add), wait=w0)

        def e_negmu(ac):
            self.pin(ac)
            ac.activation(out=negmu, in_=su, func=AF.Identity, scale=-inv_n)
            return ac.activation(out=dump, in_=x_ap, func=AF.Square, accum_out=ssq)
        ev_negmu = self.on("scalar", e_negmu, wait=w0 + [ev_su])

        ev_nv = self.on("vector", lambda dv: dv.scalar_tensor_tensor(
            out=var0, in0=negmu, scalar=-1.0, in1=negmu,
            op0=ALU.mult, op1=ALU.mult), wait=[ev_negmu])

        def e3(ac):
            ac.activation(out=mu2, in_=ssq, func=AF.Identity, scale=inv_n, bias=var0)
            ac.activation(out=lnv, in_=mu2, func=AF.Ln, bias=self.c_eps[:, 0:1] if eps == EPS_LN else self.c_eps[:, 1:2])
            return ac.activation(out=rstd, in_=lnv, func=AF.Exp, scale=-0.5)
        ev_rstd = self.on("scalar", e3, wait=[ev_nv])

        ev_nmr = self.on("vector", lambda dv: dv.tensor_tensor(
            out=nmr, in0=negmu, in1=rstd, op=ALU.mult), wait=[ev_rstd])

        def e_y(ac):
            self.pin(ac)
            return ac.activation(out=y_ap, in_=x_ap, func=AF.Identity,
                                 scale=rstd, bias=nmr)
        ev_done = self.on("scalar", e_y, wait=[ev_nmr])
        self.ln_prev[slot] = ev_done
        return ev_done

    def transpose_mac(self, src_subs, src_evs, dst, dst_cols, pp, pp_free):
        evs = []
        nch = len(dst_cols)
        for j in range(nch):
            slot = j % 2
            pb = pp[slot]

            def em(pe, j=j, pb=pb):
                for s in range(NSUB):
                    ins = pe.transpose(pb[:, 128 * s:128 * (s + 1)],
                                       src_subs[s][:, 128 * j:128 * (j + 1)], self.c_ident[:])
                return ins
            ev_tp = self.on("tensor", em, wait=list(src_evs) + [pp_free.get(slot)])

            eng = "vector" if j % 2 == 0 else "scalar"
            if eng == "vector":
                ev_ev = self.on(eng, lambda dv, j=j, pb=pb: dv.tensor_copy(
                    out=dst[:, dst_cols[j]:dst_cols[j] + 512], in_=pb[:]), wait=[ev_tp])
            else:
                ev_ev = self.on(eng, lambda ac, j=j, pb=pb: self.aid(
                    ac, dst[:, dst_cols[j]:dst_cols[j] + 512], pb[:]), wait=[ev_tp])
            pp_free[slot] = ev_ev
            evs.append(ev_ev)
        return evs


def emit_prologue(kb: KB):
    loads = [
        (kb.c_ident, kb.ident), (kb.c_mask, kb.mask),
        (kb.c_bq, kb.bq_c), (kb.c_bk, kb.bk_c), (kb.c_bv, kb.bv_c),
        (kb.c_bo, kb.bo_c), (kb.c_gnw, kb.gnw_c), (kb.c_gnb, kb.gnb_c),
        (kb.c_gs, kb.gs_c), (kb.c_us, kb.us_c), (kb.c_ds, kb.ds_c),
    ]
    kb.c_dect = kb.sbt("c_dect", [128, DCH])
    ev_ld = kb.dma_multi("sync", [(d[:], s[:]) for d, s in loads])
    ev_dec = kb.dma("sync", kb.c_dect[:], kb.decay_c[:])

    def e_ones(dv):
        dv.memset(kb.c_ones1[:], 1.0)
        dv.memset(kb.c_eps[:, 0:1], EPS_LN)
        dv.memset(kb.c_eps[:, 1:2], EPS_RMS)
        return dv.memset(kb.c_ones[:], 1.0)
    ev_ones = kb.on("vector", e_ones)

    ev_sig = kb.on("scalar", lambda ac: ac.activation(
        out=kb.c_dect[:], in_=kb.c_dect[:], func=AF.Sigmoid), wait=[ev_dec])
    ev_clip = kb.on("vector", lambda dv: dv.tensor_scalar_max(
        kb.c_dect[:], kb.c_dect[:], 1e-7), wait=[ev_sig])
    ev_logd = kb.on("scalar", lambda ac: ac.activation(
        out=kb.c_logd[:], in_=kb.c_dect[:], func=AF.Ln), wait=[ev_clip])

    kb.ev_consts = ev_ld
    kb.ev_logd = ev_logd
    kb.ev_ones = ev_ones


def emit_sb1(kb: KB, qs=None):
    nc = kb.nc
    entry = [kb.ev_consts, kb.ev_ones]

    with ExitStack() as st:
        def sbt(n, sh, dt=F32):
            return st.enter_context(nc.sbuf_tensor(n, sh, dt))
        kb.c_cos = sbt("c_cos", [128, S])
        kb.c_sin = sbt("c_sin", [128, S])
        kb.c_lng = sbt("c_lng", [128, D])
        kb.c_lnb = sbt("c_lnb", [128, D])
        ev_c1 = kb.dma_multi("sync", [(kb.c_cos[:], kb.cosf[:]), (kb.c_sin[:], kb.sinf[:]),
                                      (kb.c_lng[:], kb.lng_b[:]), (kb.c_lnb[:], kb.lnb_b[:])])
        entry = entry + [ev_c1]
        z_mac = sbt("z1_mac", [128, NSUB * D])
        zT = sbt("z1T", [128, DCH * 512])
        xbuf = [sbt(f"x1ld{i}", [128, D]) for i in range(2)]
        wbuf = [sbt(f"w1buf{i}", [128, DCH * 128]) for i in range(3)]
        stg = [sbt(f"stg1_{i}", [128, 512]) for i in range(2)]
        rop = [sbt(f"rop1_{i}", [128, 512]) for i in range(2)]
        rtmp = sbt("rtmp1", [128, 512])
        ru = sbt("ru1", [128, 512])
        vblk = [sbt(f"vblk1_{i}", [128, 512]) for i in range(2)]

        xbuf_free = [entry, entry]
        wbuf_free = [entry, entry, entry]
        pbank_free = [None, None]
        stg_free = [entry, entry]
        rop_free = [entry, entry]
        vblk_free = [entry, entry]
        vtp_free = [None, None]
        tpp_free = {}
        prev_tp_last = None

        for m in range(NMAC):
            b, tw = divmod(m, MPB)
            t0 = tw * MAC
            r0 = b * S + t0
            aff_evs = []
            for s in range(NSUB):
                buf = xbuf[s % 2]
                ev_x = kb.dma("scalar", buf[:], kb.x_in[r0 + 128 * s:r0 + 128 * (s + 1), :],
                              wait=[xbuf_free[s % 2]])
                yv = z_mac[:, D * s:D * (s + 1)]
                ev_ln = kb.ln(buf[:], yv, [ev_x, prev_tp_last])
                ev_ys = kb.dma("gpsimd", kb.y_s[r0 + 128 * s:r0 + 128 * (s + 1), :], yv,
                               wait=[ev_ln])
                xbuf_free[s % 2] = ev_ln

                def e_aff(dv, yv=yv):
                    dv.tensor_tensor(out=yv, in0=yv, in1=kb.c_lng[:], op=ALU.mult)
                    return dv.tensor_tensor(out=yv, in0=yv, in1=kb.c_lnb[:], op=ALU.add)
                aff_evs.append(kb.on("vector", e_aff, wait=[ev_ys]))

            src_subs = [z_mac[:, D * s:D * (s + 1)] for s in range(NSUB)]
            tp_evs = kb.transpose_mac(src_subs, aff_evs, zT,
                                      [512 * j for j in range(DCH)],
                                      (kb.PB[2], kb.PB[3]), tpp_free)
            zT_ready = tp_evs

            projs = [("q", kb.wq_t, kb.c_bq, kb.q_s), ("k", kb.wk_t, kb.c_bk, kb.k_s),
                     ("v", kb.wv_t, kb.c_bv, None)]
            for pi, (pname, wt, bias_c, dst_s) in enumerate(projs):
                for i in range(DCH):
                    g = pi * DCH + i
                    ws = g % 3
                    pb = kb.PB[g % 2]
                    ev_w = kb.dma("sync", wbuf[ws][:], wt[i].flatten_outer_dims(),
                                  wait=[wbuf_free[ws]])

                    def e_mm(pe, ws=ws, pb=pb):
                        for j in range(DCH):
                            ins = pe.matmul(pb, wbuf[ws][:, 128 * j:128 * (j + 1)],
                                            zT[:, 512 * j:512 * (j + 1)],
                                            start=(j == 0), stop=(j == DCH - 1))
                        return ins
                    ev_mm = kb.on("tensor", e_mm,
                                  wait=[ev_w, pbank_free[g % 2]] + list(zT_ready))
                    wbuf_free[ws] = ev_mm
                    prev_tp_last = ev_mm

                    sl = g % 2
                    if pname in ("q", "k"):
                        def e_ev(ac, pb=pb, sl=sl, i=i, bias_c=bias_c):
                            kb.pin(ac)
                            return ac.activation(out=stg[sl][:], in_=pb, func=AF.Identity,
                                                 bias=bias_c[:, i:i + 1])
                        ev_evac = kb.on("scalar", e_ev, wait=[ev_mm, stg_free[sl]])
                        pbank_free[g % 2] = ev_evac

                        def e_rope(dv, sl=sl, t0=t0):
                            cs = kb.c_cos[:, t0:t0 + 512]
                            sn = kb.c_sin[:, t0:t0 + 512]
                            dv.tensor_tensor(out=ru[:], in0=stg[sl][:], in1=sn, op=ALU.mult)
                            dv.tensor_copy(out=rtmp[0:64, :], in_=ru[64:128, :])
                            dv.tensor_copy(out=rtmp[64:128, :], in_=ru[0:64, :])
                            dv.tensor_tensor(out=rop[sl][:], in0=stg[sl][:], in1=cs, op=ALU.mult)
                            return dv.tensor_tensor(out=rop[sl][:], in0=rop[sl][:], in1=rtmp[:],
                                                    op=ALU.add)
                        ev_rope = kb.on("vector", e_rope, wait=[ev_evac, rop_free[sl]])
                        stg_free[sl] = ev_rope
                        ev_dma = kb.dma("gpsimd", dst_s[b, i, :, t0:t0 + 512], rop[sl][:],
                                        wait=[ev_rope])
                        rop_free[sl] = ev_dma
                    else:
                        def e_ev(ac, pb=pb, sl=sl, i=i):
                            kb.pin(ac)
                            return ac.activation(out=stg[sl][:], in_=pb, func=AF.Identity,
                                                 bias=kb.c_bv[:, i:i + 1])
                        ev_evac = kb.on("scalar", e_ev, wait=[ev_mm, stg_free[sl]])
                        pbank_free[g % 2] = ev_evac

                        pv = (kb.p4, kb.p5)[sl]

                        def e_vtp(pe, sl=sl, pv=pv):
                            for s in range(NSUB):
                                ins = pe.transpose(pv[:, 128 * s:128 * (s + 1)],
                                                   stg[sl][:, 128 * s:128 * (s + 1)],
                                                   kb.c_ident[:])
                            return ins
                        ev_vtp = kb.on("tensor", e_vtp, wait=[ev_evac, vtp_free[sl]])
                        stg_free[sl] = ev_vtp
                        ev_vcp = kb.on("vector", lambda dv, sl=sl, pv=pv: dv.tensor_copy(
                            out=vblk[sl][:], in_=pv[:]), wait=[ev_vtp, vblk_free[sl]])
                        vtp_free[sl] = ev_vcp
                        ev_vdma = kb.dma_multi("gpsimd", [
                            (kb.v_s[b, i, :, NSUB * tw + s, :], vblk[sl][:, 128 * s:128 * (s + 1)])
                            for s in range(NSUB)], wait=[ev_vcp])
                        vblk_free[sl] = ev_vdma

    barrier1 = kb.barrier_ev()

    with ExitStack() as st:
        def sbt(n, sh, dt=F32):
            return st.enter_context(nc.sbuf_tensor(n, sh, dt))
        qq2 = [sbt(f"qq1_{i}", [128, S]) for i in range(2)]
        kk2 = [sbt(f"kk1_{i}", [128, S]) for i in range(2)]
        vv2 = [sbt(f"vv1_{i}", [128, S]) for i in range(2)]
        p_sb = sbt("p_sb1", [128, S])
        pt_sb = [sbt(f"pt1_{i}", [128, 512]) for i in range(2)]
        o_stage = [sbt(f"ostg1_{i}", [128, 128]) for i in range(2)]
        o_tq = sbt("otq1", [128, 128])
        acols = sbt("acols1", [128, 5])
        mxc, nmx, rsc = (acols[:, i:i + 1] for i in range(3))
        rrc2 = [acols[:, 3 + i:4 + i] for i in range(2)]

        ld_free = [barrier1, barrier1]
        otq_free = barrier1
        otp_hist = [barrier1, barrier1]
        oev_hist = [None, None]
        ostg_free = [barrier1, barrier1]
        for b in range(BPC):
            for h in range(NH):
                hsl = (b * NH + h) % 2
                qq, kk, vv = qq2[hsl], kk2[hsl], vv2[hsl]
                ev_ld = kb.dma_multi("sync", [
                    (qq[:], kb.q_s[b, h].flatten_outer_dims()),
                    (kk[:], kb.k_s[b, h].flatten_outer_dims()),
                    (vv[:], kb.v_s[b, h].flatten_outer_dims()),
                ], wait=[ld_free[hsl]])
                if qs is not None:
                    qs.emit_next((len(qs.jobs) + BPC * NH - 1) // (BPC * NH))

                ev_scores_free = None
                ev_psb_free = barrier1
                prev_oev = None
                ev_rr = None
                ptp_free = {0: barrier1, 1: barrier1}
                ptsb_free = {0: barrier1, 1: barrier1}
                for i in range(S // 128):
                    ncols = 128 * (i + 1)

                    def e_sc(pe, i=i, ncols=ncols):
                        c0 = 0
                        while c0 < ncols:
                            nn = min(512, ncols - c0)
                            ins = pe.matmul(kb.ps_big[:, c0:c0 + nn],
                                            qq[:, 128 * i:128 * (i + 1)],
                                            kk[:, c0:c0 + nn], start=True, stop=True)
                            c0 += nn
                        return ins
                    ev_sc = kb.on("tensor", e_sc, wait=[ev_ld, ev_scores_free])

                    def e_mx(dv, i=i, ncols=ncols):
                        dv.tensor_tensor(out=kb.ps_big[:, 128 * i:128 * (i + 1)],
                                         in0=kb.ps_big[:, 128 * i:128 * (i + 1)],
                                         in1=kb.c_mask[:], op=ALU.add)
                        return dv.tensor_reduce(out=mxc, in_=kb.ps_big[:, 0:ncols],
                                                axis=AX.X, op=ALU.max)
                    ev_mx = kb.on("vector", e_mx, wait=[ev_sc])

                    ev_nmx = kb.on("gpsimd", lambda po: po.tensor_scalar_mul(
                        nmx, mxc, -SM_SCALE), wait=[ev_mx])

                    ev_exp = kb.on("scalar", lambda ac, ncols=ncols: ac.activation(
                        out=p_sb[:, 0:ncols], in_=kb.ps_big[:, 0:ncols], func=AF.Exp,
                        scale=SM_SCALE, bias=nmx, accum_out=rsc),
                        wait=[ev_nmx, ev_psb_free, ev_rr])
                    ev_scores_free = ev_exp

                    rcol = rrc2[i % 2]
                    ev_rr = kb.on("vector", lambda dv, rcol=rcol: dv.reciprocal(
                        out=rcol, in_=rsc), wait=[ev_exp, oev_hist[i % 2]])

                    ngrp = (i + 4) // 4
                    last_tp = None
                    ev_av = None
                    for gidx in range(ngrp):
                        k0 = 4 * gidx
                        kn = min(4, i + 1 - k0)
                        sl = gidx % 2
                        pv = (kb.p4, kb.p5)[sl]

                        def e_tp(pe, k0=k0, kn=kn, pv=pv):
                            for t in range(kn):
                                ins = pe.transpose(pv[:, 128 * t:128 * (t + 1)],
                                                   p_sb[:, 128 * (k0 + t):128 * (k0 + t + 1)],
                                                   kb.c_ident[:])
                            return ins
                        ev_tp = kb.on("tensor", e_tp, wait=[ev_exp, ptp_free[sl]])
                        last_tp = ev_tp

                        ev_cp = kb.on("vector", lambda dv, sl=sl, pv=pv, kn=kn:
                                      dv.tensor_copy(out=pt_sb[sl][:, 0:128 * kn],
                                                     in_=pv[:, 0:128 * kn]),
                                      wait=[ev_tp, ptsb_free[sl]])
                        ptp_free[sl] = ev_cp

                        def e_av(pe, k0=k0, kn=kn, sl=sl, i=i):
                            for t in range(kn):
                                kc = k0 + t
                                ins = pe.matmul(kb.p6[:, 0:128],
                                                pt_sb[sl][:, 128 * t:128 * (t + 1)],
                                                vv[:, 128 * kc:128 * (kc + 1)],
                                                start=(kc == 0), stop=(kc == i))
                            return ins
                        ev_av = kb.on("tensor", e_av,
                                      wait=[ev_cp] + ([prev_oev] if gidx == 0 else []))
                        ptsb_free[sl] = ev_av

                    ev_psb_free = last_tp
                    osl = i % 2

                    def e_oev(ac, osl=osl, rcol=rcol):
                        kb.pin(ac)
                        return ac.activation(out=o_tq[:], in_=kb.p6[:, 0:128],
                                             func=AF.Identity, scale=rcol)
                    ev_oev = kb.on("scalar", e_oev,
                                   wait=[ev_av, otq_free, ev_rr])
                    prev_oev = ev_oev
                    oev_hist[i % 2] = ev_oev

                    ev_otp = kb.on("tensor", lambda pe, osl=osl: pe.transpose(
                        kb.p7[:, 128 * osl:128 * (osl + 1)], o_tq[:], kb.c_ident[:]),
                        wait=[ev_oev, otp_hist[osl]])
                    otq_free = ev_otp

                    ev_oev2 = kb.on("vector", lambda dv, osl=osl: dv.tensor_copy(
                        out=o_stage[osl][:], in_=kb.p7[:, 128 * osl:128 * (osl + 1)]),
                        wait=[ev_otp, ostg_free[osl]])
                    otp_hist[osl] = ev_oev2
                    ev_odma = kb.dma("gpsimd", kb.o_s[b, h, :, 128 * i:128 * (i + 1)],
                                     o_stage[osl][:], wait=[ev_oev2])
                    ostg_free[osl] = ev_odma
                ld_free[hsl] = ev_odma

    barrier2 = kb.barrier_ev()

    with ExitStack() as st:
        def sbt(n, sh, dt=F32):
            return st.enter_context(nc.sbuf_tensor(n, sh, dt))
        obuf = sbt("obuf1", [128, DCH * 512])
        att_tm = sbt("att1_tm", [128, NSUB * D])
        wbuf = [sbt(f"w1cbuf{i}", [128, DCH * 128]) for i in range(2)]
        wo_stage = [sbt(f"wostg{i}", [128, 512]) for i in range(2)]
        ybuf = sbt("ybuf1", [128, D])
        x0buf = sbt("x0buf1", [128, D])

        wbuf_free = [barrier2, barrier2]
        pbank_free = [barrier2, barrier2]
        wostg_free = [barrier2, barrier2]
        tpp5_free = {0: barrier2, 1: barrier2}
        obuf_free = barrier2
        att_free = barrier2
        epi_prev = barrier2
        x1_evs = []

        for m in range(NMAC):
            b, tw = divmod(m, MPB)
            t0 = tw * MAC
            r0 = b * S + t0
            ev_old = kb.dma_multi("sync", [
                (obuf[:, 512 * hc:512 * (hc + 1)], kb.o_s[b, hc, :, t0:t0 + 512])
                for hc in range(DCH)], wait=[obuf_free])

            att_done = []
            for i in range(DCH):
                ws = i % 2
                pb = kb.PB[i % 2]
                ev_w = kb.dma("sync", wbuf[ws][:], kb.wo_t[i].flatten_outer_dims(),
                              wait=[wbuf_free[ws]])

                def e_mm(pe, ws=ws, pb=pb):
                    for j in range(DCH):
                        ins = pe.matmul(pb, wbuf[ws][:, 128 * j:128 * (j + 1)],
                                        obuf[:, 512 * j:512 * (j + 1)],
                                        start=(j == 0), stop=(j == DCH - 1))
                    return ins
                ev_mm = kb.on("tensor", e_mm, wait=[ev_w, ev_old, pbank_free[i % 2]])
                wbuf_free[ws] = ev_mm

                sl = i % 2
                ev_evac = kb.on("scalar", lambda ac, pb=pb, i=i, sl=sl: ac.activation(
                    out=wo_stage[sl][:], in_=pb, func=AF.Identity, bias=kb.c_bo[:, i:i + 1]),
                    wait=[ev_mm, wostg_free[sl]])
                pbank_free[i % 2] = ev_evac

                pv = (kb.p4, kb.p5)[sl]

                def e_tp(pe, sl=sl, pv=pv):
                    for s in range(NSUB):
                        ins = pe.transpose(pv[:, 128 * s:128 * (s + 1)],
                                           wo_stage[sl][:, 128 * s:128 * (s + 1)],
                                           kb.c_ident[:])
                    return ins
                ev_tp = kb.on("tensor", e_tp, wait=[ev_evac, tpp5_free[sl], att_free])
                wostg_free[sl] = ev_tp

                def e_cp(dv, sl=sl, pv=pv, i=i):
                    for s in range(NSUB):
                        ins = dv.tensor_copy(
                            out=att_tm[:, D * s + 128 * i:D * s + 128 * (i + 1)],
                            in_=pv[:, 128 * s:128 * (s + 1)])
                    return ins
                ev_cp = kb.on("vector", e_cp, wait=[ev_tp])
                tpp5_free[sl] = ev_cp
                att_done.append(ev_cp)
            obuf_free = ev_mm

            for s in range(NSUB):
                ev_yld = kb.dma_multi("scalar", [
                    (ybuf[:], kb.y_s[r0 + 128 * s:r0 + 128 * (s + 1), :]),
                    (x0buf[:], kb.x_in[r0 + 128 * s:r0 + 128 * (s + 1), :]),
                ], wait=[epi_prev])
                av = att_tm[:, D * s:D * (s + 1)]
                ev_fout = kb.on("vector", lambda dv, av=av: dv.tensor_tensor(
                    out=av, in0=av, in1=ybuf[:], op=ALU.add), wait=[ev_yld] + att_done)
                ev_ln = kb.ln(av, ybuf[:], [ev_fout])
                ev_x1 = kb.on("vector", lambda dv: dv.tensor_tensor(
                    out=x0buf[:], in0=x0buf[:], in1=ybuf[:], op=ALU.add), wait=[ev_ln])
                ev_st = kb.dma("scalar", kb.x1_s[r0 + 128 * s:r0 + 128 * (s + 1), :],
                               x0buf[:], wait=[ev_x1])
                epi_prev = ev_st
                x1_evs.append(ev_st)
            att_free = x1_evs[-1]

    return kb.x1_s, [x1_evs[-1]]


class QuantStream:

    def __init__(self, kb, st):
        nc = kb.nc
        self.kb = kb

        def sbt(n, sh, dt=F32):
            return st.enter_context(nc.sbuf_tensor(n, sh, dt))

        self.wt_in = [sbt(f"qw_in{i}", [128, D]) for i in range(2)]
        self.wb = sbt("qw_wb", [128, D], BF16)
        self.quot = sbt("qw_quot", [128, D], BF16)
        self.rq = sbt("qw_rq", [128, D])
        self.what = sbt("qw_hat", [128, D], BF16)
        qc = sbt("qw_cols", [128, 160])
        self.asum = qc[:, 0:32]
        self.sbf = qc[:, 32:64].bitcast(BF16)[:, 0:32]
        self.smax = qc[:, 64:96]
        self.thr = qc[:, 96:128]
        self.nthr = qc[:, 128:160]
        self.cv = [sbt(f"cv_in{i}", [128, D], BF16) for i in range(2)]
        self.tb = [sbt(f"qt_tb{i}", [128, 512], BF16) for i in range(2)]

        qw_jobs, qp_jobs, t_jobs, t4_jobs, c_jobs = [], [], [], [], []
        for rt in range(2 * H_FFN // 128):
            qw_jobs.append(("q", kb.wgu[128 * rt:128 * (rt + 1), :],
                            kb.wgub_s[128 * rt:128 * (rt + 1), :]))
        for rt in range(DCH):
            for cs in range(4):
                qp_jobs.append(("q", kb.wp[128 * rt:128 * (rt + 1), 2048 * cs:2048 * (cs + 1)],
                                kb.wpb_s[128 * rt:128 * (rt + 1), 2048 * cs:2048 * (cs + 1)]))
        for g in range(2 * H_FFN // 512):
            for j in range(DCH):
                t_jobs.append(("t", kb.wgub_s[512 * g:512 * (g + 1), 128 * j:128 * (j + 1)],
                               kb.wgut_s[g, :, j, :]))
        for dg in range(4):
            for jw in range(HCH):
                t4_jobs.append(("t4", kb.wpb_s[512 * dg:512 * (dg + 1), 128 * jw:128 * (jw + 1)],
                                (jw, dg)))
        for ic in range(HCH):
            c_jobs.append(("c", kb.gw_t[ic].rearrange("p j c -> p (j c)"),
                           kb.gwb_s[ic].rearrange("p j c -> p (j c)")))
            c_jobs.append(("c", kb.uw_t[ic].rearrange("p j c -> p (j c)"),
                           kb.uwb_s[ic].rearrange("p j c -> p (j c)")))
        for dc in range(DCH):
            for hq in range(4):
                c_jobs.append(("c", kb.dw_t[dc].rearrange("p j c -> p (j c)")[:, 2048 * hq:2048 * (hq + 1)],
                               kb.dwb_s[dc].rearrange("p j c -> p (j c)")[:, 2048 * hq:2048 * (hq + 1)]))
        self.jobs = []
        nqw = nqp = pt = pt4 = pc = 0
        for k in range(32):
            for kind, qn in (("qw", 4), ("qp", 2), ("c", 6), ("t", 16), ("t4", 8)):
                for _ in range(qn):
                    if kind == "qw" and nqw < len(qw_jobs):
                        self.jobs.append(qw_jobs[nqw]); nqw += 1
                    elif kind == "qp" and nqp < len(qp_jobs):
                        self.jobs.append(qp_jobs[nqp]); nqp += 1
                    elif kind == "c" and pc < len(c_jobs):
                        self.jobs.append(c_jobs[pc]); pc += 1
                    elif kind == "t" and pt < len(t_jobs) and nqw >= 4 * (pt // 16) + 4:
                        self.jobs.append(t_jobs[pt]); pt += 1
                    elif kind == "t4" and pt4 < len(t4_jobs) and nqp >= 16 * (pt4 // 64) + 16:
                        self.jobs.append(t4_jobs[pt4]); pt4 += 1
        self.jobs += t_jobs[pt:] + t4_jobs[pt4:] + c_jobs[pc:]
        assert len(self.jobs) == (len(qw_jobs) + len(qp_jobs) + len(t_jobs)
                                  + len(t4_jobs) + len(c_jobs))
        self.idx = 0
        self.in_free = [None, None]
        self.cv_free = [None, None]
        self.pipe_free = None
        self.ev_done = None

    def emit_next(self, n):
        kb = self.kb
        end = min(self.idx + n, len(self.jobs))
        while self.idx < end:
            idx = self.idx
            kind, srcap, dstap = self.jobs[idx]
            self.idx += 1
            if kind == "q":
                sl = idx % 2
                wt_in, wb, quot, rq, what = self.wt_in, self.wb, self.quot, self.rq, self.what
                asum, sbf, smax = self.asum, self.sbf, self.smax
                thr, nthr = self.thr, self.nthr
                ev_in = kb.dma("sync", wt_in[sl][:], srcap, wait=[self.in_free[sl]])

                def e_p1(dv, sl=sl):
                    dv.tensor_copy(out=wb[:], in_=wt_in[sl][:])
                    return dv.tensor_reduce(out=asum,
                                            in_=wb[:].rearrange("p (g k) -> p g k", k=64),
                                            axis=AX.X, op=ALU.add, apply_absolute_value=True)
                ev_p1 = kb.on("vector", e_p1, wait=[ev_in, self.pipe_free])
                self.in_free[sl] = ev_p1

                ev_s1 = kb.on("scalar", lambda ac: ac.activation(
                    out=sbf, in_=asum, func=AF.Identity, scale=1.0 / 64.0), wait=[ev_p1])
                ev_s2 = kb.on("vector", lambda dv: dv.tensor_scalar_max(
                    smax, sbf, BF16_CLIP_LO), wait=[ev_s1])

                def e_thr(ac):
                    ac.activation(out=thr, in_=smax, func=AF.Identity, scale=BF16_HALF_UP)
                    return ac.activation(out=nthr, in_=smax, func=AF.Identity,
                                         scale=-BF16_HALF_UP)
                ev_s3 = kb.on("scalar", e_thr, wait=[ev_s2])

                def e_cmp(dv):
                    dv.tensor_tensor(out=rq[:].rearrange("p (g k) -> p g k", k=64),
                                     in0=wb[:].rearrange("p (g k) -> p g k", k=64),
                                     in1=thr.unsqueeze(2).broadcast_to([128, 32, 64]),
                                     op=ALU.is_gt)
                    return dv.tensor_tensor(out=quot[:].rearrange("p (g k) -> p g k", k=64),
                                            in0=wb[:].rearrange("p (g k) -> p g k", k=64),
                                            in1=nthr.unsqueeze(2).broadcast_to([128, 32, 64]),
                                            op=ALU.is_lt)
                ev_cmp = kb.on("vector", e_cmp, wait=[ev_s3])

                def e_p2(dv):
                    dv.tensor_tensor(out=rq[:], in0=rq[:], in1=quot[:], op=ALU.subtract)
                    return dv.tensor_tensor(out=what[:].rearrange("p (g k) -> p g k", k=64),
                                            in0=rq[:].rearrange("p (g k) -> p g k", k=64),
                                            in1=smax.unsqueeze(2).broadcast_to([128, 32, 64]),
                                            op=ALU.mult)
                ev_p2 = kb.on("vector", e_p2, wait=[ev_cmp])
                self.ev_done = kb.dma("sync", dstap, what[:], wait=[ev_p2])
                self.pipe_free = self.ev_done
            elif kind == "t":
                sl = idx % 2
                ev_in = kb.dma("sync", self.tb[sl][:], srcap,
                               wait=[self.cv_free[sl], self.pipe_free], transpose=True)
                self.ev_done = kb.dma("sync", dstap, self.tb[sl][:], wait=[ev_in])
                self.cv_free[sl] = self.ev_done
            elif kind == "t4":
                jw, dg = dstap
                sl = idx % 2
                ev_in = kb.dma("sync", self.tb[sl][:], srcap,
                               wait=[self.cv_free[sl], self.pipe_free], transpose=True)
                ev_o = None
                for q in range(4):
                    ev_o = kb.dma("sync", kb.wpt_s[4 * dg + q, :, jw, :],
                                  self.tb[sl][:, 128 * q:128 * (q + 1)],
                                  wait=[ev_in])
                self.ev_done = ev_o
                self.cv_free[sl] = ev_o
            else:
                sl = idx % 2
                cv = self.cv
                ev_in = kb.dma("gpsimd", cv[sl][:], srcap, wait=[self.cv_free[sl]])
                ev_sub = kb.on("vector", lambda dv, sl=sl: dv.tensor_scalar_add(
                    cv[sl][:], cv[sl][:], -128.0), wait=[ev_in])
                self.ev_done = kb.dma("gpsimd", dstap, cv[sl][:], wait=[ev_sub])
                self.cv_free[sl] = self.ev_done

    def finish(self):
        self.emit_next(len(self.jobs))
        return self.ev_done


def emit_sb2(kb: KB, x_src, in_evs):
    nc = kb.nc
    entry = kb.barrier_ev()

    with ExitStack() as stp:
        def sbtp(n, sh, dt=F32):
            return stp.enter_context(nc.sbuf_tensor(n, sh, dt))
        hhi = sbtp("hh2hi", [128, HCH * 512], BF16)
        hlo = sbtp("hh2lo", [128, HCH * 512], BF16)


        x2_evs = [entry]
        ffn_free = entry
        epi_prev = entry
        hh_free = entry
        p7_free = entry
        rb_free = entry
        prev_pe = entry

        for m in range(NMAC):
            b, tw = divmod(m, MPB)
            t0 = tw * MAC
            r0 = b * S + t0

            with ExitStack() as stA:
                def sbtA(n, sh, dt=F32):
                    return stA.enter_context(nc.sbuf_tensor(f"{n}_m{m}", sh, dt))
                yhi = sbtA("y2hi", [128, DCH * 512], BF16)
                ylo = sbtA("y2lo", [128, DCH * 512], BF16)
                stA1 = stA.enter_context(ExitStack())
                def sbtA1(n, sh, dt=F32):
                    return stA1.enter_context(nc.sbuf_tensor(f"{n}_m{m}", sh, dt))
                z_mac = sbtA1("z2", [128, 2 * D])
                xbuf = sbtA1("x2ld", [128, D])

                macro_gate = [epi_prev, prev_pe]
                tpp_free = {0: macro_gate, 1: macro_gate}
                xbuf_free = macro_gate
                last_lo = None
                z_free = [prev_pe, epi_prev]
                for sp in range(2):
                    ln_evs = []
                    for half in range(2):
                        s = 2 * sp + half
                        ev_x = kb.dma("scalar", xbuf[:],
                                      x_src[r0 + 128 * s:r0 + 128 * (s + 1), :],
                                      wait=[xbuf_free] + (in_evs if m == 0 and s == 0 else []))
                        ev_ln = kb.ln(xbuf[:], z_mac[:, D * half:D * (half + 1)],
                                      [ev_x] + z_free)
                        xbuf_free = ev_ln
                        ln_evs.append(ev_ln)
                    for j in range(DCH):
                        pv = (kb.PB[2], kb.PB[3])[j % 2]

                        def e_tp(pe, j=j, pv=pv):
                            for half in range(2):
                                ins = pe.transpose(pv[:, 128 * half:128 * (half + 1)],
                                                   z_mac[:, D * half + 128 * j:D * half + 128 * (j + 1)],
                                                   kb.c_ident[:])
                            return ins
                        ev_tp = kb.on("tensor", e_tp, wait=ln_evs + [tpp_free[j % 2]])

                        col = 512 * j + 256 * sp
                        ev_hi = kb.on("scalar", lambda ac, pv=pv, col=col: kb.aid(
                            ac, yhi[:, col:col + 256], pv[:, 0:256]), wait=[ev_tp])
                        ev_lo = kb.on("vector", lambda dv, pv=pv, col=col: dv.tensor_tensor(
                            out=ylo[:, col:col + 256], in0=pv[:, 0:256],
                            in1=yhi[:, col:col + 256], op=ALU.subtract), wait=[ev_hi])
                        tpp_free[j % 2] = ev_lo
                        last_lo = ev_lo
                        last_tp = ev_tp
                    z_free = [last_tp]
                y_ready = [last_lo]
                stA1.close()
                wgbuf = sbtA("wg2", [128, DCH * 512], BF16)
                wubuf = sbtA("wu2", [128, DCH * 512], BF16)
                sg = sbtA("sg2", [128, 512])
                hhf = sbtA("hhf2", [128, 512])
                hsq = sbtA("hsq2", [128, 512])

                wg_free = [last_tp] + macro_gate
                wu_free = [last_tp] + macro_gate
                sg_free = macro_gate
                hhf_free = macro_gate
                hsq_free = macro_gate
                pAB_free = {0: macro_gate, 1: macro_gate, 2: macro_gate, 3: macro_gate}
                ev_ssum = None
                for g in range(16):
                    ev_wg = kb.dma("sync", wgbuf[:], kb.wgut_s[g].rearrange("p j c -> p (j c)"),
                                   wait=[wg_free])
                    ev_wu = kb.dma("sync", wubuf[:], kb.wgut_s[16 + g].rearrange("p j c -> p (j c)"),
                                   wait=[wu_free])

                    for isub in range(4):
                        ic = 4 * g + isub
                        pg = kb.PB[2 * (ic % 2)]
                        pu = kb.PB[2 * (ic % 2) + 1]

                        def e_mm(pe, isub=isub, pg=pg, pu=pu):
                            for j in range(DCH):
                                wgt = wgbuf[:, 512 * j + 128 * isub:512 * j + 128 * (isub + 1)]
                                wut = wubuf[:, 512 * j + 128 * isub:512 * j + 128 * (isub + 1)]
                                pe.matmul(pg, wgt, yhi[:, 512 * j:512 * (j + 1)],
                                          start=(j == 0), stop=False)
                                kb.ldw_patch.append(pe.matmul(pg, wgt, ylo[:, 512 * j:512 * (j + 1)],
                                          start=False, stop=(j == DCH - 1)).ins)
                                pe.matmul(pu, wut, yhi[:, 512 * j:512 * (j + 1)],
                                          start=(j == 0), stop=False)
                                ins = pe.matmul(pu, wut, ylo[:, 512 * j:512 * (j + 1)],
                                                start=False, stop=(j == DCH - 1))
                                kb.ldw_patch.append(ins.ins)
                            return ins
                        ev_mm = kb.on("tensor", e_mm,
                                      wait=[ev_wg, ev_wu, pAB_free[2 * (ic % 2)],
                                            pAB_free[2 * (ic % 2) + 1]] + y_ready)
                        prev_pe = ev_mm
                        if isub == 3:
                            wg_free = ev_mm
                            wu_free = ev_mm

                        ev_sg = kb.on("scalar", lambda ac, pg=pg: ac.activation(
                            out=sg[:], in_=pg, func=AF.Silu), wait=[ev_mm, sg_free])
                        pAB_free[2 * (ic % 2)] = ev_sg

                        def e_fuse(dv, pu=pu):
                            dv.tensor_tensor(out=hhf[:], in0=sg[:], in1=pu, op=ALU.mult)
                            return dv.tensor_tensor(out=hsq[:], in0=hhf[:],
                                                    in1=hhf[:], op=ALU.mult)
                        ev_hf = kb.on("vector", e_fuse,
                                      wait=[ev_sg, hhf_free, hsq_free])
                        ev_hi2 = kb.on("gpsimd", lambda po, ic=ic: po.tensor_copy(
                            out=hhi[:, 512 * ic:512 * (ic + 1)], in_=hhf[:]),
                            wait=[ev_hf, hh_free])
                        ev_lo2 = kb.on("vector", lambda dv, ic=ic: dv.tensor_tensor(
                            out=hlo[:, 512 * ic:512 * (ic + 1)], in0=hhf[:],
                            in1=hhi[:, 512 * ic:512 * (ic + 1)], op=ALU.subtract),
                            wait=[ev_hi2])
                        pAB_free[2 * (ic % 2) + 1] = ev_hf
                        sg_free = ev_hf
                        hhf_free = ev_lo2

                        def e_ssum(pe, ic=ic):
                            return pe.matmul(kb.p7[0:1, :], kb.c_ones1[:], hsq[:],
                                             start=(ic == 0), stop=(ic == HCH - 1))
                        ev_ssum = kb.on("tensor", e_ssum,
                                        wait=[ev_hf] + ([p7_free] if ic == 0 else []))
                        hsq_free = ev_ssum
                        prev_pe = ev_ssum
                hh_last = [ev_lo2]

            with ExitStack() as stB:
                def sbtB(n, sh, dt=F32):
                    return stB.enter_context(nc.sbuf_tensor(f"{n}_m{m}", sh, dt))
                wpbuf = sbtB("wp2_0", [128, HCH * 128], BF16)
                rrow = sbtB("rrow2", [1, 512])
                rb = sbtB("rb2", [128, 512])
                ostg = [sbtB(f"ostg2_{i}", [128, 512]) for i in range(2)]
                ffn_tm = sbtB("ffn2", [128, NSUB * D])
                ybuf = sbtB("yb2", [128, D])
                x1buf = sbtB("x1b2", [128, D])

                def e_rr(ac):
                    ac.activation(out=rrow[0:1, :], in_=kb.p7[0:1, :], func=AF.Identity,
                                  scale=1.0 / H_FFN)
                    return ac.activation(out=rrow[0:1, :], in_=rrow[0:1, :], func=AF.Ln,
                                         bias=kb.c_eps[0:1, 1:2])
                ev_r1 = kb.on("scalar", e_rr, wait=[ev_ssum])
                p7_free = ev_r1
                ev_r2 = kb.on("scalar", lambda ac: ac.activation(
                    out=rrow[0:1, :], in_=rrow[0:1, :], func=AF.Exp, scale=-0.5),
                    wait=[ev_r1])

                ev_bc = kb.on("tensor", lambda pe: pe.matmul(
                    kb.p6[:, 0:512], kb.c_ones[0:1, 0:128], rrow[0:1, :],
                    start=True, stop=True), wait=[ev_r2, rb_free])
                ev_rb = kb.on("vector", lambda dv: dv.tensor_copy(
                    out=rb[:], in_=kb.p6[:, 0:512]), wait=[ev_bc])
                prev_pe = ev_bc

                HH2 = HCH // 2
                wh_free = [[prev_pe] + hh_last, [prev_pe] + hh_last]
                pd_free = [entry, entry]
                ostg_free = [entry, entry]
                tpB_free = {0: entry, 1: entry}
                tp_done = []
                rb_users = []
                for dc in range(DCH):
                    ev_wh = []
                    for h in range(2):
                        ev = kb.dma(
                            "sync", wpbuf[:, 128 * HH2 * h:128 * HH2 * (h + 1)],
                            kb.wpt_s[dc][:, HH2 * h:HH2 * (h + 1), :]
                            .rearrange("p j c -> p (j c)"),
                            wait=[wh_free[h]])
                        ev_wh.append(ev)

                    pd = (kb.PB[0], kb.PB[1])[dc % 2]

                    def e_dmmA(pe, pd=pd):
                        for j in range(HH2):
                            pe.matmul(pd, wpbuf[:, 128 * j:128 * (j + 1)],
                                      hhi[:, 512 * j:512 * (j + 1)],
                                      start=(j == 0), stop=False)
                            ins = pe.matmul(pd, wpbuf[:, 128 * j:128 * (j + 1)],
                                            hlo[:, 512 * j:512 * (j + 1)],
                                            start=False, stop=False)
                            kb.ldw_patch.append(ins.ins)
                        return ins
                    ev_dA = kb.on("tensor", e_dmmA,
                                  wait=[ev_wh[0], pd_free[dc % 2]] + hh_last)
                    wh_free[0] = ev_dA

                    def e_dmmB(pe, pd=pd):
                        for j in range(HH2, HCH):
                            pe.matmul(pd, wpbuf[:, 128 * j:128 * (j + 1)],
                                      hhi[:, 512 * j:512 * (j + 1)],
                                      start=False, stop=False)
                            ins = pe.matmul(pd, wpbuf[:, 128 * j:128 * (j + 1)],
                                            hlo[:, 512 * j:512 * (j + 1)],
                                            start=False, stop=(j == HCH - 1))
                            kb.ldw_patch.append(ins.ins)
                        return ins
                    ev_dmm = kb.on("tensor", e_dmmB, wait=[ev_wh[1]])
                    wh_free[1] = ev_dmm
                    prev_pe = ev_dmm

                    osl = dc % 2
                    ev_oev = kb.on("vector", lambda dv, pd=pd, osl=osl: dv.tensor_tensor(
                        out=ostg[osl][:], in0=pd, in1=rb[:], op=ALU.mult),
                        wait=[ev_dmm, ostg_free[osl], ev_rb])
                    pd_free[dc % 2] = ev_oev
                    rb_users.append(ev_oev)

                    pv = (kb.p4, kb.p5)[osl]

                    def e_tp(pe, pv=pv, osl=osl):
                        for s in range(NSUB):
                            ins = pe.transpose(pv[:, 128 * s:128 * (s + 1)],
                                               ostg[osl][:, 128 * s:128 * (s + 1)],
                                               kb.c_ident[:])
                        return ins
                    ev_tp = kb.on("tensor", e_tp, wait=[ev_oev, tpB_free[osl], ffn_free])
                    ostg_free[osl] = ev_tp
                    prev_pe = ev_tp

                    def e_cp(ac, pv=pv, dc=dc):
                        for s in range(NSUB):
                            ins = kb.aid(ac, ffn_tm[:, D * s + 128 * dc:D * s + 128 * (dc + 1)],
                                         pv[:, 128 * s:128 * (s + 1)])
                        return ins
                    ev_cp = kb.on("scalar", e_cp, wait=[ev_tp])
                    tpB_free[osl] = ev_cp
                    tp_done.append(ev_cp)
                hh_free = prev_pe
                rb_free = rb_users[-1]

                for s in range(NSUB):
                    ev_x1 = kb.dma("scalar", x1buf[:],
                                   x_src[r0 + 128 * s:r0 + 128 * (s + 1), :],
                                   wait=[epi_prev])
                    fv = ffn_tm[:, D * s:D * (s + 1)]
                    ev_ln = kb.ln(fv, ybuf[:], [ev_x1] + tp_done)
                    ev_x2 = kb.on("vector", lambda dv: dv.tensor_tensor(
                        out=x1buf[:], in0=x1buf[:], in1=ybuf[:], op=ALU.add), wait=[ev_ln])
                    ev_st = kb.dma("scalar", kb.x2_s[r0 + 128 * s:r0 + 128 * (s + 1), :],
                                   x1buf[:], wait=[ev_x2])
                    epi_prev = ev_st
                    x2_evs.append(ev_st)
                ffn_free = x2_evs[-1]

    return kb.x2_s, [x2_evs[-1]]




def emit_sb3(kb: KB, x_src, in_evs):
    nc = kb.nc
    entry = kb.barrier_ev()

    with ExitStack() as stp:
        def sbtp(n, sh, dt=F32):
            return stp.enter_context(nc.sbuf_tensor(n, sh, dt))
        kb.c_iota = sbtp("c_iota", [128, S])
        ev_iota = kb.dma("sync", kb.c_iota[:], kb.iota[:], wait=entry)
        carry = sbtp("carry3", [128, DCH])
        acc_s = sbtp("acc3_s", [128, DCH])
        acc_q = sbtp("acc3_q", [128, DCH])
        srow = sbtp("srow3", [1, 4])
        stc = sbtp("stc3", [128, 8])
        A_col = sbtp("A3", [128, DCH])
        B_col = sbtp("B3", [128, DCH])

        x3_evs = [entry]
        epi_prev = entry
        prev_pe = entry
        att_rd_free = entry
        ab_users = entry

        for b in range(BPC):
            def e_rst(po):
                po.memset(carry[:], 0.0)
                po.memset(acc_s[:], 0.0)
                return po.memset(acc_q[:], 0.0)
            ev_rst = kb.on("gpsimd", e_rst, wait=[ab_users])

            with ExitStack() as stA:
                def sbtA(n, sh, dt=F32):
                    return stA.enter_context(nc.sbuf_tensor(f"{n}_b{b}", sh, dt))
                z_mac = sbtA("z3", [128, NSUB * D])
                xbuf = sbtA("x3ld", [128, D])
                zT = sbtA("z3T", [128, DCH * 512])
                r_all = sbtA("r3", [128, DCH * 512])
                kv_all = sbtA("kv3", [128, DCH * 512])
                wbuf = [sbtA(f"w3buf{i}", [128, DCH * 128]) for i in range(3)]
                scl = [sbtA(f"scl3_{i}", [128, 512]) for i in range(2)]
                arg = sbtA("arg3", [128, 512])
                clp = sbtA("clp3", [128, 512])
                inv = sbtA("inv3", [128, 512])
                scd = sbtA("scd3", [128, 512])
                cs = [sbtA(f"cs3_{i}", [128, 512]) for i in range(2)]
                stt_ = [sbtA(f"st3_{i}", [128, 512]) for i in range(2)]
                attst = [sbtA(f"att3_{i}", [128, 512]) for i in range(2)]
                sqd = sbtA("sqd3", [128, 512])

                macro_gate = [epi_prev, prev_pe, ev_rst]
                xbuf_free = macro_gate
                wbuf_free = [macro_gate, macro_gate, macro_gate]
                pbank_free = {0: macro_gate, 1: macro_gate}
                tpp_free = {0: macro_gate, 1: macro_gate}
                kcol_free = {}
                scl_free = [macro_gate, macro_gate]
                cs_free = [macro_gate, macro_gate]
                st_free = [macro_gate, macro_gate]
                att_free = [macro_gate, macro_gate]
                arg_free = macro_gate
                clp_free = macro_gate
                inv_free = macro_gate
                scd_free = macro_gate
                sqd_free = macro_gate
                racc_prev = macro_gate
                qacc_prev = macro_gate
                carry_rd = {c: [ev_rst] for c in range(DCH)}
                r_free = {c: macro_gate for c in range(DCH)}
                kv_free = {c: macro_gate for c in range(DCH)}

                for tw in range(MPB):
                    t0 = tw * MAC
                    r0 = b * S + t0
                    ln_evs = []
                    for s in range(NSUB):
                        ev_x = kb.dma("scalar", xbuf[:],
                                      x_src[r0 + 128 * s:r0 + 128 * (s + 1), :],
                                      wait=[xbuf_free] + (in_evs if b == 0 and tw == 0 and s == 0 else []))
                        ev_ln = kb.ln(xbuf[:], z_mac[:, D * s:D * (s + 1)],
                                      [ev_x] + macro_gate)
                        xbuf_free = ev_ln
                        ln_evs.append(ev_ln)
                    tp_evs = kb.transpose_mac(
                        [z_mac[:, D * s:D * (s + 1)] for s in range(NSUB)],
                        ln_evs, zT, [512 * j for j in range(DCH)],
                        (kb.PB[2], kb.PB[3]), tpp_free)
                    zT_ready = [tp_evs[-1]]
                    prev_pe = tp_evs[-1]

                    kv_done = {}
                    r_done = {}
                    for pi, wt in enumerate([kb.wr1_t, kb.wk1_t, kb.wv1_t]):
                        for i in range(DCH):
                            g = pi * DCH + i
                            ws = g % 3
                            pb = kb.PB[g % 2]
                            ev_w = kb.dma("sync", wbuf[ws][:], wt[i].flatten_outer_dims(),
                                          wait=[wbuf_free[ws]])

                            def e_mm(pe, ws=ws, pb=pb):
                                for j in range(DCH):
                                    ins = pe.matmul(pb, wbuf[ws][:, 128 * j:128 * (j + 1)],
                                                    zT[:, 512 * j:512 * (j + 1)],
                                                    start=(j == 0), stop=(j == DCH - 1))
                                return ins
                            ev_mm = kb.on("tensor", e_mm,
                                          wait=[ev_w, pbank_free[g % 2]] + zT_ready)
                            wbuf_free[ws] = ev_mm
                            prev_pe = ev_mm

                            if pi == 0:
                                ev_ev = kb.on("scalar", lambda ac, pb=pb, i=i: ac.activation(
                                    out=r_all[:, 512 * i:512 * (i + 1)], in_=pb,
                                    func=AF.Sigmoid), wait=[ev_mm, r_free[i]])
                                r_done[i] = ev_ev
                            elif pi == 1:
                                ev_ev = kb.on("scalar", lambda ac, pb=pb, i=i: kb.aid(
                                    ac, kv_all[:, 512 * i:512 * (i + 1)], pb),
                                    wait=[ev_mm, kv_free[i]])
                                kcol_free[i] = ev_ev
                            else:
                                ev_ev = kb.on("vector", lambda dv, pb=pb, i=i: dv.tensor_tensor(
                                    out=kv_all[:, 512 * i:512 * (i + 1)],
                                    in0=kv_all[:, 512 * i:512 * (i + 1)],
                                    in1=pb, op=ALU.mult), wait=[ev_mm, kcol_free[i]])
                                kv_done[i] = ev_ev
                            pbank_free[g % 2] = ev_ev

                    for c in range(DCH):
                        sl = c % 2
                        it = kb.c_iota[:, t0:t0 + 512]
                        ld_col = kb.c_logd[:, c:c + 1]

                        def e_arg(dv, it=it, ld_col=ld_col):
                            dv.tensor_scalar_mul(arg[:], it, ld_col)
                            return dv.tensor_scalar(out=clp[:], in0=it, scalar1=ld_col,
                                                    scalar2=LN_CLIP_LO,
                                                    op0=ALU.mult, op1=ALU.max)
                        ev_arg = kb.on("vector", e_arg,
                                       wait=[ev_iota, kb.ev_logd, arg_free, clp_free])

                        def e_exp(ac, sl=sl):
                            ac.activation(out=scl[sl][:], in_=arg[:], func=AF.Exp)
                            return ac.activation(out=inv[:], in_=clp[:], func=AF.Exp,
                                                 scale=-1.0)
                        ev_scl = kb.on("scalar", e_exp,
                                       wait=[ev_arg, scl_free[sl], inv_free])
                        arg_free = ev_scl
                        clp_free = ev_scl

                        def e_scan(dv, sl=sl, c=c):
                            dv.tensor_tensor(out=scd[:], in0=kv_all[:, 512 * c:512 * (c + 1)],
                                             in1=inv[:], op=ALU.mult)
                            return dv.tensor_tensor_scan(
                                out=cs[sl][:], data0=kb.c_ones[:], data1=scd[:],
                                initial=carry[:, c:c + 1], op0=ALU.mult, op1=ALU.add)
                        ev_cs = kb.on("vector", e_scan,
                                      wait=[ev_scl, kv_done[c], scd_free, cs_free[sl]]
                                      + carry_rd[c] + [kb.ev_ones])
                        inv_free = ev_cs
                        scd_free = ev_cs
                        kv_free[c] = ev_cs
                        ev_cw = kb.on("scalar", lambda ac, sl=sl, c=c: kb.aid(
                            ac, carry[:, c:c + 1], cs[sl][:, 511:512]), wait=[ev_cs])
                        carry_rd[c] = [ev_cw]

                        def e_att(dv, sl=sl, c=c):
                            dv.tensor_tensor(out=stt_[sl][:], in0=cs[sl][:], in1=scl[sl][:],
                                             op=ALU.mult)
                            return dv.tensor_tensor(out=attst[sl][:], in0=stt_[sl][:],
                                                    in1=r_all[:, 512 * c:512 * (c + 1)],
                                                    op=ALU.mult)
                        ev_att = kb.on("vector", e_att,
                                       wait=[ev_cw, r_done[c], st_free[sl], att_free[sl]])
                        cs_free[sl] = [ev_att, ev_cw]
                        scl_free[sl] = ev_att
                        st_free[sl] = ev_att
                        r_free[c] = ev_att

                        def e_stats(dv, sl=sl):
                            dv.tensor_reduce(out=stc[:, 4:5], in_=attst[sl][:], axis=AX.X,
                                             op=ALU.add)
                            dv.tensor_tensor(out=sqd[:], in0=attst[sl][:], in1=attst[sl][:],
                                             op=ALU.mult)
                            return dv.tensor_reduce(out=stc[:, 5:6], in_=sqd[:], axis=AX.X,
                                                    op=ALU.add)
                        ev_stats = kb.on("vector", e_stats,
                                         wait=[ev_att, sqd_free, racc_prev, qacc_prev])

                        def e_acc(ac, c=c):
                            kb.pin(ac)
                            ac.activation(out=acc_s[:, c:c + 1], in_=acc_s[:, c:c + 1],
                                          func=AF.Identity, bias=stc[:, 4:5])
                            return ac.activation(out=acc_q[:, c:c + 1], in_=acc_q[:, c:c + 1],
                                                 func=AF.Identity, bias=stc[:, 5:6])
                        ev_acc = kb.on("scalar", e_acc, wait=[ev_stats])
                        racc_prev = ev_acc
                        qacc_prev = ev_acc
                        sqd_free = ev_stats
                        ev_dma = kb.dma("gpsimd", kb.att_s[b, c, :, t0:t0 + 512],
                                        attst[sl][:], wait=[ev_att])
                        att_free[sl] = [ev_dma, ev_stats]
                ab_last = [racc_prev, qacc_prev]
                stat_dma = (kb.sW, kb.sW.v)

            def e_sums(pe):
                pe.matmul(kb.p7[0:1, 0:16], kb.c_ones1[:], acc_s[:], start=True, stop=True)
                return pe.matmul(kb.p7[0:1, 16:32], kb.c_ones1[:], acc_q[:],
                                 start=False, stop=True)
            ev_sums = kb.on("tensor", e_sums, wait=ab_last + [prev_pe])
            prev_pe = ev_sums

            def e_red(dv):
                dv.tensor_reduce(out=srow[0:1, 0:1], in_=kb.p7[0:1, 0:16], axis=AX.X,
                                 op=ALU.add)
                return dv.tensor_reduce(out=srow[0:1, 1:2], in_=kb.p7[0:1, 16:32],
                                        axis=AX.X, op=ALU.add)
            ev_red = kb.on("vector", e_red, wait=[ev_sums])

            invN = 1.0 / (S * D)

            def e_mm2(ac):
                ac.activation(out=srow[0:1, 2:3], in_=srow[0:1, 0:1], func=AF.Identity,
                              scale=invN)
                return ac.activation(out=srow[0:1, 3:4], in_=srow[0:1, 1:2],
                                     func=AF.Identity, scale=invN)
            ev_m2 = kb.on("scalar", e_mm2, wait=[ev_red])

            ev_nm2 = kb.on("vector", lambda dv: dv.scalar_tensor_tensor(
                out=srow[0:1, 0:1], in0=srow[0:1, 2:3], scalar=-1.0,
                in1=srow[0:1, 2:3], op0=ALU.mult, op1=ALU.mult), wait=[ev_m2])

            def e_rstd(ac):
                ac.activation(out=srow[0:1, 1:2], in_=srow[0:1, 3:4], func=AF.Identity,
                              bias=srow[0:1, 0:1])
                ac.activation(out=srow[0:1, 1:2], in_=srow[0:1, 1:2], func=AF.Ln,
                              bias=kb.c_eps[0:1, 0:1])
                return ac.activation(out=srow[0:1, 1:2], in_=srow[0:1, 1:2],
                                     func=AF.Exp, scale=-0.5)
            ev_rstd = kb.on("scalar", e_rstd, wait=[ev_nm2])

            def e_bc(pe):
                return pe.matmul(kb.p7[:, 32:34], kb.c_ones[0:1, 0:128], srow[0:1, 1:3],
                                 start=True, stop=True)
            ev_bc = kb.on("tensor", e_bc, wait=[ev_rstd])
            prev_pe = ev_bc

            ev_bcc = kb.on("vector", lambda dv: dv.tensor_copy(
                out=stc[:, 0:2], in_=kb.p7[:, 32:34]), wait=[ev_bc])

            ev_A = kb.on("scalar", lambda ac: ac.activation(
                out=A_col[:], in_=kb.c_gnw[:], func=AF.Identity, scale=stc[:, 0:1]),
                wait=[ev_bcc])
            ev_ngm = kb.on("vector", lambda dv: dv.tensor_scalar_mul(
                stc[:, 2:3], stc[:, 1:2], -1.0), wait=[ev_A])
            ev_t1 = kb.on("gpsimd", lambda po: po.tensor_scalar_mul(
                B_col[:], A_col[:], stc[:, 2:3]), wait=[ev_ngm])
            ev_B = kb.on("vector", lambda dv: dv.tensor_tensor(
                out=B_col[:], in0=B_col[:], in1=kb.c_gnb[:], op=ALU.add), wait=[ev_t1])
            ab_users = None

            with ExitStack() as stC:
                def sbtC(n, sh, dt=F32):
                    return stC.enter_context(nc.sbuf_tensor(f"{n}_b{b}", sh, dt))
                att_all = sbtC("attld3", [128, DCH * 512])
                wbuf = [sbtC(f"w3cbuf{i}", [128, DCH * 128]) for i in range(2)]
                wo_stage = [sbtC(f"wostg3_{i}", [128, 512]) for i in range(2)]
                ffn_tm = sbtC("att3_tm", [128, NSUB * D])
                ybuf = sbtC("yb3", [128, D])
                x2buf = sbtC("x2b3", [128, D])

                cgate = [prev_pe, epi_prev, ev_B, stat_dma]
                wbuf_free = [cgate, cgate]
                pbank_free = [cgate, cgate]
                wostg_free = [cgate, cgate]
                tpc_free = {0: cgate, 1: cgate}
                att_ld_free = cgate
                ffn_free = cgate
                ab_user_evs = []

                for tw in range(MPB):
                    t0 = tw * MAC
                    r0 = b * S + t0
                    ev_ald = kb.dma_multi("scalar", [
                        (att_all[:, 512 * c:512 * (c + 1)], kb.att_s[b, c, :, t0:t0 + 512])
                        for c in range(DCH)], wait=[att_ld_free])

                    nrm_evs = []
                    for c in range(DCH):
                        ev_n = kb.on("vector", lambda dv, c=c: dv.tensor_scalar(
                            out=att_all[:, 512 * c:512 * (c + 1)],
                            in0=att_all[:, 512 * c:512 * (c + 1)],
                            scalar1=A_col[:, c:c + 1], scalar2=B_col[:, c:c + 1],
                            op0=ALU.mult, op1=ALU.add), wait=[ev_ald, ev_B])
                        nrm_evs.append(ev_n)
                        ab_user_evs.append(ev_n)

                    tp_done = []
                    for i in range(DCH):
                        ws = i % 2
                        pb = kb.PB[i % 2]
                        ev_w = kb.dma("sync", wbuf[ws][:], kb.wo1_t[i].flatten_outer_dims(),
                                      wait=[wbuf_free[ws]])

                        def e_mm(pe, ws=ws, pb=pb):
                            for j in range(DCH):
                                ins = pe.matmul(pb, wbuf[ws][:, 128 * j:128 * (j + 1)],
                                                att_all[:, 512 * j:512 * (j + 1)],
                                                start=(j == 0), stop=(j == DCH - 1))
                            return ins
                        ev_mm = kb.on("tensor", e_mm,
                                      wait=[ev_w, pbank_free[i % 2]] + nrm_evs)
                        wbuf_free[ws] = ev_mm
                        prev_pe = ev_mm

                        sl = i % 2
                        ev_evac = kb.on("scalar", lambda ac, pb=pb, sl=sl: kb.aid(
                            ac, wo_stage[sl][:], pb), wait=[ev_mm, wostg_free[sl]])
                        pbank_free[i % 2] = ev_evac

                        pv = (kb.p4, kb.p5)[sl]

                        def e_tp(pe, sl=sl, pv=pv):
                            for s in range(NSUB):
                                ins = pe.transpose(pv[:, 128 * s:128 * (s + 1)],
                                                   wo_stage[sl][:, 128 * s:128 * (s + 1)],
                                                   kb.c_ident[:])
                            return ins
                        ev_tp = kb.on("tensor", e_tp,
                                      wait=[ev_evac, tpc_free[sl], ffn_free])
                        wostg_free[sl] = ev_tp
                        prev_pe = ev_tp

                        def e_cp(dv, pv=pv, i=i):
                            for s in range(NSUB):
                                ins = dv.tensor_copy(
                                    out=ffn_tm[:, D * s + 128 * i:D * s + 128 * (i + 1)],
                                    in_=pv[:, 128 * s:128 * (s + 1)])
                            return ins
                        ev_cp = kb.on("vector", e_cp, wait=[ev_tp])
                        tpc_free[sl] = ev_cp
                        tp_done.append(ev_cp)
                    att_ld_free = ev_mm

                    for s in range(NSUB):
                        ev_x2 = kb.dma("scalar", x2buf[:],
                                       x_src[r0 + 128 * s:r0 + 128 * (s + 1), :],
                                       wait=[epi_prev])
                        fv = ffn_tm[:, D * s:D * (s + 1)]
                        ev_ln = kb.ln(fv, ybuf[:], [ev_x2] + tp_done)
                        ev_x3 = kb.on("vector", lambda dv: dv.tensor_tensor(
                            out=x2buf[:], in0=x2buf[:], in1=ybuf[:], op=ALU.add),
                            wait=[ev_ln])
                        ev_st = kb.dma("scalar", kb.x3_s[r0 + 128 * s:r0 + 128 * (s + 1), :],
                                       x2buf[:], wait=[ev_x3])
                        epi_prev = ev_st
                        x3_evs.append(ev_st)
                    ffn_free = x3_evs[-1]
                ab_users = ab_user_evs[-1]

    return kb.x3_s, [x3_evs[-1]]




def emit_sb4(kb: KB, x_src, in_evs):
    nc = kb.nc
    entry = kb.barrier_ev()

    with ExitStack() as stp:
        def sbtp(n, sh, dt=F32):
            return stp.enter_context(nc.sbuf_tensor(n, sh, dt))
        hhi = sbtp("hh4hi", [128, HCH * 512], BF16)
        hlo = sbtp("hh4lo", [128, HCH * 512], BF16)
        wdbuf0 = sbtp("wd4p", [128, HCH * 128], BF16)

        out_evs = [entry]
        epi_prev = entry
        hh_free = entry
        prev_pe = entry
        ffn_free = entry

        for m in range(NMAC):
            b, tw = divmod(m, MPB)
            t0 = tw * MAC
            r0 = b * S + t0

            with ExitStack() as stA:
                def sbtA(n, sh, dt=F32):
                    return stA.enter_context(nc.sbuf_tensor(f"{n}_m{m}", sh, dt))
                yhi = sbtA("y4hi", [128, DCH * 512], BF16)
                ylo = sbtA("y4lo", [128, DCH * 512], BF16)
                stA1 = stA.enter_context(ExitStack())

                def sbtA1(n, sh, dt=F32):
                    return stA1.enter_context(nc.sbuf_tensor(f"{n}_m{m}", sh, dt))
                z_mac = sbtA1("z4", [128, 2 * D])
                xbuf = sbtA1("x4ld", [128, D])

                macro_gate = [epi_prev, prev_pe]
                tpp_free = {0: macro_gate, 1: macro_gate}
                xbuf_free = macro_gate
                last_lo = None
                last_tp = None
                z_free = macro_gate
                for sp in range(2):
                    ln_evs = []
                    for half in range(2):
                        s = 2 * sp + half
                        ev_x = kb.dma("scalar", xbuf[:],
                                      x_src[r0 + 128 * s:r0 + 128 * (s + 1), :],
                                      wait=[xbuf_free] + (in_evs if m == 0 and s == 0 else []))
                        ev_ln = kb.ln(xbuf[:], z_mac[:, D * half:D * (half + 1)],
                                      [ev_x] + z_free)
                        xbuf_free = ev_ln
                        ln_evs.append(ev_ln)
                    for j in range(DCH):
                        pv = (kb.PB[2], kb.PB[3])[j % 2]

                        def e_tp(pe, j=j, pv=pv):
                            for half in range(2):
                                ins = pe.transpose(pv[:, 128 * half:128 * (half + 1)],
                                                   z_mac[:, D * half + 128 * j:D * half + 128 * (j + 1)],
                                                   kb.c_ident[:])
                            return ins
                        ev_tp = kb.on("tensor", e_tp, wait=ln_evs + [tpp_free[j % 2]])

                        col = 512 * j + 256 * sp
                        ev_hi = kb.on("scalar", lambda ac, pv=pv, col=col: kb.aid(
                            ac, yhi[:, col:col + 256], pv[:, 0:256]), wait=[ev_tp])
                        ev_lo = kb.on("vector", lambda dv, pv=pv, col=col: dv.tensor_tensor(
                            out=ylo[:, col:col + 256], in0=pv[:, 0:256],
                            in1=yhi[:, col:col + 256], op=ALU.subtract), wait=[ev_hi])
                        tpp_free[j % 2] = ev_lo
                        last_lo = ev_lo
                        last_tp = ev_tp
                    z_free = [last_tp]
                y_ready = [last_lo]
                stA1.close()
                wgbuf = [sbtA(f"wg4_{i}", [128, DCH * 128], BF16) for i in range(2)]
                wubuf = [sbtA(f"wu4_{i}", [128, DCH * 128], BF16) for i in range(2)]
                sgb = sbtA("sgb4", [128, 512])
                ub = sbtA("ub4", [128, 512])
                t1 = sbtA("t1_4", [128, 512])
                hhf = sbtA("hhf4", [128, 512])

                wg_free = [[last_tp] + macro_gate, [last_tp] + macro_gate]
                sgb_free = macro_gate
                ub_free = macro_gate
                t1_free = macro_gate
                hhf_free = macro_gate
                pAB_free = {0: macro_gate, 1: macro_gate, 2: macro_gate, 3: macro_gate}

                for ic in range(HCH):
                    wsl = ic % 2
                    ev_wg = kb.dma("sync", wgbuf[wsl][:],
                                   kb.gwb_s[ic].rearrange("p j c -> p (j c)"),
                                   wait=[wg_free[wsl]])
                    ev_wu = kb.dma("sync", wubuf[wsl][:],
                                   kb.uwb_s[ic].rearrange("p j c -> p (j c)"),
                                   wait=[wg_free[wsl]])
                    pg = kb.PB[2 * (ic % 2)]
                    pu = kb.PB[2 * (ic % 2) + 1]

                    def e_mm(pe, wsl=wsl, pg=pg, pu=pu):
                        for j in range(DCH):
                            wgt = wgbuf[wsl][:, 128 * j:128 * (j + 1)]
                            wut = wubuf[wsl][:, 128 * j:128 * (j + 1)]
                            pe.matmul(pg, wgt, yhi[:, 512 * j:512 * (j + 1)],
                                      start=(j == 0), stop=False)
                            kb.ldw_patch.append(pe.matmul(pg, wgt, ylo[:, 512 * j:512 * (j + 1)],
                                      start=False, stop=(j == DCH - 1)).ins)
                            pe.matmul(pu, wut, yhi[:, 512 * j:512 * (j + 1)],
                                      start=(j == 0), stop=False)
                            ins = pe.matmul(pu, wut, ylo[:, 512 * j:512 * (j + 1)],
                                            start=False, stop=(j == DCH - 1))
                            kb.ldw_patch.append(ins.ins)
                        return ins
                    ev_mm = kb.on("tensor", e_mm,
                                  wait=[ev_wg, ev_wu, pAB_free[2 * (ic % 2)],
                                        pAB_free[2 * (ic % 2) + 1]] + y_ready)
                    prev_pe = ev_mm
                    wg_free[wsl] = ev_mm

                    gs_col = kb.c_gs[:, ic:ic + 1]
                    us_col = kb.c_us[:, ic:ic + 1]
                    ev_sgb = kb.on("scalar", lambda ac, pg=pg, gs_col=gs_col: ac.activation(
                        out=sgb[:], in_=pg, func=AF.Sigmoid, scale=gs_col),
                        wait=[ev_mm, sgb_free])

                    def e_fuse(dv, pg=pg, pu=pu, gs_col=gs_col, us_col=us_col):
                        dv.tensor_tensor(out=t1[:], in0=pg, in1=sgb[:], op=ALU.mult)
                        dv.tensor_scalar_mul(ub[:], pu, us_col)
                        return dv.scalar_tensor_tensor(out=hhf[:], in0=t1[:],
                                                       scalar=gs_col, in1=ub[:],
                                                       op0=ALU.mult, op1=ALU.mult)
                    ev_hf = kb.on("vector", e_fuse,
                                  wait=[ev_sgb, t1_free, ub_free, hhf_free])
                    ev_hi2 = kb.on("gpsimd", lambda po, ic=ic: po.tensor_copy(
                        out=hhi[:, 512 * ic:512 * (ic + 1)], in_=hhf[:]),
                        wait=[ev_hf, hh_free])
                    ev_lo2 = kb.on("vector", lambda dv, ic=ic: dv.tensor_tensor(
                        out=hlo[:, 512 * ic:512 * (ic + 1)], in0=hhf[:],
                        in1=hhi[:, 512 * ic:512 * (ic + 1)], op=ALU.subtract),
                        wait=[ev_hi2])
                    pAB_free[2 * (ic % 2)] = ev_hf
                    pAB_free[2 * (ic % 2) + 1] = ev_hf
                    sgb_free = ev_hf
                    t1_free = ev_hf
                    ub_free = ev_hf
                    hhf_free = ev_lo2
                hh_last = [ev_lo2]

            with ExitStack() as stB:
                def sbtB(n, sh, dt=F32):
                    return stB.enter_context(nc.sbuf_tensor(f"{n}_m{m}", sh, dt))
                ostg = [sbtB(f"ostg4_{i}", [128, 512]) for i in range(2)]
                ffn_tm = sbtB("ffn4", [128, NSUB * D])
                ybuf = sbtB("yb4", [128, D])
                x3buf = sbtB("x3b4", [128, D])

                HH2 = HCH // 2
                wh_free = [[prev_pe] + hh_last, [prev_pe] + hh_last]
                pd_free = [entry, entry]
                ostg_free = [entry, entry]
                tpB_free = {0: entry, 1: entry}
                tp_done = []
                for dc in range(DCH):
                    ev_wh = []
                    for h in range(2):
                        ev = kb.dma(
                            "sync", wdbuf0[:, 128 * HH2 * h:128 * HH2 * (h + 1)],
                            kb.dwb_s[dc][:, HH2 * h:HH2 * (h + 1), :]
                            .rearrange("p j c -> p (j c)"),
                            wait=[wh_free[h]])
                        ev_wh.append(ev)

                    pd = (kb.PB[0], kb.PB[1])[dc % 2]

                    def e_dmmA(pe, pd=pd):
                        for j in range(HH2):
                            pe.matmul(pd, wdbuf0[:, 128 * j:128 * (j + 1)],
                                      hhi[:, 512 * j:512 * (j + 1)],
                                      start=(j == 0), stop=False)
                            ins = pe.matmul(pd, wdbuf0[:, 128 * j:128 * (j + 1)],
                                            hlo[:, 512 * j:512 * (j + 1)],
                                            start=False, stop=False)
                            kb.ldw_patch.append(ins.ins)
                        return ins
                    ev_dA = kb.on("tensor", e_dmmA,
                                  wait=[ev_wh[0], pd_free[dc % 2]] + hh_last)
                    wh_free[0] = ev_dA

                    def e_dmmB(pe, pd=pd):
                        for j in range(HH2, HCH):
                            pe.matmul(pd, wdbuf0[:, 128 * j:128 * (j + 1)],
                                      hhi[:, 512 * j:512 * (j + 1)],
                                      start=False, stop=False)
                            ins = pe.matmul(pd, wdbuf0[:, 128 * j:128 * (j + 1)],
                                            hlo[:, 512 * j:512 * (j + 1)],
                                            start=False, stop=(j == HCH - 1))
                            kb.ldw_patch.append(ins.ins)
                        return ins
                    ev_dmm = kb.on("tensor", e_dmmB, wait=[ev_wh[1]])
                    wh_free[1] = ev_dmm
                    prev_pe = ev_dmm

                    osl = dc % 2
                    ev_oev = kb.on("scalar", lambda ac, pd=pd, osl=osl, dc=dc: ac.activation(
                        out=ostg[osl][:], in_=pd, func=AF.Identity,
                        scale=kb.c_ds[:, dc:dc + 1]),
                        wait=[ev_dmm, ostg_free[osl]])
                    pd_free[dc % 2] = ev_oev

                    pv = (kb.p4, kb.p5)[osl]

                    def e_tp(pe, pv=pv, osl=osl):
                        for s in range(NSUB):
                            ins = pe.transpose(pv[:, 128 * s:128 * (s + 1)],
                                               ostg[osl][:, 128 * s:128 * (s + 1)],
                                               kb.c_ident[:])
                        return ins
                    ev_tp = kb.on("tensor", e_tp, wait=[ev_oev, tpB_free[osl], ffn_free])
                    ostg_free[osl] = ev_tp
                    prev_pe = ev_tp

                    def e_cp(dv, pv=pv, dc=dc):
                        for s in range(NSUB):
                            ins = dv.tensor_copy(
                                out=ffn_tm[:, D * s + 128 * dc:D * s + 128 * (dc + 1)],
                                in_=pv[:, 128 * s:128 * (s + 1)])
                        return ins
                    ev_cp = kb.on("vector", e_cp, wait=[ev_tp])
                    tpB_free[osl] = ev_cp
                    tp_done.append(ev_cp)
                hh_free = prev_pe

                for s in range(NSUB):
                    ev_x3 = kb.dma("scalar", x3buf[:],
                                   x_src[r0 + 128 * s:r0 + 128 * (s + 1), :],
                                   wait=[epi_prev])
                    fv = ffn_tm[:, D * s:D * (s + 1)]
                    ev_ln = kb.ln(fv, ybuf[:], [ev_x3] + tp_done)
                    ev_x4 = kb.on("vector", lambda dv: dv.tensor_tensor(
                        out=x3buf[:], in0=x3buf[:], in1=ybuf[:], op=ALU.add), wait=[ev_ln])
                    ev_st = kb.dma("scalar", kb.out[r0 + 128 * s:r0 + 128 * (s + 1), :],
                                   x3buf[:], wait=[ev_x4])
                    epi_prev = ev_st
                    out_evs.append(ev_st)
                ffn_free = out_evs[-1]

    return kb.out, [out_evs[-1]]




def emit_finale(kb, src, evs, upto):
    if upto < 4:
        ev = kb.dma("sync", kb.out[:], src[:], wait=kb.barrier_ev() + list(evs))
    else:
        ev = None
    for name in DEBUG_DUMPS:
        scr_ap = getattr(kb, name)
        dbg = kb.nc.dram_tensor("dbg_" + name, list(scr_ap.shape), scr_ap.dtype,
                                kind="ExternalOutput").ap()
        ev = kb.dma("sync", dbg.flatten_outer_dims(), scr_ap.flatten_outer_dims(),
                    wait=[ev] + kb.barrier_ev() + list(evs))

    def f(sy):
        _wait_all(sy, [ev] + list(evs) + kb.barrier_ev())
    kb.block.sync(f)


def _patch_ldweights(kb):
    import copy as _copy
    ids = {id(x) for x in kb.ldw_patch}
    if not ids:
        return
    n = 0
    for blk in kb.nc.main_func.blocks:
        il = blk.instructions
        for idx, ins in enumerate(il):
            if id(ins) in ids:
                il[idx] = _copy.replace(ins, ldweights=False)
                n += 1
    assert n == len(ids), (n, len(ids))


def _build(upto=None):
    if upto is None:
        upto = UPTO
    kb = KB(upto)
    with kb.es:
        emit_prologue(kb)
        with ExitStack() as qscope:
            qs = QuantStream(kb, qscope) if upto >= 2 else None
            src, evs = emit_sb1(kb, qs)
            if qs is not None:
                qs.finish()
        if upto >= 2:
            src, evs = emit_sb2(kb, src, evs)
        if upto >= 3:
            src, evs = emit_sb3(kb, src, evs)
        if upto >= 4:
            src, evs = emit_sb4(kb, src, evs)
        emit_finale(kb, src, evs, upto)
        _patch_ldweights(kb)
    return kb.nc


def kernel(**inputs):
    shared, shards = _prep_inputs(inputs)
    nc = _build()
    in_maps = [dict(shared, x_in=shards[c]) for c in range(NCORES)]
    res = run_bass_kernel_spmd(nc, in_maps, core_ids=list(range(NCORES)))
    out = np.empty((B, S, D), np.float32)
    for c, r in enumerate(res.results):
        out[BPC * c:BPC * (c + 1)] = r["out"].reshape(BPC, S, D)
    return out
